# version 05
# baseline (speedup 1.0000x reference)
import numpy as np
import ml_dtypes
import concourse.bass as bass
import concourse.mybir as mybir
from concourse.bass_utils import run_bass_kernel_spmd
from concourse import dve_ops
from concourse.dve_op import DveOp
from concourse.dve_spec import AluOp as DAlu, Bin, C0, C1, C2, Spec, Src0, Src1, maxx
from concourse.library_overlay import lower_extended_insts

F32 = mybir.dt.float32
BF16 = mybir.dt.bfloat16
ALU = mybir.AluOpType
AXX = mybir.AxisListType.X
ACT = mybir.ActivationFunctionType

_FP8_X = Src0 * Src1
_FP8_NX = Bin(DAlu.BITWISE_NOT, _FP8_X, _FP8_X)
_FP8_P = Bin(DAlu.BITWISE_NOT, Bin(DAlu.BITWISE_OR, _FP8_NX, C0), _FP8_NX)
_FP8_CM = maxx(_FP8_P, C1) * C2
FP8_NOTMASK = float(np.uint32(0x807FFFFF).view(np.float32))
FP8_C1 = float(2.0 ** -6)
FP8_C2 = float(1.5 * 2.0 ** 20)


def _fp8_rne_ref(in0, in1, s0, s1, imm2):
    f = np.float64 if in0.dtype == np.float64 else np.float32
    x = in0.astype(f) * in1.astype(f)
    if f is np.float64:
        ax = np.abs(x)
        _, e = np.frexp(ax)
        pc = np.maximum(np.ldexp(np.float64(1.0), e - 1), 2.0 ** -6)
        ulp = pc * (2.0 ** -3)
        return np.round(x / ulp) * ulp
    x = x.astype(np.float32)
    p = (x.view(np.int32) & np.int32(0x7F800000)).view(np.float32)
    pc = np.maximum(p, np.float32(2.0 ** -6))
    cm = (pc * np.float32(1.5 * 2.0 ** 20)).astype(np.float32)
    t = (x + cm).astype(np.float32)
    return (t - cm).astype(np.float32)


FP8_RNE_MUL = DveOp(
    "FP8_RNE_MUL",
    Spec(body=(_FP8_X + _FP8_CM) - _FP8_CM),
    subdim=False,
    uops_sha={"v3": "23f3018ff8a04129"},
    reference=_fp8_rne_ref,
)

if not any(o.name == "FP8_RNE_MUL" for o in dve_ops.OPS):
    dve_ops.OPS.append(FP8_RNE_MUL)
    dve_ops.CUSTOM_DVE_SPECS[FP8_RNE_MUL.name] = FP8_RNE_MUL.spec
    dve_ops.CUSTOM_DVE_REFS[FP8_RNE_MUL.name] = FP8_RNE_MUL.reference
    dve_ops._SUB_OPCODE_FOR_NAME[FP8_RNE_MUL.name] = 31

RCF = dve_ops.RECIPROCAL_APPROX_FAST
RCF_C = dve_ops.RECIP_APPROX_FAST_CONSTS

_WR_R = (Src0 * C0 + C1) - C1
WRAP_RNE = DveOp(
    "WRAP_RNE",
    Spec(body=Src0 - _WR_R * C2),
    subdim=False,
    uops_sha={"v3": "384a601106e5b300"},
    reference=lambda in0, in1, s0, s1, imm2: (
        (lambda f, x: (x - np.round(x * np.float64(s0)) * np.float64(imm2))
         if f is np.float64 else
         (x - (((x * np.float32(s0)).astype(np.float32) + np.float32(s1)).astype(np.float32)
               - np.float32(s1)).astype(np.float32) * np.float32(imm2)).astype(np.float32)
         )(np.float64 if in0.dtype == np.float64 else np.float32, in0)
    ),
)
if not any(o.name == "WRAP_RNE" for o in dve_ops.OPS):
    dve_ops.OPS.append(WRAP_RNE)
    dve_ops.CUSTOM_DVE_SPECS[WRAP_RNE.name] = WRAP_RNE.spec
    dve_ops.CUSTOM_DVE_REFS[WRAP_RNE.name] = WRAP_RNE.reference
    dve_ops._SUB_OPCODE_FOR_NAME[WRAP_RNE.name] = 30
WRAP_MAGIC = float(1.5 * 2.0 ** 23)
PI_CLAMP = float(np.float32(3.14159251))

NCORES = 8
T = 2048
D = 2048
DH = 2048
I_FFN = 8192
SEG = 256
NSEG = T // SEG
SUB = 64
NSUB = SEG // SUB
NTT = T // 128
EPS = 1e-5
PI = float(np.pi)
INV2PI = float(1.0 / (2.0 * np.pi))
TWOPI = float(2.0 * np.pi)
NHCH = I_FFN // 512
QT = 512
NQT = T // QT


class Buf:
    __slots__ = ("wtok", "rtoks")

    def __init__(self):
        self.wtok = None
        self.rtoks = []


class Plan:
    def __init__(self):
        self.ops = {e: [] for e in ("sync", "scalar", "tensor", "vector", "gpsimd")}
        self.cnt = {}
        self.bufs = {}
        self.pending = {e: [] for e in ("sync", "scalar", "tensor", "vector", "gpsimd")}

    def barrier(self):
        snap = [(s, v) for (s, v) in self.cnt.items()]
        for e in self.pending:
            self.pending[e] = list(snap)

    def buf(self, name):
        if name not in self.bufs:
            self.bufs[name] = Buf()
        return self.bufs[name]

    def emit(self, eng, fn, reads=(), writes=(), waits=(), sig=None, dma=False):
        wl = list(waits) + self.pending[eng]
        self.pending[eng] = []
        for n in reads:
            b = self.buf(n)
            if b.wtok is not None:
                wl.append(b.wtok)
        for n in writes:
            b = self.buf(n)
            if b.wtok is not None:
                wl.append(b.wtok)
            wl.extend(b.rtoks)
        wd = {}
        for (s, v) in wl:
            if v > wd.get(s, -1):
                wd[s] = v
        tok = None
        if sig is not None:
            self.cnt[sig] = self.cnt.get(sig, 0) + (16 if dma else 1)
            tok = (sig, self.cnt[sig])
        self.ops[eng].append((fn, sorted(wd.items()), (sig, 16 if dma else 1) if sig else None))
        if tok is not None:
            for n in writes:
                b = self.buf(n)
                b.wtok = tok
                b.rtoks = []
            for n in reads:
                self.buf(n).rtoks.append(tok)
        return tok


def build_program():
    nc = bass.Bass("TRN2", target_bir_lowering=False)
    dt = nc.dram_tensor

    x_tr = dt("x_tr", [D, T], F32, kind="ExternalInput").ap()
    x_nat = dt("x_nat", [T, D], F32, kind="ExternalInput").ap()
    w_in = dt("w_in", [128, 16 * 16 * 128], F32, kind="ExternalInput").ap()
    w_out = dt("w_out", [128, 16 * 2048], F32, kind="ExternalInput").ap()
    w_g = dt("w_g", [128, 2 * 8 * 2 * 128], F32, kind="ExternalInput").ap()
    b_g = dt("b_g", [128, 16], F32, kind="ExternalInput").ap()
    b_in_row = dt("b_in_row", [1, DH], F32, kind="ExternalInput").ap()
    b_out_row = dt("b_out_row", [1, D], F32, kind="ExternalInput").ap()
    ar_col = dt("ar_col", [128, 8], F32, kind="ExternalInput").ap()
    aim_col = dt("aim_col", [128, 8], F32, kind="ExternalInput").ap()
    gate_w_nat = dt("gate_w_nat", [I_FFN, D], F32, kind="ExternalInput").ap()
    up_w_nat = dt("up_w_nat", [I_FFN, D], F32, kind="ExternalInput").ap()
    down_w_nat = dt("down_w_nat", [D, I_FFN], F32, kind="ExternalInput").ap()
    ident_f = dt("ident_f", [128, 128], F32, kind="ExternalInput").ap()
    ident_b = dt("ident_b", [128, 128], BF16, kind="ExternalInput").ap()
    ones_col_b = dt("ones_col_b", [128, 1], BF16, kind="ExternalInput").ap()
    ones_row_f = dt("ones_row_f", [1, 128], F32, kind="ExternalInput").ap()
    ones_row_b = dt("ones_row_b", [1, 128], BF16, kind="ExternalInput").ap()
    lit_in = dt("lit_in", [128, 8], F32, kind="ExternalInput").ap()

    out_hbm = dt("out", [T, D], F32, kind="ExternalOutput").ap()
    hm_sp = dt("hm_sp", [DH, T], BF16, kind="Internal").ap()
    x1_sp = dt("x1_sp", [T, D], F32, kind="Internal").ap()
    xq_sp = dt("xq_sp", [T, D], BF16, kind="Internal").ap()
    hq_sp = dt("hq_sp", [I_FFN, T], BF16, kind="Internal").ap()
    dwq_sp = dt("dwq_sp", [I_FFN, D], BF16, kind="Internal").ap()
    gwq_sp = dt("gwq_sp", [D, I_FFN], BF16, kind="Internal").ap()
    uwq_sp = dt("uwq_sp", [D, I_FFN], BF16, kind="Internal").ap()

    ctx = {}
    handles = {}
    alloc_seq = []

    def sb(name, shape, dtype):
        h = nc.sbuf_tensor(name, shape, dtype)
        t = h.__enter__()
        handles[name] = h
        ctx[name] = t
        alloc_seq.append(name)
        return t

    def free_group(names):
        ns = set(names)
        for n in reversed(alloc_seq):
            if n in ns:
                handles[n].__exit__(None, None, None)
        alloc_seq[:] = [n for n in alloc_seq if n not in ns]

    psum_handles = {}

    def psum(name, shape=(128, 512), dtype=F32):
        h = nc.psum_tensor(list(shape), dtype)
        t = h.__enter__()
        ctx[name] = t
        psum_handles[name] = h
        return t

    def psum_free(names):
        for n in names:
            psum_handles[n].__exit__(None, None, None)

    sb("c_lit", [128, 8], F32)
    sb("c_idb", [128, 128], BF16)
    sb("c_bg", [128, 16], F32)
    sb("c_ar", [128, 8], F32)
    sb("c_c1", [128, 8], F32)
    sb("c_aim", [128, 8], F32)
    sb("c_idf", [128, 128], F32)
    sb("c_ocb", [128, 1], BF16)
    sb("c_orf", [1, 128], F32)
    sb("c_binrow", [1, DH], BF16)
    sb("c_colsum", [1, DH], BF16)
    sb("c_tmp8", [128, 8], F32)
    A_GROUP = (["a_win", "a_wg", "a_xtb", "a_xh0", "a_xh1", "a_xhb", "a_gx", "a_ga",
                "a_hm0", "a_hm1", "a_rows", "a_rows_b", "a_rrep", "a_stage", "a_sm",
                "a_carry", "qa_nat0", "qa_q", "qa_dq", "qa_am",
                "qa_s", "qa_is", "qa_tst0"]
               + [f"a_b{i}_{pp}" for i in range(1, 10) for pp in range(2)])
    sb("a_win", [128, 16 * 16 * 128], BF16)
    sb("a_wg", [128, 2 * 8 * 2 * 128], BF16)
    sb("a_xtb", [128, 16 * SEG], BF16)
    sb("a_xh0", [128, 16 * SEG], F32)
    sb("a_xh1", [128, 16 * SEG], F32)
    sb("a_xhb", [128, 16 * SEG], BF16)
    sb("a_gx", [128, 8 * SEG], F32)
    sb("a_ga", [128, 8 * SEG], F32)
    sb("qa_nat0", [128, 2048], F32)
    sb("qa_q", [128, 2048], BF16)
    sb("qa_dq", [128, 2048], BF16)
    sb("qa_am", [128, 64], F32)
    sb("qa_s", [128, 64], F32)
    sb("qa_is", [128, 64], F32)
    sb("qa_tst0", [128, 1024], BF16)
    for i in range(1, 10):
        for pp in range(2):
            sb(f"a_b{i}_{pp}", [128, 8 * SUB], F32)
    sb("a_hm0", [128, 16 * SUB], BF16)
    sb("a_hm1", [128, 16 * SUB], BF16)
    sb("a_rows", [1, SEG], F32)
    sb("a_rows_b", [1, 2 * SEG], BF16)
    sb("a_rrep", [128, SEG], F32)
    sb("a_stage", [128, 4], F32)
    sb("a_sm", [128, 16], F32)
    sb("a_carry", [128, 24], F32)
    P = [psum(f"P{i}") for i in range(6)]
    PN = [f"P{i}" for i in range(6)]
    PB6 = psum("PB6", (128, 1024), BF16)
    PB7 = psum("PB7", (128, 1024), BF16)

    p = Plan()

    def em(eng, fn, reads=(), writes=(), waits=(), sig=None, dma=False):
        return p.emit(eng, fn, reads, writes, waits, sig, dma)

    SV, SS, ST, SG = "v", "s", "t", "g"

    LITS = [0.0, 1.0, 2.0, 0.5, EPS, PI / 2, -PI, 1e-24]
    for i, v in enumerate(LITS):
        nc.const_aps.register(F32, v, ctx["c_lit"][:, i:i + 1])
    em("sync", lambda e: e.dma_start(out=ctx["c_lit"][:], in_=lit_in),
       writes=["c_lit"], sig="dh", dma=True)
    p.barrier()

    for (dst, srcap) in [("c_bg", b_g), ("c_ar", ar_col), ("c_aim", aim_col),
                         ("c_idf", ident_f), ("c_idb", ident_b), ("c_ocb", ones_col_b),
                         ("c_orf", ones_row_f)]:
        em("sync", (lambda e, d=dst, s=srcap: e.dma_start(out=ctx[d][:], in_=s)),
           writes=[dst], sig="dh", dma=True)
    em("gpsimd", lambda e: e.dma_start(out=ctx["c_binrow"][:], in_=b_in_row),
       writes=["c_binrow"], sig="dg", dma=True)
    em("gpsimd", lambda e: e.dma_start(out=ctx["c_boutrow"][:], in_=b_out_row),
       writes=["c_boutrow"], sig="dg", dma=True)
    em("gpsimd", lambda e: e.dma_start(out=ctx["a_win"][:], in_=w_in),
       writes=["a_win"], sig="dg", dma=True)
    em("gpsimd", lambda e: e.dma_start(out=ctx["a_wg"][:], in_=w_g),
       writes=["a_wg"], sig="dg", dma=True)
    em("vector", lambda e: e.memset(ctx["a_carry"][:], 0.0), writes=["a_carry"], sig=SV)
    em("scalar", lambda e: e.activation(ctx["c_tmp8"][:], ctx["c_ar"][:], ACT.Exp),
       reads=["c_ar"], writes=["c_tmp8"], sig=SS)
    em("scalar", lambda e: e.activation(ctx["c_tmp8"][:], ctx["c_tmp8"][:] if False else ctx["c_tmp8"][:], ACT.Ln, bias=1.0) if False else None,
       sig=None) if False else None
    em("scalar", lambda e: e.activation(ctx["c_c1"][:], ctx["c_tmp8"][:], ACT.Ln, bias=1.0),
       reads=["c_tmp8"], writes=["c_c1"], sig=SS)
    em("vector", lambda e: e.tensor_scalar(ctx["c_c1"][:], ctx["c_c1"][:], -8.0, None, ALU.mult),
       reads=["c_c1"], writes=["c_c1"], sig=SV)

    for cc in range(4):
        for c4 in range(4):
            cht = cc * 4 + c4
            for kt in range(16):
                r0 = ctx["a_win"][:, (kt * 16 + cht) * 128:(kt * 16 + cht + 1) * 128]
                em("tensor",
                   (lambda e, o=P[5].ap()[0:1, c4 * 128:(c4 + 1) * 128], l=ctx["c_ocb"][:],
                    r=r0, st=(kt == 0), sp=(kt == 15): e.matmul(o, l, r, start=st, stop=sp)),
                   reads=["a_win", "c_ocb"], writes=["P5"],
                   sig=ST if (kt == 15 and c4 == 3) else None)
        em("vector",
           (lambda e, cc=cc: e.tensor_copy(ctx["c_colsum"][0:1, cc * 512:(cc + 1) * 512],
                                           P[5].ap()[0:1, 0:512])),
           reads=["P5"], writes=["c_colsum"], sig=SV)

    def quant_unit(kind, i, par):
        nat = "qa_nat0"
        if kind == 0:
            src_ap = gate_w_nat[i * 128:(i + 1) * 128, :]
            dst = gwq_sp
            roff, coff = 0, i * 128
        elif kind == 1:
            src_ap = up_w_nat[i * 128:(i + 1) * 128, :]
            dst = uwq_sp
            roff, coff = 0, i * 128
        else:
            n_i, q_i = i % 16, i // 16
            src_ap = down_w_nat[n_i * 128:(n_i + 1) * 128, q_i * 2048:(q_i + 1) * 2048]
            dst = dwq_sp
            roff, coff = q_i * 16, n_i * 128
        dstn = ["gwq_sp", "uwq_sp", "dwq_sp"][kind]
        em("gpsimd", (lambda e, s=src_ap, nat=nat: e.dma_start(out=ctx[nat][:], in_=s)),
           writes=[nat], sig="dg", dma=True)
        em("vector",
           (lambda e, nat=nat: e.tensor_reduce(
               ctx["qa_am"][:], ctx[nat][:].rearrange("p (nb k) -> p nb k", k=32),
               AXX, ALU.max, apply_absolute_value=True)),
           reads=[nat], writes=["qa_am"], sig=SV)
        em("vector",
           (lambda e: e.tensor_scalar(ctx["qa_s"][:], ctx["qa_am"][:], 1.0 / 448.0,
                                      1e-30, ALU.mult, ALU.max)),
           reads=["qa_am"], writes=["qa_s"], sig=SV)
        em("vector", (lambda e: e.reciprocal(ctx["qa_is"][:], ctx["qa_s"][:])),
           reads=["qa_s"], writes=["qa_is"], sig=SV)
        em("vector",
           (lambda e, nat=nat: e._custom_dve(
               FP8_RNE_MUL,
               out=ctx["qa_q"][:].rearrange("p (nb k) -> p nb k", k=32),
               in0=ctx["qa_is"][:].unsqueeze(2).broadcast_to([128, 64, 32]),
               in1=ctx[nat][:].rearrange("p (nb k) -> p nb k", k=32),
               s0=FP8_NOTMASK, s1=FP8_C1, imm2=FP8_C2)),
           reads=["qa_is", nat], writes=["qa_q"], sig=SV)
        em("gpsimd",
           (lambda e: e.tensor_tensor(
               ctx["qa_dq"][:].rearrange("p (nb k) -> p nb k", k=32),
               ctx["qa_q"][:].rearrange("p (nb k) -> p nb k", k=32),
               ctx["qa_s"][:].unsqueeze(2).broadcast_to([128, 64, 32]), ALU.mult)),
           reads=["qa_q", "qa_s"], writes=["qa_dq"], sig=SG)
        for batch in range(2):
            tst = "qa_tst0"
            pbk = PB6 if batch == 0 else PB7
            pbn = "PB6" if batch == 0 else "PB7"
            for k8 in range(8):
                kt = batch * 8 + k8
                em("tensor",
                   (lambda e, o=pbk.ap()[:, k8 * 128:(k8 + 1) * 128],
                    i_=ctx["qa_dq"][:, kt * 128:(kt + 1) * 128]:
                    e.transpose(o, i_, ctx["c_idb"][:])),
                   reads=["qa_dq", "c_idb"], writes=[pbn], sig=ST if k8 == 7 else None)
            em("scalar",
               (lambda e, tst=tst, pbk=pbk: e.activation(ctx[tst][:], pbk.ap()[:, 0:1024], ACT.Copy)),
               reads=[pbn], writes=[tst], sig=SS)
            em("sync",
               (lambda e, dst=dst, roff=roff, coff=coff, batch=batch, tst=tst: e.dma_start(
                   out=dst.rearrange("(r p) n -> p r n", p=128)[
                       :, roff + batch * 8:roff + (batch + 1) * 8, coff:coff + 128],
                   in_=ctx[tst][:].rearrange("p (r n) -> p r n", r=8))),
               reads=[tst], writes=[dstn], sig="dh", dma=True)

    QUANT_UNITS = ([(0, i) for i in range(64)] + [(1, i) for i in range(64)]
                   + [(2, i) for i in range(64)])
    q_cursor = [0]

    def emit_quant_batch(n):
        for _ in range(n):
            if q_cursor[0] >= len(QUANT_UNITS):
                return
            kind, i = QUANT_UNITS[q_cursor[0]]
            quant_unit(kind, i, q_cursor[0] % 2)
            q_cursor[0] += 1

    for seg in range(NSEG):
        emit_quant_batch(24)
        t0 = seg * SEG
        xhn = f"a_xh{seg % 2}"
        em("gpsimd",
           (lambda e, t0=t0: e.dma_start(
               out=ctx["a_xtb"][:].rearrange("p (kt t) -> p kt t", kt=16),
               in_=x_tr.rearrange("(kt p) t -> p kt t", p=128)[:, :, t0:t0 + SEG])),
           writes=["a_xtb"], sig="dg", dma=True)

        for c in range(SEG // 128):
            for kt in range(16):
                xt = ctx["a_xtb"][:, kt * SEG + c * 128: kt * SEG + (c + 1) * 128]
                em("tensor",
                   (lambda e, o=P[4].ap()[:, 0:128], l=xt, r=xt, st=(kt == 0), sp=(kt == 15):
                    e.matmul(o, l, r, start=st, stop=sp)),
                   reads=["a_xtb"], writes=["P4"], sig=None)
            for kt in range(16):
                xt = ctx["a_xtb"][:, kt * SEG + c * 128: kt * SEG + (c + 1) * 128]
                em("tensor",
                   (lambda e, o=P[4].ap()[:, 128:129], l=xt, r=ctx["c_ocb"][:],
                    st=(kt == 0), sp=(kt == 15): e.matmul(o, l, r, start=st, stop=sp)),
                   reads=["a_xtb", "c_ocb"], writes=["P4"], sig=ST if kt == 15 else None)
            em("vector",
               (lambda e: e.tensor_tensor_reduce(ctx["a_b1_0"][:, 0:128], P[4].ap()[:, 0:128],
                                                 ctx["c_idf"][:], 1.0, 0.0, ALU.mult,
                                                 ALU.add, ctx["a_sm"][:, 0:1])),
               reads=["P4", "c_idf"], writes=["a_b1_0", "a_sm"], sig=SV)
            em("vector",
               (lambda e: e.tensor_scalar(ctx["a_sm"][:, 1:2], P[4].ap()[:, 128:129],
                                          1.0 / D, None, ALU.mult)),
               reads=["P4"], writes=["a_sm"], sig=SV)
            em("vector",
               (lambda e: e.tensor_scalar(ctx["a_sm"][:, 2:3], ctx["a_sm"][:, 0:1],
                                          1.0 / D, None, ALU.mult)),
               reads=["a_sm"], writes=["a_sm"], sig=SV)
            em("vector",
               (lambda e: e.scalar_tensor_tensor(ctx["a_sm"][:, 3:4], ctx["a_sm"][:, 1:2],
                                                 -1.0, ctx["a_sm"][:, 1:2], ALU.mult, ALU.mult)),
               reads=["a_sm"], writes=["a_sm"], sig=SV)
            em("vector",
               (lambda e: e.tensor_tensor(ctx["a_sm"][:, 4:5], ctx["a_sm"][:, 2:3],
                                          ctx["a_sm"][:, 3:4], ALU.add)),
               reads=["a_sm"], writes=["a_sm"], sig=SV)
            em("scalar",
               (lambda e: e.activation(ctx["a_stage"][:, 1:2], ctx["a_sm"][:, 4:5],
                                       ACT.Sqrt, bias=EPS)),
               reads=["a_sm"], writes=["a_stage"], sig=SS)
            em("vector",
               (lambda e: e.reciprocal(ctx["a_stage"][:, 2:3], ctx["a_stage"][:, 1:2])),
               reads=["a_stage"], writes=["a_stage"], sig=SV)
            em("vector",
               (lambda e: e.scalar_tensor_tensor(ctx["a_stage"][:, 0:1], ctx["a_sm"][:, 1:2],
                                                 -1.0, ctx["a_stage"][:, 2:3], ALU.mult, ALU.mult)),
               reads=["a_sm", "a_stage"], writes=["a_stage"], sig=SV)
            for k in range(3):
                em("tensor",
                   (lambda e, k=k: e.transpose(P[5].ap()[0:1, k * 128:(k + 1) * 128],
                                               ctx["a_stage"][:, k:k + 1], ctx["c_idf"][:])),
                   reads=["a_stage", "c_idf"], writes=["P5"], sig=ST if k == 2 else None)
            em("vector",
               (lambda e, c=c: e.tensor_copy(
                   ctx["a_rows_b"][0:1, :].rearrange("o (r s) -> o r s", r=2)[:, :, c * 128:(c + 1) * 128],
                   P[5].ap()[0:1, 0:256].rearrange("o (r s) -> o r s", r=2))),
               reads=["P5"], writes=["a_rows_b"], sig=SV)
            em("vector",
               (lambda e, c=c: e.tensor_copy(ctx["a_rows"][0:1, c * 128:(c + 1) * 128],
                                             P[5].ap()[0:1, 256:384])),
               reads=["P5"], writes=["a_rows"], sig=SV)
        em("tensor",
           (lambda e: e.matmul(P[5].ap()[:, 0:SEG], ctx["c_orf"][0:1, :],
                               ctx["a_rows"][0:1, 0:SEG], start=True, stop=True)),
           reads=["c_orf", "a_rows"], writes=["P5"], sig=ST)
        em("vector", (lambda e: e.tensor_copy(ctx["a_rrep"][:], P[5].ap()[:, 0:SEG])),
           reads=["P5"], writes=["a_rrep"], sig=SV)

        for cht in range(16):
            bk, bnm = P[cht % 2], PN[cht % 2]
            for kt in range(16):
                l = ctx["a_win"][:, (kt * 16 + cht) * 128:(kt * 16 + cht + 1) * 128]
                r = ctx["a_xtb"][:, kt * SEG:(kt + 1) * SEG]
                em("tensor",
                   (lambda e, o=bk.ap()[:, 0:SEG], l=l, r=r, st=(kt == 0):
                    e.matmul(o, l, r, start=st, stop=False)),
                   reads=(["a_win", "a_xtb"] if kt < 15 else ["a_win", "a_xtb"]),
                   writes=[bnm], sig=ST if kt == 15 else None)
            em("tensor",
               (lambda e, o=bk.ap()[:, 0:SEG], l=ctx["c_colsum"][0:1, cht * 128:(cht + 1) * 128],
                r=ctx["a_rows_b"][0:1, 0:SEG]: e.matmul(o, l, r, start=False, stop=False)),
               reads=["c_colsum", "a_rows_b"], writes=[bnm], sig=None)
            em("tensor",
               (lambda e, o=bk.ap()[:, 0:SEG], l=ctx["c_binrow"][0:1, cht * 128:(cht + 1) * 128],
                r=ctx["a_rows_b"][0:1, SEG:2 * SEG]: e.matmul(o, l, r, start=False, stop=True)),
               reads=["c_binrow", "a_rows_b"], writes=[bnm], sig=ST)
            em("vector",
               (lambda e, cht=cht, bk=bk, xhn=xhn: e.tensor_tensor(
                   ctx[xhn][:, cht * SEG:(cht + 1) * SEG], bk.ap()[:, 0:SEG],
                   ctx["a_rrep"][:], ALU.mult)),
               reads=[bnm, "a_rrep"], writes=[xhn], sig=SV)
            em("vector",
               (lambda e, cht=cht, xhn=xhn: e.tensor_copy(
                   ctx["a_xhb"][:, cht * SEG:(cht + 1) * SEG],
                   ctx[xhn][:, cht * SEG:(cht + 1) * SEG])),
               reads=[xhn], writes=["a_xhb"], sig=SV)

        for blk in range(8):
            for g in range(2):
                bk, bnm = P[2 + (blk * 2 + g) % 2], PN[2 + (blk * 2 + g) % 2]
                for kt2 in range(2):
                    l = ctx["a_wg"][:, ((g * 8 + blk) * 2 + kt2) * 128:
                                    ((g * 8 + blk) * 2 + kt2 + 1) * 128]
                    r = ctx["a_xhb"][:, (blk * 2 + kt2) * SEG:(blk * 2 + kt2 + 1) * SEG]
                    em("tensor",
                       (lambda e, o=bk.ap()[:, 0:SEG], l=l, r=r, st=(kt2 == 0), sp=(kt2 == 1):
                        e.matmul(o, l, r, start=st, stop=sp)),
                       reads=["a_wg", "a_xhb"], writes=[bnm], sig=ST if kt2 == 1 else None)
                dstn = "a_gx" if g == 0 else "a_ga"
                em("scalar",
                   (lambda e, bk=bk, dstn=dstn, blk=blk, g=g: e.activation(
                       ctx[dstn][:, blk * SEG:(blk + 1) * SEG], bk.ap()[:, 0:SEG],
                       ACT.Sigmoid, bias=ctx["c_bg"][:, g * 8 + blk:g * 8 + blk + 1])),
                   reads=[bnm, "c_bg"], writes=[dstn], sig=SS)

        c1b = ctx["c_c1"][:].unsqueeze(2).broadcast_to([128, 8, SUB])
        aimb = ctx["c_aim"][:].unsqueeze(2).broadcast_to([128, 8, SUB])
        for sub in range(NSUB):
            spp = (seg * NSUB + sub) % 2
            ts = slice(sub * SUB, (sub + 1) * SUB)
            gas = ctx["a_ga"][:].rearrange("p (g t) -> p g t", g=8)[:, :, ts]
            gxs = ctx["a_gx"][:].rearrange("p (g t) -> p g t", g=8)[:, :, ts]
            xh16 = ctx[xhn][:].rearrange("p (g t) -> p g t", g=16)
            xr = xh16[:, 0:8, ts]
            xi = xh16[:, 8:16, ts]
            hmn = f"a_hm{spp}"
            hm16 = ctx[hmn][:].rearrange("p (g t) -> p g t", g=16)

            def bb(i, spp=spp):
                return ctx[f"a_b{i}_{spp}"][:].rearrange("p (g t) -> p g t", g=8)

            def bf(i, spp=spp):
                return ctx[f"a_b{i}_{spp}"][:]

            def bn(i, spp=spp):
                return f"a_b{i}_{spp}"

            def V(o, fn, rd, wr):
                em("vector", fn, reads=rd, writes=wr, sig=SV)

            em("vector", (lambda e, o=bb(1), a=gas, b=c1b: e.tensor_tensor(o, a, b, ALU.mult)),
               reads=["a_ga", "c_c1"], writes=[bn(1)], sig=SV)
            em("vector", (lambda e, o=bb(2), a=gas, b=aimb: e.tensor_tensor(o, a, b, ALU.mult)),
               reads=["a_ga", "c_aim"], writes=[bn(2)], sig=SV)
            em("scalar", (lambda e, o=bf(3), i=bf(1): e.activation(o, i, ACT.Exp)),
               reads=[bn(1)], writes=[bn(3)], sig=SS)
            em("scalar", (lambda e, o=bf(4), i=bf(1): e.activation(o, i, ACT.Exp, scale=2.0)),
               reads=[bn(1)], writes=[bn(4)], sig=SS)
            em("scalar", (lambda e, o=bf(5), i=bf(2): e.activation(o, i, ACT.Sin, scale=2.0, bias=PI / 2)),
               reads=[bn(2)], writes=[bn(5)], sig=SS)
            em("scalar", (lambda e, o=bf(6), i=bf(2): e.activation(o, i, ACT.Sin, scale=2.0)),
               reads=[bn(2)], writes=[bn(6)], sig=SS)
            em("vector", (lambda e, o=bf(1), a=bf(4), b=bf(5): e.tensor_tensor(o, a, b, ALU.mult)),
               reads=[bn(4), bn(5)], writes=[bn(1)], sig=SV)
            em("vector", (lambda e, o=bf(5), a=bf(4), b=bf(6): e.tensor_tensor(o, a, b, ALU.mult)),
               reads=[bn(4), bn(6)], writes=[bn(5)], sig=SV)
            em("vector", (lambda e, o=bf(4), i=bf(1): e.tensor_scalar(o, i, -1.0, 1.0, ALU.mult, ALU.add)),
               reads=[bn(1)], writes=[bn(4)], sig=SV)
            em("scalar", (lambda e, o=bf(1), i=bf(4): e.activation(o, i, ACT.Square)),
               reads=[bn(4)], writes=[bn(1)], sig=SS)
            em("scalar", (lambda e, o=bf(6), i=bf(5): e.activation(o, i, ACT.Square)),
               reads=[bn(5)], writes=[bn(6)], sig=SS)
            em("vector", (lambda e, o=bf(6), a=bf(1), b=bf(6): e.tensor_tensor(o, a, b, ALU.add)),
               reads=[bn(1), bn(6)], writes=[bn(6)], sig=SV)
            em("scalar", (lambda e, o=bf(1), i=bf(6): e.activation(o, i, ACT.Sqrt)),
               reads=[bn(6)], writes=[bn(1)], sig=SS)
            em("vector", (lambda e, o=bf(6), a=bf(1), b=bf(4): e.tensor_tensor(o, a, b, ALU.add)),
               reads=[bn(1), bn(4)], writes=[bn(6)], sig=SV)
            em("scalar", (lambda e, o=bf(1), i=bf(6): e.activation(o, i, ACT.Sqrt, scale=0.5, bias=1e-24)),
               reads=[bn(6)], writes=[bn(1)], sig=SS)
            em("vector", (lambda e, o=bf(6), i=bf(1): e._custom_dve(
                RCF, out=o, in0=i, s0=RCF_C["s0"], s1=RCF_C["s1"], imm2=RCF_C["imm2"])),
               reads=[bn(1)], writes=[bn(6)], sig=SV)
            em("vector", (lambda e, o=bf(7), i=bf(5), b=bf(6): e.scalar_tensor_tensor(
                o, i, -0.5, b, ALU.mult, ALU.mult)),
               reads=[bn(5), bn(6)], writes=[bn(7)], sig=SV)
            for grp in range(8):
                em("vector",
                   (lambda e, grp=grp, spp=spp: e.tensor_tensor_scan(
                       ctx[f"a_b8_{spp}"][:, grp * SUB:(grp + 1) * SUB],
                       ctx[f"a_b2_{spp}"][:, grp * SUB:(grp + 1) * SUB],
                       ctx[f"a_b2_{spp}"][:, grp * SUB:(grp + 1) * SUB],
                       ctx["a_carry"][:, 16 + grp:17 + grp], ALU.add, ALU.bypass)),
                   reads=[bn(2), "a_carry"], writes=[bn(8)], sig=SV if grp == 7 else None)
            em("vector",
               (lambda e, spp=spp: e.tensor_copy(
                   ctx["a_carry"][:, 16:24],
                   ctx[f"a_b8_{spp}"][:].rearrange("p (g t) -> p g t", g=8)[:, :, SUB - 1:SUB].squeeze(2))),
               reads=[bn(8)], writes=["a_carry"], sig=SV)
            em("vector", (lambda e, o=bf(2), i=bf(8): e._custom_dve(
                WRAP_RNE, out=o, in0=i, s0=INV2PI, s1=WRAP_MAGIC, imm2=TWOPI)),
               reads=[bn(8)], writes=[bn(2)], sig=SV)
            em("vector", (lambda e, o=bf(4), i=bf(2): e.tensor_scalar(
                o, i, -PI_CLAMP, PI_CLAMP, ALU.max, ALU.min)),
               reads=[bn(2)], writes=[bn(4)], sig=SV)
            em("scalar", (lambda e, o=bf(2), i=bf(4): e.activation(o, i, ACT.Sin)),
               reads=[bn(4)], writes=[bn(2)], sig=SS)
            em("scalar", (lambda e, o=bf(5), i=bf(4): e.activation(o, i, ACT.Abs)),
               reads=[bn(4)], writes=[bn(5)], sig=SS)
            em("scalar", (lambda e, o=bf(4), i=bf(5): e.activation(o, i, ACT.Sin, scale=-1.0, bias=PI / 2)),
               reads=[bn(5)], writes=[bn(4)], sig=SS)
            em("vector", (lambda e, o=bf(5), a=bf(1), b=bf(4): e.tensor_tensor(o, a, b, ALU.mult)),
               reads=[bn(1), bn(4)], writes=[bn(5)], sig=SV)
            em("vector", (lambda e, o=bf(9), a=bf(7), b=bf(2): e.tensor_tensor(o, a, b, ALU.mult)),
               reads=[bn(7), bn(2)], writes=[bn(9)], sig=SV)
            em("vector", (lambda e, o=bf(9), a=bf(5), b=bf(9): e.tensor_tensor(o, a, b, ALU.add)),
               reads=[bn(5), bn(9)], writes=[bn(9)], sig=SV)
            em("vector", (lambda e, o=bb(5), a=gxs, b=bb(9): e.tensor_tensor(o, a, b, ALU.mult)),
               reads=["a_gx", bn(9)], writes=[bn(5)], sig=SV)
            em("vector", (lambda e, o=bf(9), a=bf(7), b=bf(4): e.tensor_tensor(o, a, b, ALU.mult)),
               reads=[bn(7), bn(4)], writes=[bn(9)], sig=SV)
            em("vector", (lambda e, o=bf(6), a=bf(1), b=bf(2): e.tensor_tensor(o, a, b, ALU.mult)),
               reads=[bn(1), bn(2)], writes=[bn(6)], sig=SV)
            em("vector", (lambda e, o=bf(9), a=bf(9), b=bf(6): e.tensor_tensor(o, a, b, ALU.subtract)),
               reads=[bn(9), bn(6)], writes=[bn(9)], sig=SV)
            em("vector", (lambda e, o=bb(6), a=gxs, b=bb(9): e.tensor_tensor(o, a, b, ALU.mult)),
               reads=["a_gx", bn(9)], writes=[bn(6)], sig=SV)
            em("gpsimd", (lambda e, o=bb(1), a=xr, b=bb(5): e.tensor_tensor(o, a, b, ALU.mult)),
               reads=[xhn, bn(5)], writes=[bn(1)], sig=SG)
            em("gpsimd", (lambda e, o=bb(9), a=xi, b=bb(6): e.tensor_tensor(o, a, b, ALU.mult)),
               reads=[xhn, bn(6)], writes=[bn(9)], sig=SG)
            em("gpsimd", (lambda e, o=bf(4), a=bf(1), b=bf(9): e.tensor_tensor(o, a, b, ALU.subtract)),
               reads=[bn(1), bn(9)], writes=[bn(4)], sig=SG)
            em("gpsimd", (lambda e, o=bb(1), a=xr, b=bb(6): e.tensor_tensor(o, a, b, ALU.mult)),
               reads=[xhn, bn(6)], writes=[bn(1)], sig=SG)
            em("gpsimd", (lambda e, o=bb(9), a=xi, b=bb(5): e.tensor_tensor(o, a, b, ALU.mult)),
               reads=[xhn, bn(5)], writes=[bn(9)], sig=SG)
            em("gpsimd", (lambda e, o=bf(2), a=bf(1), b=bf(9): e.tensor_tensor(o, a, b, ALU.add)),
               reads=[bn(1), bn(9)], writes=[bn(2)], sig=SG)
            for grp in range(8):
                em("vector",
                   (lambda e, grp=grp, spp=spp: e.tensor_tensor_scan(
                       ctx[f"a_b5_{spp}"][:, grp * SUB:(grp + 1) * SUB],
                       ctx[f"a_b3_{spp}"][:, grp * SUB:(grp + 1) * SUB],
                       ctx[f"a_b4_{spp}"][:, grp * SUB:(grp + 1) * SUB],
                       ctx["a_carry"][:, grp:grp + 1], ALU.mult, ALU.add)),
                   reads=[bn(3), bn(4), "a_carry"], writes=[bn(5)], sig=SV if grp == 7 else None)
            for grp in range(8):
                em("vector",
                   (lambda e, grp=grp, spp=spp: e.tensor_tensor_scan(
                       ctx[f"a_b6_{spp}"][:, grp * SUB:(grp + 1) * SUB],
                       ctx[f"a_b3_{spp}"][:, grp * SUB:(grp + 1) * SUB],
                       ctx[f"a_b2_{spp}"][:, grp * SUB:(grp + 1) * SUB],
                       ctx["a_carry"][:, 8 + grp:9 + grp], ALU.mult, ALU.add)),
                   reads=[bn(3), bn(2), "a_carry"], writes=[bn(6)], sig=SV if grp == 7 else None)
            em("vector",
               (lambda e, spp=spp: e.tensor_copy(
                   ctx["a_carry"][:, 0:8],
                   ctx[f"a_b5_{spp}"][:].rearrange("p (g t) -> p g t", g=8)[:, :, SUB - 1:SUB].squeeze(2))),
               reads=[bn(5)], writes=["a_carry"], sig=SV)
            em("vector",
               (lambda e, spp=spp: e.tensor_copy(
                   ctx["a_carry"][:, 8:16],
                   ctx[f"a_b6_{spp}"][:].rearrange("p (g t) -> p g t", g=8)[:, :, SUB - 1:SUB].squeeze(2))),
               reads=[bn(6)], writes=["a_carry"], sig=SV)
            em("vector", (lambda e, o=bf(1), i=bf(8): e._custom_dve(
                WRAP_RNE, out=o, in0=i, s0=INV2PI, s1=WRAP_MAGIC, imm2=TWOPI)),
               reads=[bn(8)], writes=[bn(1)], sig=SV)
            em("vector", (lambda e, o=bf(4), i=bf(1): e.tensor_scalar(
                o, i, -PI_CLAMP, PI_CLAMP, ALU.max, ALU.min)),
               reads=[bn(1)], writes=[bn(4)], sig=SV)
            em("scalar", (lambda e, o=bf(1), i=bf(4): e.activation(o, i, ACT.Sin)),
               reads=[bn(4)], writes=[bn(1)], sig=SS)
            em("scalar", (lambda e, o=bf(7), i=bf(4): e.activation(o, i, ACT.Abs)),
               reads=[bn(4)], writes=[bn(7)], sig=SS)
            em("scalar", (lambda e, o=bf(4), i=bf(7): e.activation(o, i, ACT.Sin, scale=-1.0, bias=PI / 2)),
               reads=[bn(7)], writes=[bn(4)], sig=SS)
            em("gpsimd", (lambda e, o=bf(7), a=bf(5), b=bf(4): e.tensor_tensor(o, a, b, ALU.mult)),
               reads=[bn(5), bn(4)], writes=[bn(7)], sig=SG)
            em("gpsimd", (lambda e, o=bf(9), a=bf(6), b=bf(1): e.tensor_tensor(o, a, b, ALU.mult)),
               reads=[bn(6), bn(1)], writes=[bn(9)], sig=SG)
            em("gpsimd", (lambda e, o=hm16[:, 0:8, :], a=bb(7), b=bb(9):
                          e.tensor_tensor(o, a, b, ALU.subtract)),
               reads=[bn(7), bn(9)], writes=[hmn], sig=SG)
            em("gpsimd", (lambda e, o=bf(7), a=bf(5), b=bf(1): e.tensor_tensor(o, a, b, ALU.mult)),
               reads=[bn(5), bn(1)], writes=[bn(7)], sig=SG)
            em("gpsimd", (lambda e, o=bf(9), a=bf(6), b=bf(4): e.tensor_tensor(o, a, b, ALU.mult)),
               reads=[bn(6), bn(4)], writes=[bn(9)], sig=SG)
            em("gpsimd", (lambda e, o=hm16[:, 8:16, :], a=bb(7), b=bb(9):
                          e.tensor_tensor(o, a, b, ALU.add)),
               reads=[bn(7), bn(9)], writes=[hmn], sig=SG)
            em("sync",
               (lambda e, ts0=t0 + sub * SUB, hmn=hmn: e.dma_start(
                   out=hm_sp.rearrange("(kt p) t -> p kt t", p=128)[:, :, ts0:ts0 + SUB],
                   in_=ctx[hmn][:].rearrange("p (kt t) -> p kt t", kt=16))),
               reads=[hmn], writes=[f"hm_sp{seg}"], sig="dh", dma=True)

    p.barrier()
    free_group(A_GROUP)
    B_GROUP = ["b_wout", "b_hmt0", "b_hmt1", "b_y0", "b_y1", "b_xnat", "b_x1", "b_xn",
               "b_qx", "b_dqx", "b_scr0", "b_am", "b_s", "b_is",
               "b_sm0", "b_sm1", "c_boutrow", "c_orb"]
    sb("c_xqt", [128, 16 * 2048], BF16)
    sb("b_wout", [128, 16 * 2048], BF16)
    sb("b_hmt0", [128, 16 * 128], BF16)
    sb("b_hmt1", [128, 16 * 128], BF16)
    sb("b_y0", [128, 2048], F32)
    sb("b_y1", [128, 2048], F32)
    sb("b_xnat", [128, 2048], F32)
    sb("b_x1", [128, 2048], F32)
    sb("b_xn", [128, 2048], F32)
    sb("b_qx", [128, 2048], BF16)
    sb("b_dqx", [128, 2048], BF16)
    sb("b_scr0", [128, 2048], F32)
    sb("b_am", [128, 64], F32)
    sb("b_s", [128, 64], F32)
    sb("b_is", [128, 64], F32)
    sb("b_sm0", [128, 16], F32)
    sb("b_sm1", [128, 16], F32)
    sb("c_boutrow", [1, D], BF16)
    sb("c_orb", [1, 128], BF16)
    em("gpsimd", lambda e: e.dma_start(out=ctx["c_boutrow"][:], in_=b_out_row),
       writes=["c_boutrow"], sig="dg", dma=True)
    em("sync", lambda e: e.dma_start(out=ctx["c_orb"][:], in_=ones_row_b),
       writes=["c_orb"], sig="dh", dma=True)
    em("gpsimd", lambda e: e.dma_start(out=ctx["b_wout"][:], in_=w_out),
       writes=["b_wout"], sig="dg", dma=True)

    for tt in range(NTT):
        hmt = "b_hmt0" if tt % 2 == 0 else "b_hmt1"
        byn = f"b_y{tt % 2}"
        bscrn = "b_scr0"
        bsmn = f"b_sm{tt % 2}"
        em("sync",
           (lambda e, tt=tt, hmt=hmt: e.dma_start(
               out=ctx[hmt][:].rearrange("p (kt t) -> p kt t", kt=16),
               in_=hm_sp.rearrange("(kt p) t -> p kt t", p=128)[:, :, tt * 128:(tt + 1) * 128])),
           reads=[f"hm_sp{tt // 4}"], writes=[hmt], sig="dh", dma=True)
        em("gpsimd",
           (lambda e, tt=tt: e.dma_start(out=ctx["b_xnat"][:],
                                         in_=x_nat[tt * 128:(tt + 1) * 128, :])),
           writes=["b_xnat"], sig="dg", dma=True)
        for oc in range(4):
            bk, bnm = P[oc], PN[oc]
            for kt in range(16):
                em("tensor",
                   (lambda e, o=bk.ap()[:, 0:512],
                    l=ctx[hmt][:, kt * 128:(kt + 1) * 128],
                    r=ctx["b_wout"][:, kt * 2048 + oc * 512: kt * 2048 + (oc + 1) * 512],
                    st=(kt == 0): e.matmul(o, l, r, start=st, stop=False)),
                   reads=[hmt, "b_wout"], writes=[bnm], sig=ST if kt == 15 else None)
            em("tensor",
               (lambda e, o=bk.ap()[:, 0:512], l=ctx["c_orb"][0:1, :],
                r=ctx["c_boutrow"][0:1, oc * 512:(oc + 1) * 512]:
                e.matmul(o, l, r, start=False, stop=True)),
               reads=["c_orb", "c_boutrow"], writes=[bnm], sig=ST)
            em("scalar",
               (lambda e, oc=oc, bk=bk, byn=byn: e.activation(
                   ctx[byn][:, oc * 512:(oc + 1) * 512], bk.ap()[:, 0:512], ACT.Copy)),
               reads=[bnm], writes=[byn], sig=SS)

        def ln_stats(src, smn, c_ss, c_s, scrn=None):
            scrn = scrn or bscrn
            em("vector",
               (lambda e, scrn=scrn: e.scalar_tensor_tensor(ctx[scrn][:], ctx[src][:], 1.0,
                                                 ctx[src][:], ALU.mult, ALU.mult,
                                                 accum_out=ctx[smn][:, c_ss:c_ss + 1])),
               reads=[src], writes=[scrn, smn], sig=SV)
            em("vector",
               (lambda e, scrn=scrn: e.tensor_scalar(ctx[scrn][:], ctx[src][:], 1.0, None,
                                          ALU.mult, ALU.add,
                                          accum_out=ctx[smn][:, c_s:c_s + 1])),
               reads=[src], writes=[scrn, smn], sig=SV)

        def ln_smalls(smn, c_ss, c_s, c_mean, c_var, c_sd, c_r, c_mrn, n_elem, eng_sm="vector"):
            em(eng_sm,
               (lambda e: e.tensor_scalar(ctx[smn][:, c_mean:c_mean + 1],
                                          ctx[smn][:, c_s:c_s + 1], 1.0 / n_elem, None, ALU.mult)),
               reads=[smn], writes=[smn], sig=SV)
            em(eng_sm,
               (lambda e: e.tensor_scalar(ctx[smn][:, c_var:c_var + 1],
                                          ctx[smn][:, c_ss:c_ss + 1], 1.0 / n_elem, None, ALU.mult)),
               reads=[smn], writes=[smn], sig=SV)
            em(eng_sm,
               (lambda e: e.scalar_tensor_tensor(ctx[smn][:, c_sd:c_sd + 1],
                                                 ctx[smn][:, c_mean:c_mean + 1], -1.0,
                                                 ctx[smn][:, c_mean:c_mean + 1], ALU.mult, ALU.mult)),
               reads=[smn], writes=[smn], sig=SV)
            em(eng_sm,
               (lambda e: e.tensor_tensor(ctx[smn][:, c_var:c_var + 1],
                                          ctx[smn][:, c_var:c_var + 1],
                                          ctx[smn][:, c_sd:c_sd + 1], ALU.add)),
               reads=[smn], writes=[smn], sig=SV)
            em("scalar",
               (lambda e: e.activation(ctx[smn][:, c_sd:c_sd + 1], ctx[smn][:, c_var:c_var + 1],
                                       ACT.Sqrt, bias=EPS)),
               reads=[smn], writes=[smn], sig=SS)
            em(eng_sm,
               (lambda e: e.reciprocal(ctx[smn][:, c_r:c_r + 1], ctx[smn][:, c_sd:c_sd + 1])),
               reads=[smn], writes=[smn], sig=SV)
            em(eng_sm,
               (lambda e: e.scalar_tensor_tensor(ctx[smn][:, c_mrn:c_mrn + 1],
                                                 ctx[smn][:, c_mean:c_mean + 1], -1.0,
                                                 ctx[smn][:, c_r:c_r + 1], ALU.mult, ALU.mult)),
               reads=[smn], writes=[smn], sig=SV)

        ln_stats(byn, bsmn, 0, 1)
        ln_smalls(bsmn, 0, 1, 2, 3, 4, 5, 6, D)
        em("scalar",
           (lambda e, byn=byn, bsmn=bsmn, bscrn=bscrn: e.activation(
               ctx[bscrn][:], ctx[byn][:], ACT.Identity,
               scale=ctx[bsmn][:, 5:6], bias=ctx[bsmn][:, 6:7])),
           reads=[byn, bsmn], writes=[bscrn], sig=SS)
        em("gpsimd",
           (lambda e, bscrn=bscrn: e.tensor_tensor(ctx["b_x1"][:], ctx[bscrn][:],
                                                   ctx["b_xnat"][:], ALU.add)),
           reads=[bscrn, "b_xnat"], writes=["b_x1"], sig=SG)
        em("sync",
           (lambda e, tt=tt: e.dma_start(out=x1_sp[tt * 128:(tt + 1) * 128, :], in_=ctx["b_x1"][:])),
           reads=["b_x1"], writes=[f"x1_sp{tt}"], sig="dh", dma=True)
        ln_stats("b_x1", bsmn, 8, 9)
        ln_smalls(bsmn, 8, 9, 10, 11, 12, 13, 14, D)
        em("scalar",
           (lambda e, bsmn=bsmn: e.activation(ctx["b_xn"][:], ctx["b_x1"][:], ACT.Identity,
                                   scale=ctx[bsmn][:, 13:14], bias=ctx[bsmn][:, 14:15])),
           reads=["b_x1", bsmn], writes=["b_xn"], sig=SS)
        em("vector",
           (lambda e: e.tensor_reduce(ctx["b_am"][:],
                                      ctx["b_xn"][:].rearrange("p (nb k) -> p nb k", k=32),
                                      AXX, ALU.max, apply_absolute_value=True)),
           reads=["b_xn"], writes=["b_am"], sig=SV)
        em("vector",
           (lambda e: e.tensor_scalar(ctx["b_s"][:], ctx["b_am"][:], 1.0 / 448.0, 1e-30,
                                      ALU.mult, ALU.max)),
           reads=["b_am"], writes=["b_s"], sig=SV)
        em("vector", (lambda e: e.reciprocal(ctx["b_is"][:], ctx["b_s"][:])),
           reads=["b_s"], writes=["b_is"], sig=SV)
        em("vector",
           (lambda e: e._custom_dve(
               FP8_RNE_MUL,
               out=ctx["b_qx"][:].rearrange("p (nb k) -> p nb k", k=32),
               in0=ctx["b_is"][:].unsqueeze(2).broadcast_to([128, 64, 32]),
               in1=ctx["b_xn"][:].rearrange("p (nb k) -> p nb k", k=32),
               s0=FP8_NOTMASK, s1=FP8_C1, imm2=FP8_C2)),
           reads=["b_is", "b_xn"], writes=["b_qx"], sig=SV)
        em("vector",
           (lambda e: e.tensor_tensor(
               ctx["b_dqx"][:].rearrange("p (nb k) -> p nb k", k=32),
               ctx["b_qx"][:].rearrange("p (nb k) -> p nb k", k=32),
               ctx["b_s"][:].unsqueeze(2).broadcast_to([128, 64, 32]), ALU.mult)),
           reads=["b_qx", "b_s"], writes=["b_dqx"], sig=SV)
        em("sync",
           (lambda e, tt=tt: e.dma_start(out=xq_sp[tt * 128:(tt + 1) * 128, :], in_=ctx["b_dqx"][:])),
           reads=["b_dqx"], writes=[f"xq_sp{tt}"], sig="dh", dma=True)
        for kt in range(16):
            qeng, qsig = ("sync", "dh") if kt % 2 == 0 else ("scalar", "ds")
            em(qeng,
               (lambda e, tt=tt, kt=kt: e.dma_start(
                   out=ctx["c_xqt"][:, kt * 2048 + tt * 128: kt * 2048 + (tt + 1) * 128],
                   in_=xq_sp[tt * 128:(tt + 1) * 128, kt * 128:(kt + 1) * 128],
                   transpose=True)),
               reads=[f"xq_sp{tt}"], writes=["c_xqt"], sig=qsig, dma=True)

    p.barrier()
    free_group(B_GROUP)
    C_GROUP = ["c_xqt", "c_rhs_g0", "c_rhs_g1", "c_rhs_u0", "c_rhs_u1",
               "c_sg0", "c_sg1", "c_tg0", "c_tg1", "c_hs0", "c_hs1",
               "c_qh0", "c_qh1", "c_dqh0", "c_dqh1", "c_dqht0", "c_dqht1",
               "c_ham", "c_hss", "c_his"]
    for par in range(2):
        sb(f"c_rhs_g{par}", [128, 16 * 512], BF16)
        sb(f"c_rhs_u{par}", [128, 16 * 512], BF16)
        sb(f"c_sg{par}", [128, 512], F32)
        sb(f"c_tg{par}", [128, 512], F32)
        sb(f"c_hs{par}", [128, 512], F32)
        sb(f"c_qh{par}", [128, 512], BF16)
        sb(f"c_dqh{par}", [128, 512], BF16)
        sb(f"c_dqht{par}", [128, 512], BF16)
    sb("c_ham", [128, 16], F32)
    sb("c_hss", [128, 16], F32)
    sb("c_his", [128, 16], F32)
    PB = [PB6, PB7]
    PBN = ["PB6", "PB7"]
    def quantize(src, n_free, nblk, amn, sn, isn, qn, dqn, eng_dq):
        em("vector",
           (lambda e: e.tensor_reduce(ctx[amn][:, 0:nblk],
                                      ctx[src][:, 0:n_free].rearrange("p (nb k) -> p nb k", k=32),
                                      AXX, ALU.max, apply_absolute_value=True)),
           reads=[src], writes=[amn], sig=SV)
        em("vector",
           (lambda e: e.tensor_scalar(ctx[sn][:, 0:nblk], ctx[amn][:, 0:nblk],
                                      1.0 / 448.0, 1e-30, ALU.mult, ALU.max)),
           reads=[amn], writes=[sn], sig=SV)
        em("vector", (lambda e: e.reciprocal(ctx[isn][:, 0:nblk], ctx[sn][:, 0:nblk])),
           reads=[sn], writes=[isn], sig=SV)
        em("vector",
           (lambda e: e._custom_dve(
               FP8_RNE_MUL,
               out=ctx[qn][:, 0:n_free].rearrange("p (nb k) -> p nb k", k=32),
               in0=ctx[isn][:, 0:nblk].unsqueeze(2).broadcast_to([128, nblk, 32]),
               in1=ctx[src][:, 0:n_free].rearrange("p (nb k) -> p nb k", k=32),
               s0=FP8_NOTMASK, s1=FP8_C1, imm2=FP8_C2)),
           reads=[isn, src], writes=[qn], sig=SV)
        em(eng_dq,
           (lambda e: e.tensor_tensor(
               ctx[dqn][:, 0:n_free].rearrange("p (nb k) -> p nb k", k=32),
               ctx[qn][:, 0:n_free].rearrange("p (nb k) -> p nb k", k=32),
               ctx[sn][:, 0:nblk].unsqueeze(2).broadcast_to([128, nblk, 32]), ALU.mult)),
           reads=[qn, sn], writes=[dqn], sig=SG if eng_dq == "gpsimd" else SV)

    for hch in range(NHCH):
        par = hch % 2
        rg, ru = f"c_rhs_g{par}", f"c_rhs_u{par}"
        for (sp_t, rhsn) in ((gwq_sp, rg), (uwq_sp, ru)):
            em("gpsimd",
               (lambda e, sp_t=sp_t, rhsn=rhsn, hch=hch: e.dma_start(
                   out=ctx[rhsn][:].rearrange("p (kt n) -> p kt n", kt=16),
                   in_=sp_t.rearrange("(kt p) n -> p kt n", p=128)[
                       :, :, hch * 512:(hch + 1) * 512])),
               reads=[["gwq_sp", "uwq_sp"][0 if sp_t is gwq_sp else 1]],
               writes=[rhsn], sig="dg", dma=True)
        for tt in range(NTT):
            tpar = tt % 2
            bg_, bgn = P[tpar * 2], PN[tpar * 2]
            bu_, bun = P[tpar * 2 + 1], PN[tpar * 2 + 1]
            sgn, tgn, hsn = f"c_sg{tpar}", f"c_tg{tpar}", f"c_hs{tpar}"
            qhn, dqhn, dqhtn = f"c_qh{tpar}", f"c_dqh{tpar}", f"c_dqht{tpar}"
            for kt in range(16):
                l = ctx["c_xqt"][:, kt * 2048 + tt * 128: kt * 2048 + (tt + 1) * 128]
                em("tensor",
                   (lambda e, o=bg_.ap()[:, 0:512], l=l,
                    r=ctx[rg][:, kt * 512:(kt + 1) * 512], st=(kt == 0), sp=(kt == 15):
                    e.matmul(o, l, r, start=st, stop=sp)),
                   reads=["c_xqt", rg], writes=[bgn], sig=ST if kt == 15 else None)
            for kt in range(16):
                l = ctx["c_xqt"][:, kt * 2048 + tt * 128: kt * 2048 + (tt + 1) * 128]
                em("tensor",
                   (lambda e, o=bu_.ap()[:, 0:512], l=l,
                    r=ctx[ru][:, kt * 512:(kt + 1) * 512], st=(kt == 0), sp=(kt == 15):
                    e.matmul(o, l, r, start=st, stop=sp)),
                   reads=["c_xqt", ru], writes=[bun], sig=ST if kt == 15 else None)
            em("scalar",
               (lambda e, bg_=bg_, sgn=sgn: e.activation(ctx[sgn][:], bg_.ap()[:, 0:512], ACT.Sigmoid)),
               reads=[bgn], writes=[sgn], sig=SS)
            em("vector",
               (lambda e, bg_=bg_, sgn=sgn, tgn=tgn: e.tensor_tensor(
                   ctx[tgn][:], ctx[sgn][:], bg_.ap()[:, 0:512], ALU.mult)),
               reads=[sgn, bgn], writes=[tgn], sig=SV)
            em("vector",
               (lambda e, bu_=bu_, tgn=tgn, hsn=hsn: e.tensor_tensor(
                   ctx[hsn][:], ctx[tgn][:], bu_.ap()[:, 0:512], ALU.mult)),
               reads=[tgn, bun], writes=[hsn], sig=SV)
            quantize(hsn, 512, 16, "c_ham", "c_hss", "c_his", qhn, dqhn, "gpsimd")
            bk, bnm = PB[tpar], PBN[tpar]
            for k4 in range(4):
                em("tensor",
                   (lambda e, o=bk.ap()[:, k4 * 128:(k4 + 1) * 128],
                    i_=ctx[dqhn][:, k4 * 128:(k4 + 1) * 128]:
                    e.transpose(o, i_, ctx["c_idb"][:])),
                   reads=[dqhn, "c_idb"], writes=[bnm], sig=ST if k4 == 3 else None)
            em("scalar",
               (lambda e, bk=bk, dqhtn=dqhtn: e.activation(ctx[dqhtn][:], bk.ap()[:, 0:512], ACT.Copy)),
               reads=[bnm], writes=[dqhtn], sig=SS)
            em("sync",
               (lambda e, hch=hch, tt=tt, dqhtn=dqhtn: e.dma_start(
                   out=hq_sp.rearrange("(r p) t -> p r t", p=128)[
                       :, hch * 4:(hch + 1) * 4, tt * 128:(tt + 1) * 128],
                   in_=ctx[dqhtn][:].rearrange("p (r t) -> p r t", r=4))),
               reads=[dqhtn], writes=["hq_sp"], sig="dh", dma=True)

    p.barrier()
    free_group(C_GROUP)
    psum_free(["PB7", "PB6"])
    Q6 = psum("Q6")
    Q7 = psum("Q7")
    DP = P + [Q6, Q7]
    DPN = PN + ["Q6", "Q7"]
    sb("d_hq", [128, 64 * QT], BF16)
    sb("d_rhs0", [128, 8 * 512], BF16)
    sb("d_rhs1", [128, 8 * 512], BF16)
    for i in range(4):
        sb(f"d_y{i}", [128, 2048], F32)
    sb("d_xn1", [128, 2048], F32)
    sb("d_fin", [128, 2048], F32)
    sb("d_sm", [128, 16], F32)
    for q in range(NQT):
        em("gpsimd",
           (lambda e, q=q: e.dma_start(
               out=ctx["d_hq"][:].rearrange("p (kt t) -> p kt t", kt=64),
               in_=hq_sp.rearrange("(kt p) t -> p kt t", p=128)[:, :, q * QT:(q + 1) * QT])),
           reads=["hq_sp"], writes=["d_hq"], sig="dg", dma=True)
        for oc in range(4):
            pb = 4 * (oc % 2)
            for kg in range(8):
                rhs = "d_rhs0" if kg % 2 == 0 else "d_rhs1"
                em("sync",
                   (lambda e, kg=kg, oc=oc, rhs=rhs: e.dma_start(
                       out=ctx[rhs][:].rearrange("p (k n) -> p k n", k=8),
                       in_=dwq_sp.rearrange("(k p) n -> p k n", p=128)[
                           :, kg * 8:(kg + 1) * 8, oc * 512:(oc + 1) * 512])),
                   reads=["dwq_sp"], writes=[rhs], sig="dh", dma=True)
                for k8 in range(8):
                    kt = kg * 8 + k8
                    for t4 in range(4):
                        bk, bnm = DP[pb + t4], DPN[pb + t4]
                        em("tensor",
                           (lambda e, o=bk.ap()[:, 0:512],
                            l=ctx["d_hq"][:, kt * QT + t4 * 128: kt * QT + (t4 + 1) * 128],
                            r=ctx[rhs][:, k8 * 512:(k8 + 1) * 512],
                            st=(kt == 0), sp=(kt == 63): e.matmul(o, l, r, start=st, stop=sp)),
                           reads=["d_hq", rhs], writes=[bnm],
                           sig=ST if (kt == 63 or t4 == 3) else None)
            for t4 in range(4):
                bk, bnm = DP[pb + t4], DPN[pb + t4]
                em("scalar",
                   (lambda e, t4=t4, oc=oc, bk=bk: e.activation(
                       ctx[f"d_y{t4}"][:, oc * 512:(oc + 1) * 512], bk.ap()[:, 0:512], ACT.Copy)),
                   reads=[bnm], writes=[f"d_y{t4}"], sig=SS)
        for t4 in range(4):
            tt = q * 4 + t4
            yn = f"d_y{t4}"
            em("vector",
               (lambda e, yn=yn: e.scalar_tensor_tensor(ctx["d_fin"][:], ctx[yn][:], 1.0,
                                                        ctx[yn][:], ALU.mult, ALU.mult,
                                                        accum_out=ctx["d_sm"][:, 0:1])),
               reads=[yn], writes=["d_fin", "d_sm"], sig=SV)
            em("vector",
               (lambda e, yn=yn: e.tensor_scalar(ctx["d_fin"][:], ctx[yn][:], 1.0, None,
                                                 ALU.mult, ALU.add,
                                                 accum_out=ctx["d_sm"][:, 1:2])),
               reads=[yn], writes=["d_fin", "d_sm"], sig=SV)
            em("vector",
               (lambda e: e.tensor_scalar(ctx["d_sm"][:, 2:3], ctx["d_sm"][:, 1:2],
                                          1.0 / D, None, ALU.mult)),
               reads=["d_sm"], writes=["d_sm"], sig=SV)
            em("vector",
               (lambda e: e.tensor_scalar(ctx["d_sm"][:, 3:4], ctx["d_sm"][:, 0:1],
                                          1.0 / D, None, ALU.mult)),
               reads=["d_sm"], writes=["d_sm"], sig=SV)
            em("vector",
               (lambda e: e.scalar_tensor_tensor(ctx["d_sm"][:, 4:5], ctx["d_sm"][:, 2:3],
                                                 -1.0, ctx["d_sm"][:, 2:3], ALU.mult, ALU.mult)),
               reads=["d_sm"], writes=["d_sm"], sig=SV)
            em("vector",
               (lambda e: e.tensor_tensor(ctx["d_sm"][:, 3:4], ctx["d_sm"][:, 3:4],
                                          ctx["d_sm"][:, 4:5], ALU.add)),
               reads=["d_sm"], writes=["d_sm"], sig=SV)
            em("scalar",
               (lambda e: e.activation(ctx["d_sm"][:, 5:6], ctx["d_sm"][:, 3:4], ACT.Sqrt, bias=EPS)),
               reads=["d_sm"], writes=["d_sm"], sig=SS)
            em("vector",
               (lambda e: e.reciprocal(ctx["d_sm"][:, 6:7], ctx["d_sm"][:, 5:6])),
               reads=["d_sm"], writes=["d_sm"], sig=SV)
            em("vector",
               (lambda e: e.scalar_tensor_tensor(ctx["d_sm"][:, 7:8], ctx["d_sm"][:, 2:3],
                                                 -1.0, ctx["d_sm"][:, 6:7], ALU.mult, ALU.mult)),
               reads=["d_sm"], writes=["d_sm"], sig=SV)
            em("gpsimd",
               (lambda e, tt=tt: e.dma_start(out=ctx["d_xn1"][:],
                                             in_=x1_sp[tt * 128:(tt + 1) * 128, :])),
               reads=[f"x1_sp{tt}"], writes=["d_xn1"], sig="dg", dma=True)
            em("scalar",
               (lambda e, yn=yn: e.activation(ctx["d_fin"][:], ctx[yn][:], ACT.Identity,
                                              scale=ctx["d_sm"][:, 6:7], bias=ctx["d_sm"][:, 7:8])),
               reads=[yn, "d_sm"], writes=["d_fin"], sig=SS)
            em("vector",
               (lambda e: e.tensor_tensor(ctx["d_fin"][:], ctx["d_fin"][:], ctx["d_xn1"][:], ALU.add)),
               reads=["d_fin", "d_xn1"], writes=["d_fin"], sig=SV)
            em("sync",
               (lambda e, tt=tt: e.dma_start(out=out_hbm[tt * 128:(tt + 1) * 128, :],
                                             in_=ctx["d_fin"][:])),
               reads=["d_fin"], sig="dh", dma=True)

    final_waits = [(s, p.cnt[s]) for s in ("dh", "dg", "ds") if s in p.cnt]

    sems = {}
    for name in sorted(p.cnt):
        sems[name] = nc.semaphore(name).__enter__()

    with nc.Block() as block:
        def run_engine(eng_obj, name):
            emitted = {}
            for (fn, waits, sig) in p.ops[name]:
                for (sname, val) in waits:
                    if val > emitted.get(sname, -1):
                        eng_obj.wait_ge(sems[sname], val)
                        emitted[sname] = val
                ins = fn(eng_obj)
                if sig is not None:
                    sname, amt = sig
                    ins.then_inc(sems[sname], amt)
            if name == "sync":
                for (sname, val) in final_waits:
                    eng_obj.wait_ge(sems[sname], val)

        @block.sync
        def _(e):
            run_engine(e, "sync")

        @block.scalar
        def _(e):
            run_engine(e, "scalar")

        @block.tensor
        def _(e):
            run_engine(e, "tensor")

        @block.vector
        def _(e):
            run_engine(e, "vector")

        @block.gpsimd
        def _(e):
            run_engine(e, "gpsimd")

    lower_extended_insts(nc)
    return nc


_NC_CACHE = [None]


def kernel(L0_attn_x, L0_attn_W_in, L0_attn_b_in, L0_attn_W_gx, L0_attn_b_gx,
           L0_attn_W_ga, L0_attn_b_ga, L0_attn_a_real, L0_attn_a_imag,
           L0_attn_W_out, L0_attn_b_out, L0_ffn_gate_w, L0_ffn_up_w, L0_ffn_down_w):
    x = np.ascontiguousarray(np.asarray(L0_attn_x, dtype=np.float32))
    W_in = np.asarray(L0_attn_W_in, dtype=np.float32)
    W_out = np.asarray(L0_attn_W_out, dtype=np.float32)
    W_gx = np.asarray(L0_attn_W_gx, dtype=np.float32)
    W_ga = np.asarray(L0_attn_W_ga, dtype=np.float32)
    b_gx = np.asarray(L0_attn_b_gx, dtype=np.float32)
    b_ga = np.asarray(L0_attn_b_ga, dtype=np.float32)
    b_in = np.asarray(L0_attn_b_in, dtype=np.float32)
    b_out = np.asarray(L0_attn_b_out, dtype=np.float32)
    a_real = np.asarray(L0_attn_a_real, dtype=np.float32)
    a_imag = np.asarray(L0_attn_a_imag, dtype=np.float32)
    gate_w = np.ascontiguousarray(np.asarray(L0_ffn_gate_w, dtype=np.float32))
    up_w = np.ascontiguousarray(np.asarray(L0_ffn_up_w, dtype=np.float32))
    down_w = np.ascontiguousarray(np.asarray(L0_ffn_down_w, dtype=np.float32))

    w_in_l = np.ascontiguousarray(
        W_in.reshape(16, 128, 16, 128).transpose(1, 0, 2, 3).reshape(128, 16 * 16 * 128))
    w_out_l = np.ascontiguousarray(
        W_out.reshape(16, 128, 2048).transpose(1, 0, 2).reshape(128, 16 * 2048))
    wg = np.stack([W_gx, W_ga], 0).reshape(2, 8, 2, 128, 128)
    w_g_l = np.ascontiguousarray(wg.transpose(3, 0, 1, 2, 4).reshape(128, 2 * 8 * 2 * 128))
    b_g_l = np.ascontiguousarray(np.concatenate([b_gx.reshape(-1), b_ga.reshape(-1)], 0)
                                 .reshape(16, 128).T)
    ar_l = np.ascontiguousarray(a_real.reshape(8, 128).T)
    aim_l = np.ascontiguousarray(a_imag.reshape(8, 128).T)
    b_in_l = np.ascontiguousarray(b_in.reshape(1, DH))
    b_out_l = np.ascontiguousarray(b_out.reshape(1, D))
    ident_f = np.eye(128, dtype=np.float32)
    ident_b = np.eye(128).astype(ml_dtypes.bfloat16)
    ones_col_b = np.ones((128, 1), dtype=ml_dtypes.bfloat16)
    ones_row_f = np.ones((1, 128), dtype=np.float32)
    ones_row_b = np.ones((1, 128), dtype=ml_dtypes.bfloat16)
    lit_arr = np.tile(np.array([0.0, 1.0, 2.0, 0.5, EPS, PI / 2, -PI, 1e-24],
                               dtype=np.float32), (128, 1))

    if _NC_CACHE[0] is None:
        _NC_CACHE[0] = build_program()
    nc = _NC_CACHE[0]

    in_maps = []
    for c in range(NCORES):
        xc = np.ascontiguousarray(x[c])
        in_maps.append({
            "x_tr": np.ascontiguousarray(xc.T),
            "x_nat": xc,
            "w_in": w_in_l,
            "w_out": w_out_l,
            "w_g": w_g_l,
            "b_g": b_g_l,
            "b_in_row": b_in_l,
            "b_out_row": b_out_l,
            "ar_col": ar_l,
            "aim_col": aim_l,
            "gate_w_nat": gate_w,
            "up_w_nat": up_w,
            "down_w_nat": down_w,
            "ident_f": ident_f,
            "ident_b": ident_b,
            "ones_col_b": ones_col_b,
            "ones_row_f": ones_row_f,
            "ones_row_b": ones_row_b,
            "lit_in": lit_arr,
        })
    res = run_bass_kernel_spmd(nc, in_maps, core_ids=list(range(NCORES)))
    out = np.stack([np.asarray(r["out"]) for r in res.results], 0)
    return out.astype(np.float32)


# version 07
# speedup vs baseline: 1.1968x; 1.1968x over previous
import numpy as np
import ml_dtypes
import concourse.bass as bass
import concourse.mybir as mybir
from concourse.bass_utils import run_bass_kernel_spmd
from concourse import dve_ops
from concourse.dve_op import DveOp
from concourse.dve_spec import AluOp as DAlu, Bin, C0, C1, C2, Spec, Src0, Src1, maxx
from concourse.library_overlay import lower_extended_insts

F32 = mybir.dt.float32
BF16 = mybir.dt.bfloat16
ALU = mybir.AluOpType
AXX = mybir.AxisListType.X
ACT = mybir.ActivationFunctionType

_FP8_X = Src0 * Src1
_FP8_NX = Bin(DAlu.BITWISE_NOT, _FP8_X, _FP8_X)
_FP8_P = Bin(DAlu.BITWISE_NOT, Bin(DAlu.BITWISE_OR, _FP8_NX, C0), _FP8_NX)
_FP8_CM = maxx(_FP8_P, C1) * C2
FP8_NOTMASK = float(np.uint32(0x807FFFFF).view(np.float32))
FP8_C1 = float(2.0 ** -6)
FP8_C2 = float(1.5 * 2.0 ** 20)


def _fp8_rne_ref(in0, in1, s0, s1, imm2):
    f = np.float64 if in0.dtype == np.float64 else np.float32
    x = in0.astype(f) * in1.astype(f)
    if f is np.float64:
        ax = np.abs(x)
        _, e = np.frexp(ax)
        pc = np.maximum(np.ldexp(np.float64(1.0), e - 1), 2.0 ** -6)
        ulp = pc * (2.0 ** -3)
        return np.round(x / ulp) * ulp
    x = x.astype(np.float32)
    p = (x.view(np.int32) & np.int32(0x7F800000)).view(np.float32)
    pc = np.maximum(p, np.float32(2.0 ** -6))
    cm = (pc * np.float32(1.5 * 2.0 ** 20)).astype(np.float32)
    t = (x + cm).astype(np.float32)
    return (t - cm).astype(np.float32)


FP8_RNE_MUL = DveOp(
    "FP8_RNE_MUL",
    Spec(body=(_FP8_X + _FP8_CM) - _FP8_CM),
    subdim=False,
    uops_sha={"v3": "23f3018ff8a04129"},
    reference=_fp8_rne_ref,
)

if not any(o.name == "FP8_RNE_MUL" for o in dve_ops.OPS):
    dve_ops.OPS.append(FP8_RNE_MUL)
    dve_ops.CUSTOM_DVE_SPECS[FP8_RNE_MUL.name] = FP8_RNE_MUL.spec
    dve_ops.CUSTOM_DVE_REFS[FP8_RNE_MUL.name] = FP8_RNE_MUL.reference
    dve_ops._SUB_OPCODE_FOR_NAME[FP8_RNE_MUL.name] = 31

RCF = dve_ops.RECIPROCAL_APPROX_FAST
RCF_C = dve_ops.RECIP_APPROX_FAST_CONSTS

_WR_R = (Src0 * C0 + C1) - C1
WRAP_RNE = DveOp(
    "WRAP_RNE",
    Spec(body=Src0 - _WR_R * C2),
    subdim=False,
    uops_sha={"v3": "384a601106e5b300"},
    reference=lambda in0, in1, s0, s1, imm2: (
        (lambda f, x: (x - np.round(x * np.float64(s0)) * np.float64(imm2))
         if f is np.float64 else
         (x - (((x * np.float32(s0)).astype(np.float32) + np.float32(s1)).astype(np.float32)
               - np.float32(s1)).astype(np.float32) * np.float32(imm2)).astype(np.float32)
         )(np.float64 if in0.dtype == np.float64 else np.float32, in0)
    ),
)
if not any(o.name == "WRAP_RNE" for o in dve_ops.OPS):
    dve_ops.OPS.append(WRAP_RNE)
    dve_ops.CUSTOM_DVE_SPECS[WRAP_RNE.name] = WRAP_RNE.spec
    dve_ops.CUSTOM_DVE_REFS[WRAP_RNE.name] = WRAP_RNE.reference
    dve_ops._SUB_OPCODE_FOR_NAME[WRAP_RNE.name] = 30
WRAP_MAGIC = float(1.5 * 2.0 ** 23)
PI_CLAMP = float(np.float32(3.14159251))

NCORES = 8
T = 2048
D = 2048
DH = 2048
I_FFN = 8192
SEG = 256
NSEG = T // SEG
SUB = 64
NSUB = SEG // SUB
NTT = T // 128
EPS = 1e-5
PI = float(np.pi)
INV2PI = float(1.0 / (2.0 * np.pi))
TWOPI = float(2.0 * np.pi)
NHCH = I_FFN // 512
QT = 512
NQT = T // QT


FIFO_SCHED = False


class Buf:
    __slots__ = ("wtok", "rtoks", "lw", "ridx")

    def __init__(self):
        self.wtok = None
        self.rtoks = []
        self.lw = None
        self.ridx = []


class Plan:

    ENGINES = ("sync", "scalar", "tensor", "vector", "gpsimd")
    DEF_COST = {"sync": 1500, "scalar": 1100, "tensor": 280, "vector": 1400,
                "gpsimd": 2300}

    def __init__(self):
        self.ops = []
        self.cnt = {}
        self.tokprod = {}
        self.bufs = {}
        self.epoch = 0
        self.last_op = {e: None for e in self.ENGINES}

    def barrier(self):
        self.epoch += 1

    def buf(self, name):
        if name not in self.bufs:
            self.bufs[name] = Buf()
        return self.bufs[name]

    def emit(self, eng, fn, reads=(), writes=(), waits=(), sig=None, dma=False,
             cost=None):
        wl = list(waits)
        idx = len(self.ops)
        deps = set()
        for n in reads:
            b = self.buf(n)
            if b.wtok is not None:
                wl.append(b.wtok)
            if b.lw is not None and self.ops[b.lw]["engine"] == eng:
                deps.add(b.lw)
        for n in writes:
            b = self.buf(n)
            if b.wtok is not None:
                wl.append(b.wtok)
            wl.extend(b.rtoks)
            if b.lw is not None and self.ops[b.lw]["engine"] == eng:
                deps.add(b.lw)
            for r in b.ridx:
                if self.ops[r]["engine"] == eng:
                    deps.add(r)
        for (s, v) in wl:
            p = self.tokprod.get((s, v))
            if p is not None:
                deps.add(p)
        lo = self.last_op[eng]
        if lo is not None and (not lo[1] or sig is None):
            deps.add(lo[0])
        tok = None
        if sig is not None:
            self.cnt[sig] = self.cnt.get(sig, 0) + (16 if dma else 1)
            tok = (sig, self.cnt[sig])
            self.tokprod[tok] = idx
        self.last_op[eng] = (idx, sig is not None)
        self.ops.append({"engine": eng, "fn": fn, "deps": deps, "sig": sig,
                         "dma": dma, "epoch": self.epoch,
                         "cost": cost if cost is not None else self.DEF_COST[eng]})
        if tok is not None:
            for n in writes:
                b = self.buf(n)
                b.wtok = tok
                b.rtoks = []
            for n in reads:
                self.buf(n).rtoks.append(tok)
        for n in writes:
            b = self.buf(n)
            b.lw = idx
            b.ridx = []
        for n in reads:
            self.buf(n).ridx.append(idx)
        return tok

    def schedule(self):
        import heapq
        n = len(self.ops)
        children = [[] for _ in range(n)]
        indeg = [0] * n
        for i, op in enumerate(self.ops):
            for d in op["deps"]:
                children[d].append(i)
                indeg[i] += 1
        level = [0.0] * n
        for i in range(n - 1, -1, -1):
            c = self.ops[i]["cost"]
            m = 0.0
            for ch in children[i]:
                if level[ch] > m:
                    m = level[ch]
            level[i] = c + m
        etime = {e: 0.0 for e in self.ENGINES}
        done = [0.0] * n
        order = {e: [] for e in self.ENGINES}
        if FIFO_SCHED:
            for i, op in enumerate(self.ops):
                order[op["engine"]].append(i)
            return self._materialize(order)
        n_epochs = self.epoch + 1
        by_epoch = [[] for _ in range(n_epochs)]
        for i, op in enumerate(self.ops):
            by_epoch[op["epoch"]].append(i)
        for ep in range(n_epochs):
            ready = {e: [] for e in self.ENGINES}
            for i in by_epoch[ep]:
                if indeg[i] == 0:
                    heapq.heappush(ready[self.ops[i]["engine"]], (-level[i], i))
            remaining = len(by_epoch[ep])
            ep_floor = max(etime.values()) if ep > 0 else 0.0
            for e in self.ENGINES:
                etime[e] = max(etime[e], ep_floor)
            while remaining:
                cands = [e for e in self.ENGINES if ready[e]]
                e = min(cands, key=lambda x: etime[x])
                _, i = heapq.heappop(ready[e])
                op = self.ops[i]
                start = etime[e]
                for d in op["deps"]:
                    if done[d] > start:
                        start = done[d]
                fin = start + op["cost"]
                etime[e] = fin
                done[i] = fin
                order[e].append(i)
                remaining -= 1
                for ch in children[i]:
                    indeg[ch] -= 1
                    if indeg[ch] == 0 and self.ops[ch]["epoch"] == ep:
                        heapq.heappush(ready[self.ops[ch]["engine"]], (-level[ch], ch))
        return self._materialize(order)

    def _materialize(self, order):
        n = len(self.ops)
        n_epochs = self.epoch + 1
        NLANE = 8
        cnt = {}
        toks = [None] * n
        prewait = [None] * n
        lane_idx = {}
        for e in self.ENGINES:
            for i in order[e]:
                op = self.ops[i]
                if op["sig"] is None:
                    continue
                if op["dma"]:
                    q = op["sig"]
                    k = lane_idx.get(q, 0)
                    lane_idx[q] = k + 1
                    lane = f"{q}{k % NLANE}"
                    prev = cnt.get(lane, 0)
                    if k >= NLANE:
                        prewait[i] = (lane, prev)
                    cnt[lane] = prev + 16
                    toks[i] = (lane, cnt[lane])
                    op["lane_sig"] = (lane, 16)
                else:
                    cnt[op["sig"]] = cnt.get(op["sig"], 0) + 1
                    toks[i] = (op["sig"], cnt[op["sig"]])
                    op["lane_sig"] = (op["sig"], 1)
        ecnt = [dict() for _ in range(n_epochs + 1)]
        run = {}
        for ep in range(n_epochs):
            for e in self.ENGINES:
                for i in order[e]:
                    if self.ops[i]["epoch"] == ep and toks[i] is not None:
                        s, v = toks[i]
                        if v > run.get(s, -1):
                            run[s] = v
            ecnt[ep + 1] = dict(run)
        sched = {e: [] for e in self.ENGINES}
        for e in self.ENGINES:
            for i in order[e]:
                op = self.ops[i]
                wd = dict(ecnt[op["epoch"]])
                if prewait[i] is not None and prewait[i][1] > wd.get(prewait[i][0], -1):
                    wd[prewait[i][0]] = prewait[i][1]
                for d in op["deps"]:
                    t = toks[d]
                    if t is not None and t[1] > wd.get(t[0], -1):
                        wd[t[0]] = t[1]
                sg = op.get("lane_sig") if op["sig"] else None
                sched[e].append((op["fn"], sorted(wd.items()), sg))
        self.final_cnt = cnt
        self._dbg_order = order
        return sched


def build_program():
    nc = bass.Bass("TRN2", target_bir_lowering=False)
    dt = nc.dram_tensor

    x_tr = dt("x_tr", [D, T], F32, kind="ExternalInput").ap()
    x_nat = dt("x_nat", [T, D], F32, kind="ExternalInput").ap()
    w_in = dt("w_in", [128, 16 * 16 * 128], F32, kind="ExternalInput").ap()
    w_out = dt("w_out", [128, 16 * 2048], F32, kind="ExternalInput").ap()
    w_g = dt("w_g", [128, 2 * 8 * 2 * 128], F32, kind="ExternalInput").ap()
    b_g = dt("b_g", [128, 16], F32, kind="ExternalInput").ap()
    b_in_row = dt("b_in_row", [1, DH], F32, kind="ExternalInput").ap()
    b_out_row = dt("b_out_row", [1, D], F32, kind="ExternalInput").ap()
    ar_col = dt("ar_col", [128, 8], F32, kind="ExternalInput").ap()
    aim_col = dt("aim_col", [128, 8], F32, kind="ExternalInput").ap()
    gate_w_nat = dt("gate_w_nat", [I_FFN, D], F32, kind="ExternalInput").ap()
    up_w_nat = dt("up_w_nat", [I_FFN, D], F32, kind="ExternalInput").ap()
    down_w_nat = dt("down_w_nat", [D, I_FFN], F32, kind="ExternalInput").ap()
    ident_f = dt("ident_f", [128, 128], F32, kind="ExternalInput").ap()
    ident_b = dt("ident_b", [128, 128], BF16, kind="ExternalInput").ap()
    ones_col_b = dt("ones_col_b", [128, 1], BF16, kind="ExternalInput").ap()
    ones_row_f = dt("ones_row_f", [1, 128], F32, kind="ExternalInput").ap()
    ones_row_b = dt("ones_row_b", [1, 128], BF16, kind="ExternalInput").ap()
    lit_in = dt("lit_in", [128, 8], F32, kind="ExternalInput").ap()

    out_hbm = dt("out", [T, D], F32, kind="ExternalOutput").ap()
    hm_sp = dt("hm_sp", [DH, T], BF16, kind="Internal").ap()
    x1_sp = dt("x1_sp", [T, D], F32, kind="Internal").ap()
    xq_sp = dt("xq_sp", [T, D], BF16, kind="Internal").ap()
    hq_sp = dt("hq_sp", [I_FFN, T], BF16, kind="Internal").ap()
    dwq_sp = dt("dwq_sp", [I_FFN, D], BF16, kind="Internal").ap()
    gwq_sp = dt("gwq_sp", [D, I_FFN], BF16, kind="Internal").ap()
    uwq_sp = dt("uwq_sp", [D, I_FFN], BF16, kind="Internal").ap()

    ctx = {}
    handles = {}
    alloc_seq = []

    def sb(name, shape, dtype):
        h = nc.sbuf_tensor(name, shape, dtype)
        t = h.__enter__()
        handles[name] = h
        ctx[name] = t
        alloc_seq.append(name)
        return t

    def free_group(names):
        ns = set(names)
        for n in reversed(alloc_seq):
            if n in ns:
                handles[n].__exit__(None, None, None)
        alloc_seq[:] = [n for n in alloc_seq if n not in ns]

    psum_handles = {}

    def psum(name, shape=(128, 512), dtype=F32):
        h = nc.psum_tensor(list(shape), dtype)
        t = h.__enter__()
        ctx[name] = t
        psum_handles[name] = h
        return t

    def psum_free(names):
        for n in names:
            psum_handles[n].__exit__(None, None, None)

    sb("c_lit", [128, 8], F32)
    sb("c_idb", [128, 128], BF16)
    sb("c_bg", [128, 16], F32)
    sb("c_ar", [128, 8], F32)
    sb("c_c1", [128, 8], F32)
    sb("c_aim", [128, 8], F32)
    sb("c_idf", [128, 128], F32)
    sb("c_ocb", [128, 1], BF16)
    sb("c_orf", [1, 128], F32)
    sb("c_binrow", [1, DH], BF16)
    sb("c_colsum", [1, DH], BF16)
    sb("c_tmp8", [128, 8], F32)
    A_GROUP = (["a_win", "a_wg", "a_xtb", "a_xh0", "a_xh1", "a_xhb", "a_gx", "a_ga",
                "a_hm0", "a_hm1", "a_rows", "a_rows_b", "a_rrep", "a_stage", "a_sm",
                "a_carry", "qa_nat0", "qa_q", "qa_dq", "qa_am",
                "qa_s", "qa_is", "qa_tst0"]
               + [f"a_b{i}_{pp}" for i in range(1, 10) for pp in range(2)])
    sb("a_win", [128, 16 * 16 * 128], BF16)
    sb("a_wg", [128, 2 * 8 * 2 * 128], BF16)
    sb("a_xtb", [128, 16 * SEG], BF16)
    sb("a_xh0", [128, 16 * SEG], F32)
    sb("a_xh1", [128, 16 * SEG], F32)
    sb("a_xhb", [128, 16 * SEG], BF16)
    sb("a_gx", [128, 8 * SEG], F32)
    sb("a_ga", [128, 8 * SEG], F32)
    sb("qa_nat0", [128, 2048], F32)
    sb("qa_q", [128, 2048], BF16)
    sb("qa_dq", [128, 2048], BF16)
    sb("qa_am", [128, 64], F32)
    sb("qa_s", [128, 64], F32)
    sb("qa_is", [128, 64], F32)
    sb("qa_tst0", [128, 1024], BF16)
    for i in range(1, 10):
        for pp in range(2):
            sb(f"a_b{i}_{pp}", [128, 8 * SUB], F32)
    sb("a_hm0", [128, 16 * SUB], BF16)
    sb("a_hm1", [128, 16 * SUB], BF16)
    sb("a_rows", [1, SEG], F32)
    sb("a_rows_b", [1, 2 * SEG], BF16)
    sb("a_rrep", [128, SEG], F32)
    sb("a_stage", [128, 4], F32)
    sb("a_sm", [128, 16], F32)
    sb("a_carry", [128, 24], F32)
    P = [psum(f"P{i}") for i in range(6)]
    PN = [f"P{i}" for i in range(6)]
    PB6 = psum("PB6", (128, 1024), BF16)
    PB7 = psum("PB7", (128, 1024), BF16)

    p = Plan()

    def em(eng, fn, reads=(), writes=(), waits=(), sig=None, dma=False):
        return p.emit(eng, fn, reads, writes, waits, sig, dma)

    SV, SS, ST, SG = "v", "s", "t", "g"

    LITS = [0.0, 1.0, 2.0, 0.5, EPS, PI / 2, -PI, 1e-24]
    for i, v in enumerate(LITS):
        nc.const_aps.register(F32, v, ctx["c_lit"][:, i:i + 1])
    em("sync", lambda e: e.dma_start(out=ctx["c_lit"][:], in_=lit_in),
       writes=["c_lit"], sig="dh", dma=True)
    p.barrier()

    for (dst, srcap) in [("c_bg", b_g), ("c_ar", ar_col), ("c_aim", aim_col),
                         ("c_idf", ident_f), ("c_idb", ident_b), ("c_ocb", ones_col_b),
                         ("c_orf", ones_row_f)]:
        em("sync", (lambda e, d=dst, s=srcap: e.dma_start(out=ctx[d][:], in_=s)),
           writes=[dst], sig="dh", dma=True)
    em("gpsimd", lambda e: e.dma_start(out=ctx["c_binrow"][:], in_=b_in_row),
       writes=["c_binrow"], sig="dg", dma=True)
    em("gpsimd", lambda e: e.dma_start(out=ctx["a_win"][:], in_=w_in),
       writes=["a_win"], sig="dg", dma=True)
    em("gpsimd", lambda e: e.dma_start(out=ctx["a_wg"][:], in_=w_g),
       writes=["a_wg"], sig="dg", dma=True)
    em("vector", lambda e: e.memset(ctx["a_carry"][:], 0.0), writes=["a_carry"], sig=SV)
    em("scalar", lambda e: e.activation(ctx["c_tmp8"][:], ctx["c_ar"][:], ACT.Exp),
       reads=["c_ar"], writes=["c_tmp8"], sig=SS)
    em("scalar", lambda e: e.activation(ctx["c_tmp8"][:], ctx["c_tmp8"][:] if False else ctx["c_tmp8"][:], ACT.Ln, bias=1.0) if False else None,
       sig=None) if False else None
    em("scalar", lambda e: e.activation(ctx["c_c1"][:], ctx["c_tmp8"][:], ACT.Ln, bias=1.0),
       reads=["c_tmp8"], writes=["c_c1"], sig=SS)
    em("vector", lambda e: e.tensor_scalar(ctx["c_c1"][:], ctx["c_c1"][:], -8.0, None, ALU.mult),
       reads=["c_c1"], writes=["c_c1"], sig=SV)

    for cc in range(4):
        for c4 in range(4):
            cht = cc * 4 + c4
            for kt in range(16):
                r0 = ctx["a_win"][:, (kt * 16 + cht) * 128:(kt * 16 + cht + 1) * 128]
                em("tensor",
                   (lambda e, o=P[5].ap()[0:1, c4 * 128:(c4 + 1) * 128], l=ctx["c_ocb"][:],
                    r=r0, st=(kt == 0), sp=(kt == 15): e.matmul(o, l, r, start=st, stop=sp)),
                   reads=["a_win", "c_ocb"], writes=["P5"],
                   sig=ST if (kt == 15 and c4 == 3) else None)
        em("vector",
           (lambda e, cc=cc: e.tensor_copy(ctx["c_colsum"][0:1, cc * 512:(cc + 1) * 512],
                                           P[5].ap()[0:1, 0:512])),
           reads=["P5"], writes=["c_colsum"], sig=SV)

    def quant_unit(kind, i, par):
        nat = "qa_nat0"
        if kind == 0:
            src_ap = gate_w_nat[i * 128:(i + 1) * 128, :]
            dst = gwq_sp
            roff, coff = 0, i * 128
        elif kind == 1:
            src_ap = up_w_nat[i * 128:(i + 1) * 128, :]
            dst = uwq_sp
            roff, coff = 0, i * 128
        else:
            n_i, q_i = i % 16, i // 16
            src_ap = down_w_nat[n_i * 128:(n_i + 1) * 128, q_i * 2048:(q_i + 1) * 2048]
            dst = dwq_sp
            roff, coff = q_i * 16, n_i * 128
        dstn = ["gwq_sp", "uwq_sp", "dwq_sp"][kind]
        em("gpsimd", (lambda e, s=src_ap, nat=nat: e.dma_start(out=ctx[nat][:], in_=s)),
           writes=[nat], sig="dg", dma=True)
        em("vector",
           (lambda e, nat=nat: e.tensor_reduce(
               ctx["qa_am"][:], ctx[nat][:].rearrange("p (nb k) -> p nb k", k=32),
               AXX, ALU.max, apply_absolute_value=True)),
           reads=[nat], writes=["qa_am"], sig=SV)
        em("vector",
           (lambda e: e.tensor_scalar(ctx["qa_s"][:], ctx["qa_am"][:], 1.0 / 448.0,
                                      1e-30, ALU.mult, ALU.max)),
           reads=["qa_am"], writes=["qa_s"], sig=SV)
        em("vector", (lambda e: e.reciprocal(ctx["qa_is"][:], ctx["qa_s"][:])),
           reads=["qa_s"], writes=["qa_is"], sig=SV)
        em("vector",
           (lambda e, nat=nat: e._custom_dve(
               FP8_RNE_MUL,
               out=ctx["qa_q"][:].rearrange("p (nb k) -> p nb k", k=32),
               in0=ctx["qa_is"][:].unsqueeze(2).broadcast_to([128, 64, 32]),
               in1=ctx[nat][:].rearrange("p (nb k) -> p nb k", k=32),
               s0=FP8_NOTMASK, s1=FP8_C1, imm2=FP8_C2)),
           reads=["qa_is", nat], writes=["qa_q"], sig=SV)
        em("gpsimd",
           (lambda e: e.tensor_tensor(
               ctx["qa_dq"][:].rearrange("p (nb k) -> p nb k", k=32),
               ctx["qa_q"][:].rearrange("p (nb k) -> p nb k", k=32),
               ctx["qa_s"][:].unsqueeze(2).broadcast_to([128, 64, 32]), ALU.mult)),
           reads=["qa_q", "qa_s"], writes=["qa_dq"], sig=SG)
        for batch in range(2):
            tst = "qa_tst0"
            pbk = PB6 if batch == 0 else PB7
            pbn = "PB6" if batch == 0 else "PB7"
            for k8 in range(8):
                kt = batch * 8 + k8
                em("tensor",
                   (lambda e, o=pbk.ap()[:, k8 * 128:(k8 + 1) * 128],
                    i_=ctx["qa_dq"][:, kt * 128:(kt + 1) * 128]:
                    e.transpose(o, i_, ctx["c_idb"][:])),
                   reads=["qa_dq", "c_idb"], writes=[pbn], sig=ST if k8 == 7 else None)
            em("scalar",
               (lambda e, tst=tst, pbk=pbk: e.activation(ctx[tst][:], pbk.ap()[:, 0:1024], ACT.Copy)),
               reads=[pbn], writes=[tst], sig=SS)
            em("sync",
               (lambda e, dst=dst, roff=roff, coff=coff, batch=batch, tst=tst: e.dma_start(
                   out=dst.rearrange("(r p) n -> p r n", p=128)[
                       :, roff + batch * 8:roff + (batch + 1) * 8, coff:coff + 128],
                   in_=ctx[tst][:].rearrange("p (r n) -> p r n", r=8))),
               reads=[tst], writes=[dstn], sig="dh", dma=True)

    QUANT_UNITS = ([(0, i) for i in range(64)] + [(1, i) for i in range(64)]
                   + [(2, i) for i in range(64)])
    q_cursor = [0]

    def emit_quant_batch(n):
        for _ in range(n):
            if q_cursor[0] >= len(QUANT_UNITS):
                return
            kind, i = QUANT_UNITS[q_cursor[0]]
            quant_unit(kind, i, q_cursor[0] % 2)
            q_cursor[0] += 1

    for seg in range(NSEG):
        emit_quant_batch(24)
        t0 = seg * SEG
        xhn = f"a_xh{seg % 2}"
        em("gpsimd",
           (lambda e, t0=t0: e.dma_start(
               out=ctx["a_xtb"][:].rearrange("p (kt t) -> p kt t", kt=16),
               in_=x_tr.rearrange("(kt p) t -> p kt t", p=128)[:, :, t0:t0 + SEG])),
           writes=["a_xtb"], sig="dg", dma=True)

        for c in range(SEG // 128):
            for kt in range(16):
                xt = ctx["a_xtb"][:, kt * SEG + c * 128: kt * SEG + (c + 1) * 128]
                em("tensor",
                   (lambda e, o=P[4].ap()[:, 0:128], l=xt, r=xt, st=(kt == 0), sp=(kt == 15):
                    e.matmul(o, l, r, start=st, stop=sp)),
                   reads=["a_xtb"], writes=["P4"], sig=None)
            for kt in range(16):
                xt = ctx["a_xtb"][:, kt * SEG + c * 128: kt * SEG + (c + 1) * 128]
                em("tensor",
                   (lambda e, o=P[4].ap()[:, 128:129], l=xt, r=ctx["c_ocb"][:],
                    st=(kt == 0), sp=(kt == 15): e.matmul(o, l, r, start=st, stop=sp)),
                   reads=["a_xtb", "c_ocb"], writes=["P4"], sig=ST if kt == 15 else None)
            em("vector",
               (lambda e: e.tensor_tensor_reduce(ctx["a_b1_0"][:, 0:128], P[4].ap()[:, 0:128],
                                                 ctx["c_idf"][:], 1.0, 0.0, ALU.mult,
                                                 ALU.add, ctx["a_sm"][:, 0:1])),
               reads=["P4", "c_idf"], writes=["a_b1_0", "a_sm"], sig=SV)
            em("vector",
               (lambda e: e.tensor_scalar(ctx["a_sm"][:, 1:2], P[4].ap()[:, 128:129],
                                          1.0 / D, None, ALU.mult)),
               reads=["P4"], writes=["a_sm"], sig=SV)
            em("vector",
               (lambda e: e.tensor_scalar(ctx["a_sm"][:, 2:3], ctx["a_sm"][:, 0:1],
                                          1.0 / D, None, ALU.mult)),
               reads=["a_sm"], writes=["a_sm"], sig=SV)
            em("vector",
               (lambda e: e.scalar_tensor_tensor(ctx["a_sm"][:, 3:4], ctx["a_sm"][:, 1:2],
                                                 -1.0, ctx["a_sm"][:, 1:2], ALU.mult, ALU.mult)),
               reads=["a_sm"], writes=["a_sm"], sig=SV)
            em("vector",
               (lambda e: e.tensor_tensor(ctx["a_sm"][:, 4:5], ctx["a_sm"][:, 2:3],
                                          ctx["a_sm"][:, 3:4], ALU.add)),
               reads=["a_sm"], writes=["a_sm"], sig=SV)
            em("scalar",
               (lambda e: e.activation(ctx["a_stage"][:, 1:2], ctx["a_sm"][:, 4:5],
                                       ACT.Sqrt, bias=EPS)),
               reads=["a_sm"], writes=["a_stage"], sig=SS)
            em("vector",
               (lambda e: e.reciprocal(ctx["a_stage"][:, 2:3], ctx["a_stage"][:, 1:2])),
               reads=["a_stage"], writes=["a_stage"], sig=SV)
            em("vector",
               (lambda e: e.scalar_tensor_tensor(ctx["a_stage"][:, 0:1], ctx["a_sm"][:, 1:2],
                                                 -1.0, ctx["a_stage"][:, 2:3], ALU.mult, ALU.mult)),
               reads=["a_sm", "a_stage"], writes=["a_stage"], sig=SV)
            for k in range(3):
                em("tensor",
                   (lambda e, k=k: e.transpose(P[5].ap()[0:1, k * 128:(k + 1) * 128],
                                               ctx["a_stage"][:, k:k + 1], ctx["c_idf"][:])),
                   reads=["a_stage", "c_idf"], writes=["P5"], sig=ST if k == 2 else None)
            em("vector",
               (lambda e, c=c: e.tensor_copy(
                   ctx["a_rows_b"][0:1, :].rearrange("o (r s) -> o r s", r=2)[:, :, c * 128:(c + 1) * 128],
                   P[5].ap()[0:1, 0:256].rearrange("o (r s) -> o r s", r=2))),
               reads=["P5"], writes=["a_rows_b"], sig=SV)
            em("vector",
               (lambda e, c=c: e.tensor_copy(ctx["a_rows"][0:1, c * 128:(c + 1) * 128],
                                             P[5].ap()[0:1, 256:384])),
               reads=["P5"], writes=["a_rows"], sig=SV)
        em("tensor",
           (lambda e: e.matmul(P[5].ap()[:, 0:SEG], ctx["c_orf"][0:1, :],
                               ctx["a_rows"][0:1, 0:SEG], start=True, stop=True)),
           reads=["c_orf", "a_rows"], writes=["P5"], sig=ST)
        em("vector", (lambda e: e.tensor_copy(ctx["a_rrep"][:], P[5].ap()[:, 0:SEG])),
           reads=["P5"], writes=["a_rrep"], sig=SV)

        for cht in range(16):
            bk, bnm = P[cht % 2], PN[cht % 2]
            for kt in range(16):
                l = ctx["a_win"][:, (kt * 16 + cht) * 128:(kt * 16 + cht + 1) * 128]
                r = ctx["a_xtb"][:, kt * SEG:(kt + 1) * SEG]
                em("tensor",
                   (lambda e, o=bk.ap()[:, 0:SEG], l=l, r=r, st=(kt == 0):
                    e.matmul(o, l, r, start=st, stop=False)),
                   reads=(["a_win", "a_xtb"] if kt < 15 else ["a_win", "a_xtb"]),
                   writes=[bnm], sig=ST if kt == 15 else None)
            em("tensor",
               (lambda e, o=bk.ap()[:, 0:SEG], l=ctx["c_colsum"][0:1, cht * 128:(cht + 1) * 128],
                r=ctx["a_rows_b"][0:1, 0:SEG]: e.matmul(o, l, r, start=False, stop=False)),
               reads=["c_colsum", "a_rows_b"], writes=[bnm], sig=None)
            em("tensor",
               (lambda e, o=bk.ap()[:, 0:SEG], l=ctx["c_binrow"][0:1, cht * 128:(cht + 1) * 128],
                r=ctx["a_rows_b"][0:1, SEG:2 * SEG]: e.matmul(o, l, r, start=False, stop=True)),
               reads=["c_binrow", "a_rows_b"], writes=[bnm], sig=ST)
            em("vector",
               (lambda e, cht=cht, bk=bk, xhn=xhn: e.tensor_tensor(
                   ctx[xhn][:, cht * SEG:(cht + 1) * SEG], bk.ap()[:, 0:SEG],
                   ctx["a_rrep"][:], ALU.mult)),
               reads=[bnm, "a_rrep"], writes=[xhn], sig=SV)
            em("vector",
               (lambda e, cht=cht, xhn=xhn: e.tensor_copy(
                   ctx["a_xhb"][:, cht * SEG:(cht + 1) * SEG],
                   ctx[xhn][:, cht * SEG:(cht + 1) * SEG])),
               reads=[xhn], writes=["a_xhb"], sig=SV)

        for blk in range(8):
            for g in range(2):
                bk, bnm = P[2 + (blk * 2 + g) % 2], PN[2 + (blk * 2 + g) % 2]
                for kt2 in range(2):
                    l = ctx["a_wg"][:, ((g * 8 + blk) * 2 + kt2) * 128:
                                    ((g * 8 + blk) * 2 + kt2 + 1) * 128]
                    r = ctx["a_xhb"][:, (blk * 2 + kt2) * SEG:(blk * 2 + kt2 + 1) * SEG]
                    em("tensor",
                       (lambda e, o=bk.ap()[:, 0:SEG], l=l, r=r, st=(kt2 == 0), sp=(kt2 == 1):
                        e.matmul(o, l, r, start=st, stop=sp)),
                       reads=["a_wg", "a_xhb"], writes=[bnm], sig=ST if kt2 == 1 else None)
                dstn = "a_gx" if g == 0 else "a_ga"
                em("scalar",
                   (lambda e, bk=bk, dstn=dstn, blk=blk, g=g: e.activation(
                       ctx[dstn][:, blk * SEG:(blk + 1) * SEG], bk.ap()[:, 0:SEG],
                       ACT.Sigmoid, bias=ctx["c_bg"][:, g * 8 + blk:g * 8 + blk + 1])),
                   reads=[bnm, "c_bg"], writes=[dstn], sig=SS)

        c1b = ctx["c_c1"][:].unsqueeze(2).broadcast_to([128, 8, SUB])
        aimb = ctx["c_aim"][:].unsqueeze(2).broadcast_to([128, 8, SUB])
        for sub in range(NSUB):
            spp = (seg * NSUB + sub) % 2
            ts = slice(sub * SUB, (sub + 1) * SUB)
            gas = ctx["a_ga"][:].rearrange("p (g t) -> p g t", g=8)[:, :, ts]
            gxs = ctx["a_gx"][:].rearrange("p (g t) -> p g t", g=8)[:, :, ts]
            xh16 = ctx[xhn][:].rearrange("p (g t) -> p g t", g=16)
            xr = xh16[:, 0:8, ts]
            xi = xh16[:, 8:16, ts]
            hmn = f"a_hm{spp}"
            hm16 = ctx[hmn][:].rearrange("p (g t) -> p g t", g=16)

            def bb(i, spp=spp):
                return ctx[f"a_b{i}_{spp}"][:].rearrange("p (g t) -> p g t", g=8)

            def bf(i, spp=spp):
                return ctx[f"a_b{i}_{spp}"][:]

            def bn(i, spp=spp):
                return f"a_b{i}_{spp}"

            def V(o, fn, rd, wr):
                em("vector", fn, reads=rd, writes=wr, sig=SV)

            em("vector", (lambda e, o=bb(1), a=gas, b=c1b: e.tensor_tensor(o, a, b, ALU.mult)),
               reads=["a_ga", "c_c1"], writes=[bn(1)], sig=SV)
            em("vector", (lambda e, o=bb(2), a=gas, b=aimb: e.tensor_tensor(o, a, b, ALU.mult)),
               reads=["a_ga", "c_aim"], writes=[bn(2)], sig=SV)
            em("scalar", (lambda e, o=bf(3), i=bf(1): e.activation(o, i, ACT.Exp)),
               reads=[bn(1)], writes=[bn(3)], sig=SS)
            em("scalar", (lambda e, o=bf(4), i=bf(1): e.activation(o, i, ACT.Exp, scale=2.0)),
               reads=[bn(1)], writes=[bn(4)], sig=SS)
            em("scalar", (lambda e, o=bf(5), i=bf(2): e.activation(o, i, ACT.Sin, scale=2.0, bias=PI / 2)),
               reads=[bn(2)], writes=[bn(5)], sig=SS)
            em("scalar", (lambda e, o=bf(6), i=bf(2): e.activation(o, i, ACT.Sin, scale=2.0)),
               reads=[bn(2)], writes=[bn(6)], sig=SS)
            em("vector", (lambda e, o=bf(1), a=bf(4), b=bf(5): e.tensor_tensor(o, a, b, ALU.mult)),
               reads=[bn(4), bn(5)], writes=[bn(1)], sig=SV)
            em("vector", (lambda e, o=bf(5), a=bf(4), b=bf(6): e.tensor_tensor(o, a, b, ALU.mult)),
               reads=[bn(4), bn(6)], writes=[bn(5)], sig=SV)
            em("vector", (lambda e, o=bf(4), i=bf(1): e.tensor_scalar(o, i, -1.0, 1.0, ALU.mult, ALU.add)),
               reads=[bn(1)], writes=[bn(4)], sig=SV)
            em("scalar", (lambda e, o=bf(1), i=bf(4): e.activation(o, i, ACT.Square)),
               reads=[bn(4)], writes=[bn(1)], sig=SS)
            em("scalar", (lambda e, o=bf(6), i=bf(5): e.activation(o, i, ACT.Square)),
               reads=[bn(5)], writes=[bn(6)], sig=SS)
            em("vector", (lambda e, o=bf(6), a=bf(1), b=bf(6): e.tensor_tensor(o, a, b, ALU.add)),
               reads=[bn(1), bn(6)], writes=[bn(6)], sig=SV)
            em("scalar", (lambda e, o=bf(1), i=bf(6): e.activation(o, i, ACT.Sqrt)),
               reads=[bn(6)], writes=[bn(1)], sig=SS)
            em("vector", (lambda e, o=bf(6), a=bf(1), b=bf(4): e.tensor_tensor(o, a, b, ALU.add)),
               reads=[bn(1), bn(4)], writes=[bn(6)], sig=SV)
            em("scalar", (lambda e, o=bf(1), i=bf(6): e.activation(o, i, ACT.Sqrt, scale=0.5, bias=1e-24)),
               reads=[bn(6)], writes=[bn(1)], sig=SS)
            em("vector", (lambda e, o=bf(6), i=bf(1): e._custom_dve(
                RCF, out=o, in0=i, s0=RCF_C["s0"], s1=RCF_C["s1"], imm2=RCF_C["imm2"])),
               reads=[bn(1)], writes=[bn(6)], sig=SV)
            em("vector", (lambda e, o=bf(7), i=bf(5), b=bf(6): e.scalar_tensor_tensor(
                o, i, -0.5, b, ALU.mult, ALU.mult)),
               reads=[bn(5), bn(6)], writes=[bn(7)], sig=SV)
            for grp in range(8):
                em("vector",
                   (lambda e, grp=grp, spp=spp: e.tensor_tensor_scan(
                       ctx[f"a_b8_{spp}"][:, grp * SUB:(grp + 1) * SUB],
                       ctx[f"a_b2_{spp}"][:, grp * SUB:(grp + 1) * SUB],
                       ctx[f"a_b2_{spp}"][:, grp * SUB:(grp + 1) * SUB],
                       ctx["a_carry"][:, 16 + grp:17 + grp], ALU.add, ALU.bypass)),
                   reads=[bn(2), "a_carry"], writes=[bn(8)], sig=SV if grp == 7 else None)
            em("vector",
               (lambda e, spp=spp: e.tensor_copy(
                   ctx["a_carry"][:, 16:24],
                   ctx[f"a_b8_{spp}"][:].rearrange("p (g t) -> p g t", g=8)[:, :, SUB - 1:SUB].squeeze(2))),
               reads=[bn(8)], writes=["a_carry"], sig=SV)
            em("vector", (lambda e, o=bf(2), i=bf(8): e._custom_dve(
                WRAP_RNE, out=o, in0=i, s0=INV2PI, s1=WRAP_MAGIC, imm2=TWOPI)),
               reads=[bn(8)], writes=[bn(2)], sig=SV)
            em("vector", (lambda e, o=bf(4), i=bf(2): e.tensor_scalar(
                o, i, -PI_CLAMP, PI_CLAMP, ALU.max, ALU.min)),
               reads=[bn(2)], writes=[bn(4)], sig=SV)
            em("scalar", (lambda e, o=bf(2), i=bf(4): e.activation(o, i, ACT.Sin)),
               reads=[bn(4)], writes=[bn(2)], sig=SS)
            em("scalar", (lambda e, o=bf(5), i=bf(4): e.activation(o, i, ACT.Abs)),
               reads=[bn(4)], writes=[bn(5)], sig=SS)
            em("scalar", (lambda e, o=bf(4), i=bf(5): e.activation(o, i, ACT.Sin, scale=-1.0, bias=PI / 2)),
               reads=[bn(5)], writes=[bn(4)], sig=SS)
            em("vector", (lambda e, o=bf(5), a=bf(1), b=bf(4): e.tensor_tensor(o, a, b, ALU.mult)),
               reads=[bn(1), bn(4)], writes=[bn(5)], sig=SV)
            em("vector", (lambda e, o=bf(9), a=bf(7), b=bf(2): e.tensor_tensor(o, a, b, ALU.mult)),
               reads=[bn(7), bn(2)], writes=[bn(9)], sig=SV)
            em("vector", (lambda e, o=bf(9), a=bf(5), b=bf(9): e.tensor_tensor(o, a, b, ALU.add)),
               reads=[bn(5), bn(9)], writes=[bn(9)], sig=SV)
            em("vector", (lambda e, o=bb(5), a=gxs, b=bb(9): e.tensor_tensor(o, a, b, ALU.mult)),
               reads=["a_gx", bn(9)], writes=[bn(5)], sig=SV)
            em("vector", (lambda e, o=bf(9), a=bf(7), b=bf(4): e.tensor_tensor(o, a, b, ALU.mult)),
               reads=[bn(7), bn(4)], writes=[bn(9)], sig=SV)
            em("vector", (lambda e, o=bf(6), a=bf(1), b=bf(2): e.tensor_tensor(o, a, b, ALU.mult)),
               reads=[bn(1), bn(2)], writes=[bn(6)], sig=SV)
            em("vector", (lambda e, o=bf(9), a=bf(9), b=bf(6): e.tensor_tensor(o, a, b, ALU.subtract)),
               reads=[bn(9), bn(6)], writes=[bn(9)], sig=SV)
            em("vector", (lambda e, o=bb(6), a=gxs, b=bb(9): e.tensor_tensor(o, a, b, ALU.mult)),
               reads=["a_gx", bn(9)], writes=[bn(6)], sig=SV)
            em("gpsimd", (lambda e, o=bb(1), a=xr, b=bb(5): e.tensor_tensor(o, a, b, ALU.mult)),
               reads=[xhn, bn(5)], writes=[bn(1)], sig=SG)
            em("gpsimd", (lambda e, o=bb(9), a=xi, b=bb(6): e.tensor_tensor(o, a, b, ALU.mult)),
               reads=[xhn, bn(6)], writes=[bn(9)], sig=SG)
            em("gpsimd", (lambda e, o=bf(4), a=bf(1), b=bf(9): e.tensor_tensor(o, a, b, ALU.subtract)),
               reads=[bn(1), bn(9)], writes=[bn(4)], sig=SG)
            em("gpsimd", (lambda e, o=bb(1), a=xr, b=bb(6): e.tensor_tensor(o, a, b, ALU.mult)),
               reads=[xhn, bn(6)], writes=[bn(1)], sig=SG)
            em("gpsimd", (lambda e, o=bb(9), a=xi, b=bb(5): e.tensor_tensor(o, a, b, ALU.mult)),
               reads=[xhn, bn(5)], writes=[bn(9)], sig=SG)
            em("gpsimd", (lambda e, o=bf(2), a=bf(1), b=bf(9): e.tensor_tensor(o, a, b, ALU.add)),
               reads=[bn(1), bn(9)], writes=[bn(2)], sig=SG)
            for grp in range(8):
                em("vector",
                   (lambda e, grp=grp, spp=spp: e.tensor_tensor_scan(
                       ctx[f"a_b5_{spp}"][:, grp * SUB:(grp + 1) * SUB],
                       ctx[f"a_b3_{spp}"][:, grp * SUB:(grp + 1) * SUB],
                       ctx[f"a_b4_{spp}"][:, grp * SUB:(grp + 1) * SUB],
                       ctx["a_carry"][:, grp:grp + 1], ALU.mult, ALU.add)),
                   reads=[bn(3), bn(4), "a_carry"], writes=[bn(5)], sig=SV if grp == 7 else None)
            for grp in range(8):
                em("vector",
                   (lambda e, grp=grp, spp=spp: e.tensor_tensor_scan(
                       ctx[f"a_b6_{spp}"][:, grp * SUB:(grp + 1) * SUB],
                       ctx[f"a_b3_{spp}"][:, grp * SUB:(grp + 1) * SUB],
                       ctx[f"a_b2_{spp}"][:, grp * SUB:(grp + 1) * SUB],
                       ctx["a_carry"][:, 8 + grp:9 + grp], ALU.mult, ALU.add)),
                   reads=[bn(3), bn(2), "a_carry"], writes=[bn(6)], sig=SV if grp == 7 else None)
            em("vector",
               (lambda e, spp=spp: e.tensor_copy(
                   ctx["a_carry"][:, 0:8],
                   ctx[f"a_b5_{spp}"][:].rearrange("p (g t) -> p g t", g=8)[:, :, SUB - 1:SUB].squeeze(2))),
               reads=[bn(5)], writes=["a_carry"], sig=SV)
            em("vector",
               (lambda e, spp=spp: e.tensor_copy(
                   ctx["a_carry"][:, 8:16],
                   ctx[f"a_b6_{spp}"][:].rearrange("p (g t) -> p g t", g=8)[:, :, SUB - 1:SUB].squeeze(2))),
               reads=[bn(6)], writes=["a_carry"], sig=SV)
            em("vector", (lambda e, o=bf(1), i=bf(8): e._custom_dve(
                WRAP_RNE, out=o, in0=i, s0=INV2PI, s1=WRAP_MAGIC, imm2=TWOPI)),
               reads=[bn(8)], writes=[bn(1)], sig=SV)
            em("vector", (lambda e, o=bf(4), i=bf(1): e.tensor_scalar(
                o, i, -PI_CLAMP, PI_CLAMP, ALU.max, ALU.min)),
               reads=[bn(1)], writes=[bn(4)], sig=SV)
            em("scalar", (lambda e, o=bf(1), i=bf(4): e.activation(o, i, ACT.Sin)),
               reads=[bn(4)], writes=[bn(1)], sig=SS)
            em("scalar", (lambda e, o=bf(7), i=bf(4): e.activation(o, i, ACT.Abs)),
               reads=[bn(4)], writes=[bn(7)], sig=SS)
            em("scalar", (lambda e, o=bf(4), i=bf(7): e.activation(o, i, ACT.Sin, scale=-1.0, bias=PI / 2)),
               reads=[bn(7)], writes=[bn(4)], sig=SS)
            em("gpsimd", (lambda e, o=bf(7), a=bf(5), b=bf(4): e.tensor_tensor(o, a, b, ALU.mult)),
               reads=[bn(5), bn(4)], writes=[bn(7)], sig=SG)
            em("gpsimd", (lambda e, o=bf(9), a=bf(6), b=bf(1): e.tensor_tensor(o, a, b, ALU.mult)),
               reads=[bn(6), bn(1)], writes=[bn(9)], sig=SG)
            em("gpsimd", (lambda e, o=hm16[:, 0:8, :], a=bb(7), b=bb(9):
                          e.tensor_tensor(o, a, b, ALU.subtract)),
               reads=[bn(7), bn(9)], writes=[hmn], sig=SG)
            em("gpsimd", (lambda e, o=bf(7), a=bf(5), b=bf(1): e.tensor_tensor(o, a, b, ALU.mult)),
               reads=[bn(5), bn(1)], writes=[bn(7)], sig=SG)
            em("gpsimd", (lambda e, o=bf(9), a=bf(6), b=bf(4): e.tensor_tensor(o, a, b, ALU.mult)),
               reads=[bn(6), bn(4)], writes=[bn(9)], sig=SG)
            em("gpsimd", (lambda e, o=hm16[:, 8:16, :], a=bb(7), b=bb(9):
                          e.tensor_tensor(o, a, b, ALU.add)),
               reads=[bn(7), bn(9)], writes=[hmn], sig=SG)
            em("sync",
               (lambda e, ts0=t0 + sub * SUB, hmn=hmn: e.dma_start(
                   out=hm_sp.rearrange("(kt p) t -> p kt t", p=128)[:, :, ts0:ts0 + SUB],
                   in_=ctx[hmn][:].rearrange("p (kt t) -> p kt t", kt=16))),
               reads=[hmn], writes=[f"hm_sp{seg}"], sig="dh", dma=True)

    p.barrier()
    free_group(A_GROUP)
    B_GROUP = ["b_wout", "b_hmt0", "b_hmt1", "b_y0", "b_y1", "b_xnat", "b_x1", "b_xn",
               "b_qx", "b_dqx", "b_scr0", "b_am", "b_s", "b_is",
               "b_sm0", "b_sm1", "c_boutrow", "c_orb"]
    sb("c_xqt", [128, 16 * 2048], BF16)
    sb("b_wout", [128, 16 * 2048], BF16)
    sb("b_hmt0", [128, 16 * 128], BF16)
    sb("b_hmt1", [128, 16 * 128], BF16)
    sb("b_y0", [128, 2048], F32)
    sb("b_y1", [128, 2048], F32)
    sb("b_xnat", [128, 2048], F32)
    sb("b_x1", [128, 2048], F32)
    sb("b_xn", [128, 2048], F32)
    sb("b_qx", [128, 2048], BF16)
    sb("b_dqx", [128, 2048], BF16)
    sb("b_scr0", [128, 2048], F32)
    sb("b_am", [128, 64], F32)
    sb("b_s", [128, 64], F32)
    sb("b_is", [128, 64], F32)
    sb("b_sm0", [128, 16], F32)
    sb("b_sm1", [128, 16], F32)
    sb("c_boutrow", [1, D], BF16)
    sb("c_orb", [1, 128], BF16)
    em("gpsimd", lambda e: e.dma_start(out=ctx["c_boutrow"][:], in_=b_out_row),
       writes=["c_boutrow"], sig="dg", dma=True)
    em("sync", lambda e: e.dma_start(out=ctx["c_orb"][:], in_=ones_row_b),
       writes=["c_orb"], sig="dh", dma=True)
    em("gpsimd", lambda e: e.dma_start(out=ctx["b_wout"][:], in_=w_out),
       writes=["b_wout"], sig="dg", dma=True)

    for tt in range(NTT):
        hmt = "b_hmt0" if tt % 2 == 0 else "b_hmt1"
        byn = f"b_y{tt % 2}"
        bscrn = "b_scr0"
        bsmn = f"b_sm{tt % 2}"
        em("sync",
           (lambda e, tt=tt, hmt=hmt: e.dma_start(
               out=ctx[hmt][:].rearrange("p (kt t) -> p kt t", kt=16),
               in_=hm_sp.rearrange("(kt p) t -> p kt t", p=128)[:, :, tt * 128:(tt + 1) * 128])),
           reads=[f"hm_sp{tt // 4}"], writes=[hmt], sig="dh", dma=True)
        em("gpsimd",
           (lambda e, tt=tt: e.dma_start(out=ctx["b_xnat"][:],
                                         in_=x_nat[tt * 128:(tt + 1) * 128, :])),
           writes=["b_xnat"], sig="dg", dma=True)
        for oc in range(4):
            bk, bnm = P[oc], PN[oc]
            for kt in range(16):
                em("tensor",
                   (lambda e, o=bk.ap()[:, 0:512],
                    l=ctx[hmt][:, kt * 128:(kt + 1) * 128],
                    r=ctx["b_wout"][:, kt * 2048 + oc * 512: kt * 2048 + (oc + 1) * 512],
                    st=(kt == 0): e.matmul(o, l, r, start=st, stop=False)),
                   reads=[hmt, "b_wout"], writes=[bnm], sig=ST if kt == 15 else None)
            em("tensor",
               (lambda e, o=bk.ap()[:, 0:512], l=ctx["c_orb"][0:1, :],
                r=ctx["c_boutrow"][0:1, oc * 512:(oc + 1) * 512]:
                e.matmul(o, l, r, start=False, stop=True)),
               reads=["c_orb", "c_boutrow"], writes=[bnm], sig=ST)
            em("scalar",
               (lambda e, oc=oc, bk=bk, byn=byn: e.activation(
                   ctx[byn][:, oc * 512:(oc + 1) * 512], bk.ap()[:, 0:512], ACT.Copy)),
               reads=[bnm], writes=[byn], sig=SS)

        def ln_stats(src, smn, c_ss, c_s, scrn=None):
            scrn = scrn or bscrn
            em("vector",
               (lambda e, scrn=scrn: e.scalar_tensor_tensor(ctx[scrn][:], ctx[src][:], 1.0,
                                                 ctx[src][:], ALU.mult, ALU.mult,
                                                 accum_out=ctx[smn][:, c_ss:c_ss + 1])),
               reads=[src], writes=[scrn, smn], sig=SV)
            em("vector",
               (lambda e, scrn=scrn: e.tensor_scalar(ctx[scrn][:], ctx[src][:], 1.0, None,
                                          ALU.mult, ALU.add,
                                          accum_out=ctx[smn][:, c_s:c_s + 1])),
               reads=[src], writes=[scrn, smn], sig=SV)

        def ln_smalls(smn, c_ss, c_s, c_mean, c_var, c_sd, c_r, c_mrn, n_elem, eng_sm="vector"):
            em(eng_sm,
               (lambda e: e.tensor_scalar(ctx[smn][:, c_mean:c_mean + 1],
                                          ctx[smn][:, c_s:c_s + 1], 1.0 / n_elem, None, ALU.mult)),
               reads=[smn], writes=[smn], sig=SV)
            em(eng_sm,
               (lambda e: e.tensor_scalar(ctx[smn][:, c_var:c_var + 1],
                                          ctx[smn][:, c_ss:c_ss + 1], 1.0 / n_elem, None, ALU.mult)),
               reads=[smn], writes=[smn], sig=SV)
            em(eng_sm,
               (lambda e: e.scalar_tensor_tensor(ctx[smn][:, c_sd:c_sd + 1],
                                                 ctx[smn][:, c_mean:c_mean + 1], -1.0,
                                                 ctx[smn][:, c_mean:c_mean + 1], ALU.mult, ALU.mult)),
               reads=[smn], writes=[smn], sig=SV)
            em(eng_sm,
               (lambda e: e.tensor_tensor(ctx[smn][:, c_var:c_var + 1],
                                          ctx[smn][:, c_var:c_var + 1],
                                          ctx[smn][:, c_sd:c_sd + 1], ALU.add)),
               reads=[smn], writes=[smn], sig=SV)
            em("scalar",
               (lambda e: e.activation(ctx[smn][:, c_sd:c_sd + 1], ctx[smn][:, c_var:c_var + 1],
                                       ACT.Sqrt, bias=EPS)),
               reads=[smn], writes=[smn], sig=SS)
            em(eng_sm,
               (lambda e: e.reciprocal(ctx[smn][:, c_r:c_r + 1], ctx[smn][:, c_sd:c_sd + 1])),
               reads=[smn], writes=[smn], sig=SV)
            em(eng_sm,
               (lambda e: e.scalar_tensor_tensor(ctx[smn][:, c_mrn:c_mrn + 1],
                                                 ctx[smn][:, c_mean:c_mean + 1], -1.0,
                                                 ctx[smn][:, c_r:c_r + 1], ALU.mult, ALU.mult)),
               reads=[smn], writes=[smn], sig=SV)

        ln_stats(byn, bsmn, 0, 1)
        ln_smalls(bsmn, 0, 1, 2, 3, 4, 5, 6, D)
        em("scalar",
           (lambda e, byn=byn, bsmn=bsmn, bscrn=bscrn: e.activation(
               ctx[bscrn][:], ctx[byn][:], ACT.Identity,
               scale=ctx[bsmn][:, 5:6], bias=ctx[bsmn][:, 6:7])),
           reads=[byn, bsmn], writes=[bscrn], sig=SS)
        em("gpsimd",
           (lambda e, bscrn=bscrn: e.tensor_tensor(ctx["b_x1"][:], ctx[bscrn][:],
                                                   ctx["b_xnat"][:], ALU.add)),
           reads=[bscrn, "b_xnat"], writes=["b_x1"], sig=SG)
        em("sync",
           (lambda e, tt=tt: e.dma_start(out=x1_sp[tt * 128:(tt + 1) * 128, :], in_=ctx["b_x1"][:])),
           reads=["b_x1"], writes=[f"x1_sp{tt}"], sig="dh", dma=True)
        ln_stats("b_x1", bsmn, 8, 9)
        ln_smalls(bsmn, 8, 9, 10, 11, 12, 13, 14, D)
        em("scalar",
           (lambda e, bsmn=bsmn: e.activation(ctx["b_xn"][:], ctx["b_x1"][:], ACT.Identity,
                                   scale=ctx[bsmn][:, 13:14], bias=ctx[bsmn][:, 14:15])),
           reads=["b_x1", bsmn], writes=["b_xn"], sig=SS)
        em("vector",
           (lambda e: e.tensor_reduce(ctx["b_am"][:],
                                      ctx["b_xn"][:].rearrange("p (nb k) -> p nb k", k=32),
                                      AXX, ALU.max, apply_absolute_value=True)),
           reads=["b_xn"], writes=["b_am"], sig=SV)
        em("vector",
           (lambda e: e.tensor_scalar(ctx["b_s"][:], ctx["b_am"][:], 1.0 / 448.0, 1e-30,
                                      ALU.mult, ALU.max)),
           reads=["b_am"], writes=["b_s"], sig=SV)
        em("vector", (lambda e: e.reciprocal(ctx["b_is"][:], ctx["b_s"][:])),
           reads=["b_s"], writes=["b_is"], sig=SV)
        em("vector",
           (lambda e: e._custom_dve(
               FP8_RNE_MUL,
               out=ctx["b_qx"][:].rearrange("p (nb k) -> p nb k", k=32),
               in0=ctx["b_is"][:].unsqueeze(2).broadcast_to([128, 64, 32]),
               in1=ctx["b_xn"][:].rearrange("p (nb k) -> p nb k", k=32),
               s0=FP8_NOTMASK, s1=FP8_C1, imm2=FP8_C2)),
           reads=["b_is", "b_xn"], writes=["b_qx"], sig=SV)
        em("vector",
           (lambda e: e.tensor_tensor(
               ctx["b_dqx"][:].rearrange("p (nb k) -> p nb k", k=32),
               ctx["b_qx"][:].rearrange("p (nb k) -> p nb k", k=32),
               ctx["b_s"][:].unsqueeze(2).broadcast_to([128, 64, 32]), ALU.mult)),
           reads=["b_qx", "b_s"], writes=["b_dqx"], sig=SV)
        em("sync",
           (lambda e, tt=tt: e.dma_start(out=xq_sp[tt * 128:(tt + 1) * 128, :], in_=ctx["b_dqx"][:])),
           reads=["b_dqx"], writes=[f"xq_sp{tt}"], sig="dh", dma=True)
        for kt in range(16):
            qeng, qsig = ("sync", "dh") if kt % 2 == 0 else ("scalar", "ds")
            em(qeng,
               (lambda e, tt=tt, kt=kt: e.dma_start(
                   out=ctx["c_xqt"][:, kt * 2048 + tt * 128: kt * 2048 + (tt + 1) * 128],
                   in_=xq_sp[tt * 128:(tt + 1) * 128, kt * 128:(kt + 1) * 128],
                   transpose=True)),
               reads=[f"xq_sp{tt}"], writes=["c_xqt"], sig=qsig, dma=True)

    p.barrier()
    free_group(B_GROUP)
    C_GROUP = ["c_xqt", "c_rhs_g0", "c_rhs_g1", "c_rhs_u0", "c_rhs_u1",
               "c_sg0", "c_sg1", "c_tg0", "c_tg1", "c_hs0", "c_hs1",
               "c_qh0", "c_qh1", "c_dqh0", "c_dqh1", "c_dqht0", "c_dqht1",
               "c_ham", "c_hss", "c_his"]
    for par in range(2):
        sb(f"c_rhs_g{par}", [128, 16 * 512], BF16)
        sb(f"c_rhs_u{par}", [128, 16 * 512], BF16)
        sb(f"c_sg{par}", [128, 512], F32)
        sb(f"c_tg{par}", [128, 512], F32)
        sb(f"c_hs{par}", [128, 512], F32)
        sb(f"c_qh{par}", [128, 512], BF16)
        sb(f"c_dqh{par}", [128, 512], BF16)
        sb(f"c_dqht{par}", [128, 512], BF16)
    sb("c_ham", [128, 16], F32)
    sb("c_hss", [128, 16], F32)
    sb("c_his", [128, 16], F32)
    PB = [PB6, PB7]
    PBN = ["PB6", "PB7"]
    def quantize(src, n_free, nblk, amn, sn, isn, qn, dqn, eng_dq):
        em("vector",
           (lambda e: e.tensor_reduce(ctx[amn][:, 0:nblk],
                                      ctx[src][:, 0:n_free].rearrange("p (nb k) -> p nb k", k=32),
                                      AXX, ALU.max, apply_absolute_value=True)),
           reads=[src], writes=[amn], sig=SV)
        em("vector",
           (lambda e: e.tensor_scalar(ctx[sn][:, 0:nblk], ctx[amn][:, 0:nblk],
                                      1.0 / 448.0, 1e-30, ALU.mult, ALU.max)),
           reads=[amn], writes=[sn], sig=SV)
        em("vector", (lambda e: e.reciprocal(ctx[isn][:, 0:nblk], ctx[sn][:, 0:nblk])),
           reads=[sn], writes=[isn], sig=SV)
        em("vector",
           (lambda e: e._custom_dve(
               FP8_RNE_MUL,
               out=ctx[qn][:, 0:n_free].rearrange("p (nb k) -> p nb k", k=32),
               in0=ctx[isn][:, 0:nblk].unsqueeze(2).broadcast_to([128, nblk, 32]),
               in1=ctx[src][:, 0:n_free].rearrange("p (nb k) -> p nb k", k=32),
               s0=FP8_NOTMASK, s1=FP8_C1, imm2=FP8_C2)),
           reads=[isn, src], writes=[qn], sig=SV)
        em(eng_dq,
           (lambda e: e.tensor_tensor(
               ctx[dqn][:, 0:n_free].rearrange("p (nb k) -> p nb k", k=32),
               ctx[qn][:, 0:n_free].rearrange("p (nb k) -> p nb k", k=32),
               ctx[sn][:, 0:nblk].unsqueeze(2).broadcast_to([128, nblk, 32]), ALU.mult)),
           reads=[qn, sn], writes=[dqn], sig=SG if eng_dq == "gpsimd" else SV)

    for hch in range(NHCH):
        par = hch % 2
        rg, ru = f"c_rhs_g{par}", f"c_rhs_u{par}"
        for (sp_t, rhsn) in ((gwq_sp, rg), (uwq_sp, ru)):
            em("gpsimd",
               (lambda e, sp_t=sp_t, rhsn=rhsn, hch=hch: e.dma_start(
                   out=ctx[rhsn][:].rearrange("p (kt n) -> p kt n", kt=16),
                   in_=sp_t.rearrange("(kt p) n -> p kt n", p=128)[
                       :, :, hch * 512:(hch + 1) * 512])),
               reads=[["gwq_sp", "uwq_sp"][0 if sp_t is gwq_sp else 1]],
               writes=[rhsn], sig="dg", dma=True)
        for tt in range(NTT):
            tpar = tt % 2
            bg_, bgn = P[tpar * 2], PN[tpar * 2]
            bu_, bun = P[tpar * 2 + 1], PN[tpar * 2 + 1]
            sgn, tgn, hsn = f"c_sg{tpar}", f"c_tg{tpar}", f"c_hs{tpar}"
            qhn, dqhn, dqhtn = f"c_qh{tpar}", f"c_dqh{tpar}", f"c_dqht{tpar}"
            for kt in range(16):
                l = ctx["c_xqt"][:, kt * 2048 + tt * 128: kt * 2048 + (tt + 1) * 128]
                em("tensor",
                   (lambda e, o=bg_.ap()[:, 0:512], l=l,
                    r=ctx[rg][:, kt * 512:(kt + 1) * 512], st=(kt == 0), sp=(kt == 15):
                    e.matmul(o, l, r, start=st, stop=sp)),
                   reads=["c_xqt", rg], writes=[bgn], sig=ST if kt == 15 else None)
            for kt in range(16):
                l = ctx["c_xqt"][:, kt * 2048 + tt * 128: kt * 2048 + (tt + 1) * 128]
                em("tensor",
                   (lambda e, o=bu_.ap()[:, 0:512], l=l,
                    r=ctx[ru][:, kt * 512:(kt + 1) * 512], st=(kt == 0), sp=(kt == 15):
                    e.matmul(o, l, r, start=st, stop=sp)),
                   reads=["c_xqt", ru], writes=[bun], sig=ST if kt == 15 else None)
            em("scalar",
               (lambda e, bg_=bg_, sgn=sgn: e.activation(ctx[sgn][:], bg_.ap()[:, 0:512], ACT.Sigmoid)),
               reads=[bgn], writes=[sgn], sig=SS)
            em("vector",
               (lambda e, bg_=bg_, sgn=sgn, tgn=tgn: e.tensor_tensor(
                   ctx[tgn][:], ctx[sgn][:], bg_.ap()[:, 0:512], ALU.mult)),
               reads=[sgn, bgn], writes=[tgn], sig=SV)
            em("vector",
               (lambda e, bu_=bu_, tgn=tgn, hsn=hsn: e.tensor_tensor(
                   ctx[hsn][:], ctx[tgn][:], bu_.ap()[:, 0:512], ALU.mult)),
               reads=[tgn, bun], writes=[hsn], sig=SV)
            quantize(hsn, 512, 16, "c_ham", "c_hss", "c_his", qhn, dqhn, "gpsimd")
            bk, bnm = PB[tpar], PBN[tpar]
            for k4 in range(4):
                em("tensor",
                   (lambda e, o=bk.ap()[:, k4 * 128:(k4 + 1) * 128],
                    i_=ctx[dqhn][:, k4 * 128:(k4 + 1) * 128]:
                    e.transpose(o, i_, ctx["c_idb"][:])),
                   reads=[dqhn, "c_idb"], writes=[bnm], sig=ST if k4 == 3 else None)
            em("scalar",
               (lambda e, bk=bk, dqhtn=dqhtn: e.activation(ctx[dqhtn][:], bk.ap()[:, 0:512], ACT.Copy)),
               reads=[bnm], writes=[dqhtn], sig=SS)
            em("sync",
               (lambda e, hch=hch, tt=tt, dqhtn=dqhtn: e.dma_start(
                   out=hq_sp.rearrange("(r p) t -> p r t", p=128)[
                       :, hch * 4:(hch + 1) * 4, tt * 128:(tt + 1) * 128],
                   in_=ctx[dqhtn][:].rearrange("p (r t) -> p r t", r=4))),
               reads=[dqhtn], writes=["hq_sp"], sig="dh", dma=True)

    p.barrier()
    free_group(C_GROUP)
    psum_free(["PB7", "PB6"])
    Q6 = psum("Q6")
    Q7 = psum("Q7")
    DP = P + [Q6, Q7]
    DPN = PN + ["Q6", "Q7"]
    sb("d_hq", [128, 64 * QT], BF16)
    sb("d_rhs0", [128, 8 * 512], BF16)
    sb("d_rhs1", [128, 8 * 512], BF16)
    for i in range(4):
        sb(f"d_y{i}", [128, 2048], F32)
    sb("d_xn1", [128, 2048], F32)
    sb("d_fin", [128, 2048], F32)
    sb("d_sm", [128, 16], F32)
    for q in range(NQT):
        em("gpsimd",
           (lambda e, q=q: e.dma_start(
               out=ctx["d_hq"][:].rearrange("p (kt t) -> p kt t", kt=64),
               in_=hq_sp.rearrange("(kt p) t -> p kt t", p=128)[:, :, q * QT:(q + 1) * QT])),
           reads=["hq_sp"], writes=["d_hq"], sig="dg", dma=True)
        for oc in range(4):
            pb = 4 * (oc % 2)
            for kg in range(8):
                rhs = "d_rhs0" if kg % 2 == 0 else "d_rhs1"
                em("sync",
                   (lambda e, kg=kg, oc=oc, rhs=rhs: e.dma_start(
                       out=ctx[rhs][:].rearrange("p (k n) -> p k n", k=8),
                       in_=dwq_sp.rearrange("(k p) n -> p k n", p=128)[
                           :, kg * 8:(kg + 1) * 8, oc * 512:(oc + 1) * 512])),
                   reads=["dwq_sp"], writes=[rhs], sig="dh", dma=True)
                for k8 in range(8):
                    kt = kg * 8 + k8
                    for t4 in range(4):
                        bk, bnm = DP[pb + t4], DPN[pb + t4]
                        em("tensor",
                           (lambda e, o=bk.ap()[:, 0:512],
                            l=ctx["d_hq"][:, kt * QT + t4 * 128: kt * QT + (t4 + 1) * 128],
                            r=ctx[rhs][:, k8 * 512:(k8 + 1) * 512],
                            st=(kt == 0), sp=(kt == 63): e.matmul(o, l, r, start=st, stop=sp)),
                           reads=["d_hq", rhs], writes=[bnm],
                           sig=ST if (kt == 63 or t4 == 3) else None)
            for t4 in range(4):
                bk, bnm = DP[pb + t4], DPN[pb + t4]
                em("scalar",
                   (lambda e, t4=t4, oc=oc, bk=bk: e.activation(
                       ctx[f"d_y{t4}"][:, oc * 512:(oc + 1) * 512], bk.ap()[:, 0:512], ACT.Copy)),
                   reads=[bnm], writes=[f"d_y{t4}"], sig=SS)
        for t4 in range(4):
            tt = q * 4 + t4
            yn = f"d_y{t4}"
            em("vector",
               (lambda e, yn=yn: e.scalar_tensor_tensor(ctx["d_fin"][:], ctx[yn][:], 1.0,
                                                        ctx[yn][:], ALU.mult, ALU.mult,
                                                        accum_out=ctx["d_sm"][:, 0:1])),
               reads=[yn], writes=["d_fin", "d_sm"], sig=SV)
            em("vector",
               (lambda e, yn=yn: e.tensor_scalar(ctx["d_fin"][:], ctx[yn][:], 1.0, None,
                                                 ALU.mult, ALU.add,
                                                 accum_out=ctx["d_sm"][:, 1:2])),
               reads=[yn], writes=["d_fin", "d_sm"], sig=SV)
            em("vector",
               (lambda e: e.tensor_scalar(ctx["d_sm"][:, 2:3], ctx["d_sm"][:, 1:2],
                                          1.0 / D, None, ALU.mult)),
               reads=["d_sm"], writes=["d_sm"], sig=SV)
            em("vector",
               (lambda e: e.tensor_scalar(ctx["d_sm"][:, 3:4], ctx["d_sm"][:, 0:1],
                                          1.0 / D, None, ALU.mult)),
               reads=["d_sm"], writes=["d_sm"], sig=SV)
            em("vector",
               (lambda e: e.scalar_tensor_tensor(ctx["d_sm"][:, 4:5], ctx["d_sm"][:, 2:3],
                                                 -1.0, ctx["d_sm"][:, 2:3], ALU.mult, ALU.mult)),
               reads=["d_sm"], writes=["d_sm"], sig=SV)
            em("vector",
               (lambda e: e.tensor_tensor(ctx["d_sm"][:, 3:4], ctx["d_sm"][:, 3:4],
                                          ctx["d_sm"][:, 4:5], ALU.add)),
               reads=["d_sm"], writes=["d_sm"], sig=SV)
            em("scalar",
               (lambda e: e.activation(ctx["d_sm"][:, 5:6], ctx["d_sm"][:, 3:4], ACT.Sqrt, bias=EPS)),
               reads=["d_sm"], writes=["d_sm"], sig=SS)
            em("vector",
               (lambda e: e.reciprocal(ctx["d_sm"][:, 6:7], ctx["d_sm"][:, 5:6])),
               reads=["d_sm"], writes=["d_sm"], sig=SV)
            em("vector",
               (lambda e: e.scalar_tensor_tensor(ctx["d_sm"][:, 7:8], ctx["d_sm"][:, 2:3],
                                                 -1.0, ctx["d_sm"][:, 6:7], ALU.mult, ALU.mult)),
               reads=["d_sm"], writes=["d_sm"], sig=SV)
            em("gpsimd",
               (lambda e, tt=tt: e.dma_start(out=ctx["d_xn1"][:],
                                             in_=x1_sp[tt * 128:(tt + 1) * 128, :])),
               reads=[f"x1_sp{tt}"], writes=["d_xn1"], sig="dg", dma=True)
            em("scalar",
               (lambda e, yn=yn: e.activation(ctx["d_fin"][:], ctx[yn][:], ACT.Identity,
                                              scale=ctx["d_sm"][:, 6:7], bias=ctx["d_sm"][:, 7:8])),
               reads=[yn, "d_sm"], writes=["d_fin"], sig=SS)
            em("vector",
               (lambda e: e.tensor_tensor(ctx["d_fin"][:], ctx["d_fin"][:], ctx["d_xn1"][:], ALU.add)),
               reads=["d_fin", "d_xn1"], writes=["d_fin"], sig=SV)
            em("sync",
               (lambda e, tt=tt: e.dma_start(out=out_hbm[tt * 128:(tt + 1) * 128, :],
                                             in_=ctx["d_fin"][:])),
               reads=["d_fin"], sig="dh", dma=True)

    sched = p.schedule()
    final_waits = [(s, v) for (s, v) in p.final_cnt.items()
                   if s.startswith(("dh", "dg", "ds"))]

    sems = {}
    for name in sorted(p.final_cnt):
        sems[name] = nc.semaphore(name).__enter__()

    with nc.Block() as block:
        def run_engine(eng_obj, name):
            emitted = {}
            for (fn, waits, sig) in sched[name]:
                for (sname, val) in waits:
                    if val > emitted.get(sname, -1):
                        eng_obj.wait_ge(sems[sname], val)
                        emitted[sname] = val
                ins = fn(eng_obj)
                if sig is not None:
                    sname, amt = sig
                    ins.then_inc(sems[sname], amt)
            if name == "sync":
                for (sname, val) in final_waits:
                    eng_obj.wait_ge(sems[sname], val)

        @block.sync
        def _(e):
            run_engine(e, "sync")

        @block.scalar
        def _(e):
            run_engine(e, "scalar")

        @block.tensor
        def _(e):
            run_engine(e, "tensor")

        @block.vector
        def _(e):
            run_engine(e, "vector")

        @block.gpsimd
        def _(e):
            run_engine(e, "gpsimd")

    lower_extended_insts(nc)
    return nc


_NC_CACHE = [None]


def kernel(L0_attn_x, L0_attn_W_in, L0_attn_b_in, L0_attn_W_gx, L0_attn_b_gx,
           L0_attn_W_ga, L0_attn_b_ga, L0_attn_a_real, L0_attn_a_imag,
           L0_attn_W_out, L0_attn_b_out, L0_ffn_gate_w, L0_ffn_up_w, L0_ffn_down_w):
    x = np.ascontiguousarray(np.asarray(L0_attn_x, dtype=np.float32))
    W_in = np.asarray(L0_attn_W_in, dtype=np.float32)
    W_out = np.asarray(L0_attn_W_out, dtype=np.float32)
    W_gx = np.asarray(L0_attn_W_gx, dtype=np.float32)
    W_ga = np.asarray(L0_attn_W_ga, dtype=np.float32)
    b_gx = np.asarray(L0_attn_b_gx, dtype=np.float32)
    b_ga = np.asarray(L0_attn_b_ga, dtype=np.float32)
    b_in = np.asarray(L0_attn_b_in, dtype=np.float32)
    b_out = np.asarray(L0_attn_b_out, dtype=np.float32)
    a_real = np.asarray(L0_attn_a_real, dtype=np.float32)
    a_imag = np.asarray(L0_attn_a_imag, dtype=np.float32)
    gate_w = np.ascontiguousarray(np.asarray(L0_ffn_gate_w, dtype=np.float32))
    up_w = np.ascontiguousarray(np.asarray(L0_ffn_up_w, dtype=np.float32))
    down_w = np.ascontiguousarray(np.asarray(L0_ffn_down_w, dtype=np.float32))

    w_in_l = np.ascontiguousarray(
        W_in.reshape(16, 128, 16, 128).transpose(1, 0, 2, 3).reshape(128, 16 * 16 * 128))
    w_out_l = np.ascontiguousarray(
        W_out.reshape(16, 128, 2048).transpose(1, 0, 2).reshape(128, 16 * 2048))
    wg = np.stack([W_gx, W_ga], 0).reshape(2, 8, 2, 128, 128)
    w_g_l = np.ascontiguousarray(wg.transpose(3, 0, 1, 2, 4).reshape(128, 2 * 8 * 2 * 128))
    b_g_l = np.ascontiguousarray(np.concatenate([b_gx.reshape(-1), b_ga.reshape(-1)], 0)
                                 .reshape(16, 128).T)
    ar_l = np.ascontiguousarray(a_real.reshape(8, 128).T)
    aim_l = np.ascontiguousarray(a_imag.reshape(8, 128).T)
    b_in_l = np.ascontiguousarray(b_in.reshape(1, DH))
    b_out_l = np.ascontiguousarray(b_out.reshape(1, D))
    ident_f = np.eye(128, dtype=np.float32)
    ident_b = np.eye(128).astype(ml_dtypes.bfloat16)
    ones_col_b = np.ones((128, 1), dtype=ml_dtypes.bfloat16)
    ones_row_f = np.ones((1, 128), dtype=np.float32)
    ones_row_b = np.ones((1, 128), dtype=ml_dtypes.bfloat16)
    lit_arr = np.tile(np.array([0.0, 1.0, 2.0, 0.5, EPS, PI / 2, -PI, 1e-24],
                               dtype=np.float32), (128, 1))

    if _NC_CACHE[0] is None:
        _NC_CACHE[0] = build_program()
    nc = _NC_CACHE[0]

    in_maps = []
    for c in range(NCORES):
        xc = np.ascontiguousarray(x[c])
        in_maps.append({
            "x_tr": np.ascontiguousarray(xc.T),
            "x_nat": xc,
            "w_in": w_in_l,
            "w_out": w_out_l,
            "w_g": w_g_l,
            "b_g": b_g_l,
            "b_in_row": b_in_l,
            "b_out_row": b_out_l,
            "ar_col": ar_l,
            "aim_col": aim_l,
            "gate_w_nat": gate_w,
            "up_w_nat": up_w,
            "down_w_nat": down_w,
            "ident_f": ident_f,
            "ident_b": ident_b,
            "ones_col_b": ones_col_b,
            "ones_row_f": ones_row_f,
            "ones_row_b": ones_row_b,
            "lit_in": lit_arr,
        })
    res = run_bass_kernel_spmd(nc, in_maps, core_ids=list(range(NCORES)))
    out = np.stack([np.asarray(r["out"]) for r in res.results], 0)
    return out.astype(np.float32)


# version 08
# speedup vs baseline: 1.2223x; 1.0213x over previous
import numpy as np
import ml_dtypes
import concourse.bass as bass
import concourse.mybir as mybir
from concourse.bass_utils import run_bass_kernel_spmd
from concourse import dve_ops
from concourse.dve_op import DveOp
from concourse.dve_spec import AluOp as DAlu, Bin, C0, C1, C2, Spec, Src0, Src1, maxx
from concourse.library_overlay import lower_extended_insts

F32 = mybir.dt.float32
BF16 = mybir.dt.bfloat16
ALU = mybir.AluOpType
AXX = mybir.AxisListType.X
ACT = mybir.ActivationFunctionType

_FP8_X = Src0 * Src1
_FP8_NX = Bin(DAlu.BITWISE_NOT, _FP8_X, _FP8_X)
_FP8_P = Bin(DAlu.BITWISE_NOT, Bin(DAlu.BITWISE_OR, _FP8_NX, C0), _FP8_NX)
_FP8_CM = maxx(_FP8_P, C1) * C2
FP8_NOTMASK = float(np.uint32(0x807FFFFF).view(np.float32))
FP8_C1 = float(2.0 ** -6)
FP8_C2 = float(1.5 * 2.0 ** 20)


def _fp8_rne_ref(in0, in1, s0, s1, imm2):
    f = np.float64 if in0.dtype == np.float64 else np.float32
    x = in0.astype(f) * in1.astype(f)
    if f is np.float64:
        ax = np.abs(x)
        _, e = np.frexp(ax)
        pc = np.maximum(np.ldexp(np.float64(1.0), e - 1), 2.0 ** -6)
        ulp = pc * (2.0 ** -3)
        return np.round(x / ulp) * ulp
    x = x.astype(np.float32)
    p = (x.view(np.int32) & np.int32(0x7F800000)).view(np.float32)
    pc = np.maximum(p, np.float32(2.0 ** -6))
    cm = (pc * np.float32(1.5 * 2.0 ** 20)).astype(np.float32)
    t = (x + cm).astype(np.float32)
    return (t - cm).astype(np.float32)


FP8_RNE_MUL = DveOp(
    "FP8_RNE_MUL",
    Spec(body=(_FP8_X + _FP8_CM) - _FP8_CM),
    subdim=False,
    uops_sha={"v3": "23f3018ff8a04129"},
    reference=_fp8_rne_ref,
)

if not any(o.name == "FP8_RNE_MUL" for o in dve_ops.OPS):
    dve_ops.OPS.append(FP8_RNE_MUL)
    dve_ops.CUSTOM_DVE_SPECS[FP8_RNE_MUL.name] = FP8_RNE_MUL.spec
    dve_ops.CUSTOM_DVE_REFS[FP8_RNE_MUL.name] = FP8_RNE_MUL.reference
    dve_ops._SUB_OPCODE_FOR_NAME[FP8_RNE_MUL.name] = 31

RCF = dve_ops.RECIPROCAL_APPROX_FAST
RCF_C = dve_ops.RECIP_APPROX_FAST_CONSTS

_WR_R = (Src0 * C0 + C1) - C1
WRAP_RNE = DveOp(
    "WRAP_RNE",
    Spec(body=Src0 - _WR_R * C2),
    subdim=False,
    uops_sha={"v3": "384a601106e5b300"},
    reference=lambda in0, in1, s0, s1, imm2: (
        (lambda f, x: (x - np.round(x * np.float64(s0)) * np.float64(imm2))
         if f is np.float64 else
         (x - (((x * np.float32(s0)).astype(np.float32) + np.float32(s1)).astype(np.float32)
               - np.float32(s1)).astype(np.float32) * np.float32(imm2)).astype(np.float32)
         )(np.float64 if in0.dtype == np.float64 else np.float32, in0)
    ),
)
if not any(o.name == "WRAP_RNE" for o in dve_ops.OPS):
    dve_ops.OPS.append(WRAP_RNE)
    dve_ops.CUSTOM_DVE_SPECS[WRAP_RNE.name] = WRAP_RNE.spec
    dve_ops.CUSTOM_DVE_REFS[WRAP_RNE.name] = WRAP_RNE.reference
    dve_ops._SUB_OPCODE_FOR_NAME[WRAP_RNE.name] = 30
WRAP_MAGIC = float(1.5 * 2.0 ** 23)
PI_CLAMP = float(np.float32(3.14159251))

NCORES = 8
T = 2048
D = 2048
DH = 2048
I_FFN = 8192
SEG = 256
NSEG = T // SEG
SUB = 64
NSUB = SEG // SUB
NTT = T // 128
EPS = 1e-5
PI = float(np.pi)
INV2PI = float(1.0 / (2.0 * np.pi))
TWOPI = float(2.0 * np.pi)
NHCH = I_FFN // 512
QT = 512
NQT = T // QT


FIFO_SCHED = False


class Buf:
    __slots__ = ("wtok", "rtoks", "lw", "ridx")

    def __init__(self):
        self.wtok = None
        self.rtoks = []
        self.lw = None
        self.ridx = []


class Plan:

    ENGINES = ("sync", "scalar", "tensor", "vector", "gpsimd")
    DEF_COST = {"sync": 1500, "scalar": 600, "tensor": 290, "vector": 700,
                "gpsimd": 1500}

    def __init__(self):
        self.ops = []
        self.cnt = {}
        self.tokprod = {}
        self.bufs = {}
        self.epoch = 0
        self.last_op = {e: None for e in self.ENGINES}

    def barrier(self):
        self.epoch += 1

    def buf(self, name):
        if name not in self.bufs:
            self.bufs[name] = Buf()
        return self.bufs[name]

    def emit(self, eng, fn, reads=(), writes=(), waits=(), sig=None, dma=False,
             cost=None):
        wl = list(waits)
        idx = len(self.ops)
        deps = set()
        for n in reads:
            b = self.buf(n)
            if b.wtok is not None:
                wl.append(b.wtok)
            if b.lw is not None and self.ops[b.lw]["engine"] == eng:
                deps.add(b.lw)
        for n in writes:
            b = self.buf(n)
            if b.wtok is not None:
                wl.append(b.wtok)
            wl.extend(b.rtoks)
            if b.lw is not None and self.ops[b.lw]["engine"] == eng:
                deps.add(b.lw)
            for r in b.ridx:
                if self.ops[r]["engine"] == eng:
                    deps.add(r)
        for (s, v) in wl:
            p = self.tokprod.get((s, v))
            if p is not None:
                deps.add(p)
        lo = self.last_op[eng]
        if lo is not None and (not lo[1] or sig is None):
            deps.add(lo[0])
        tok = None
        if sig is not None:
            self.cnt[sig] = self.cnt.get(sig, 0) + (16 if dma else 1)
            tok = (sig, self.cnt[sig])
            self.tokprod[tok] = idx
        self.last_op[eng] = (idx, sig is not None)
        self.ops.append({"engine": eng, "fn": fn, "deps": deps, "sig": sig,
                         "dma": dma, "epoch": self.epoch,
                         "cost": cost if cost is not None else self.DEF_COST[eng]})
        if tok is not None:
            for n in writes:
                b = self.buf(n)
                b.wtok = tok
                b.rtoks = []
            for n in reads:
                self.buf(n).rtoks.append(tok)
        for n in writes:
            b = self.buf(n)
            b.lw = idx
            b.ridx = []
        for n in reads:
            self.buf(n).ridx.append(idx)
        return tok

    def schedule(self):
        import heapq
        n = len(self.ops)
        children = [[] for _ in range(n)]
        indeg = [0] * n
        for i, op in enumerate(self.ops):
            for d in op["deps"]:
                children[d].append(i)
                indeg[i] += 1
        level = [0.0] * n
        for i in range(n - 1, -1, -1):
            c = self.ops[i]["cost"]
            m = 0.0
            for ch in children[i]:
                if level[ch] > m:
                    m = level[ch]
            level[i] = c + m
        etime = {e: 0.0 for e in self.ENGINES}
        done = [0.0] * n
        order = {e: [] for e in self.ENGINES}
        if FIFO_SCHED:
            for i, op in enumerate(self.ops):
                order[op["engine"]].append(i)
            return self._materialize(order)
        n_epochs = self.epoch + 1
        by_epoch = [[] for _ in range(n_epochs)]
        for i, op in enumerate(self.ops):
            by_epoch[op["epoch"]].append(i)
        for ep in range(n_epochs):
            ready = {e: [] for e in self.ENGINES}
            for i in by_epoch[ep]:
                if indeg[i] == 0:
                    heapq.heappush(ready[self.ops[i]["engine"]], (-level[i], i))
            remaining = len(by_epoch[ep])
            ep_floor = max(etime.values()) if ep > 0 else 0.0
            for e in self.ENGINES:
                etime[e] = max(etime[e], ep_floor)
            while remaining:
                cands = [e for e in self.ENGINES if ready[e]]
                e = min(cands, key=lambda x: etime[x])
                _, i = heapq.heappop(ready[e])
                op = self.ops[i]
                start = etime[e]
                for d in op["deps"]:
                    if done[d] > start:
                        start = done[d]
                fin = start + op["cost"]
                etime[e] = fin
                done[i] = fin
                order[e].append(i)
                remaining -= 1
                for ch in children[i]:
                    indeg[ch] -= 1
                    if indeg[ch] == 0 and self.ops[ch]["epoch"] == ep:
                        heapq.heappush(ready[self.ops[ch]["engine"]], (-level[ch], ch))
        return self._materialize(order)

    def _materialize(self, order):
        n = len(self.ops)
        n_epochs = self.epoch + 1
        NLANE = 8
        cnt = {}
        toks = [None] * n
        prewait = [None] * n
        lane_idx = {}
        for e in self.ENGINES:
            for i in order[e]:
                op = self.ops[i]
                if op["sig"] is None:
                    continue
                if op["dma"]:
                    q = op["sig"]
                    k = lane_idx.get(q, 0)
                    lane_idx[q] = k + 1
                    lane = f"{q}{k % NLANE}"
                    prev = cnt.get(lane, 0)
                    if k >= NLANE:
                        prewait[i] = (lane, prev)
                    cnt[lane] = prev + 16
                    toks[i] = (lane, cnt[lane])
                    op["lane_sig"] = (lane, 16)
                else:
                    cnt[op["sig"]] = cnt.get(op["sig"], 0) + 1
                    toks[i] = (op["sig"], cnt[op["sig"]])
                    op["lane_sig"] = (op["sig"], 1)
        ecnt = [dict() for _ in range(n_epochs + 1)]
        run = {}
        for ep in range(n_epochs):
            for e in self.ENGINES:
                for i in order[e]:
                    if self.ops[i]["epoch"] == ep and toks[i] is not None:
                        s, v = toks[i]
                        if v > run.get(s, -1):
                            run[s] = v
            ecnt[ep + 1] = dict(run)
        sched = {e: [] for e in self.ENGINES}
        for e in self.ENGINES:
            for i in order[e]:
                op = self.ops[i]
                wd = dict(ecnt[op["epoch"]])
                if prewait[i] is not None and prewait[i][1] > wd.get(prewait[i][0], -1):
                    wd[prewait[i][0]] = prewait[i][1]
                for d in op["deps"]:
                    t = toks[d]
                    if t is not None and t[1] > wd.get(t[0], -1):
                        wd[t[0]] = t[1]
                sg = op.get("lane_sig") if op["sig"] else None
                sched[e].append((op["fn"], sorted(wd.items()), sg))
        self.final_cnt = cnt
        self._dbg_order = order
        return sched


def build_program():
    nc = bass.Bass("TRN2", target_bir_lowering=False)
    dt = nc.dram_tensor

    x_tr = dt("x_tr", [D, T], F32, kind="ExternalInput").ap()
    x_nat = dt("x_nat", [T, D], F32, kind="ExternalInput").ap()
    w_in = dt("w_in", [128, 16 * 16 * 128], F32, kind="ExternalInput").ap()
    w_out = dt("w_out", [128, 16 * 2048], F32, kind="ExternalInput").ap()
    w_g = dt("w_g", [128, 2 * 8 * 2 * 128], F32, kind="ExternalInput").ap()
    b_g = dt("b_g", [128, 16], F32, kind="ExternalInput").ap()
    b_in_row = dt("b_in_row", [1, DH], F32, kind="ExternalInput").ap()
    b_out_row = dt("b_out_row", [1, D], F32, kind="ExternalInput").ap()
    ar_col = dt("ar_col", [128, 8], F32, kind="ExternalInput").ap()
    aim_col = dt("aim_col", [128, 8], F32, kind="ExternalInput").ap()
    gate_w_nat = dt("gate_w_nat", [I_FFN, D], F32, kind="ExternalInput").ap()
    up_w_nat = dt("up_w_nat", [I_FFN, D], F32, kind="ExternalInput").ap()
    down_w_nat = dt("down_w_nat", [D, I_FFN], F32, kind="ExternalInput").ap()
    ident_f = dt("ident_f", [128, 128], F32, kind="ExternalInput").ap()
    ident_b = dt("ident_b", [128, 128], BF16, kind="ExternalInput").ap()
    ones_col_b = dt("ones_col_b", [128, 1], BF16, kind="ExternalInput").ap()
    ones_row_f = dt("ones_row_f", [1, 128], F32, kind="ExternalInput").ap()
    ones_row_b = dt("ones_row_b", [1, 128], BF16, kind="ExternalInput").ap()
    lit_in = dt("lit_in", [128, 8], F32, kind="ExternalInput").ap()

    out_hbm = dt("out", [T, D], F32, kind="ExternalOutput").ap()
    hm_sp = dt("hm_sp", [DH, T], BF16, kind="Internal").ap()
    x1_sp = dt("x1_sp", [T, D], F32, kind="Internal").ap()
    xq_sp = dt("xq_sp", [T, D], BF16, kind="Internal").ap()
    hq_sp = dt("hq_sp", [I_FFN, T], BF16, kind="Internal").ap()
    dwq_sp = dt("dwq_sp", [I_FFN, D], BF16, kind="Internal").ap()
    gwq_sp = dt("gwq_sp", [D, I_FFN], BF16, kind="Internal").ap()
    uwq_sp = dt("uwq_sp", [D, I_FFN], BF16, kind="Internal").ap()

    ctx = {}
    handles = {}
    alloc_seq = []

    def sb(name, shape, dtype):
        h = nc.sbuf_tensor(name, shape, dtype)
        t = h.__enter__()
        handles[name] = h
        ctx[name] = t
        alloc_seq.append(name)
        return t

    def free_group(names):
        ns = set(names)
        for n in reversed(alloc_seq):
            if n in ns:
                handles[n].__exit__(None, None, None)
        alloc_seq[:] = [n for n in alloc_seq if n not in ns]

    psum_handles = {}

    def psum(name, shape=(128, 512), dtype=F32):
        h = nc.psum_tensor(list(shape), dtype)
        t = h.__enter__()
        ctx[name] = t
        psum_handles[name] = h
        return t

    def psum_free(names):
        for n in names:
            psum_handles[n].__exit__(None, None, None)

    sb("c_lit", [128, 8], F32)
    sb("c_idb", [128, 128], BF16)
    sb("c_bg", [128, 16], F32)
    sb("c_ar", [128, 8], F32)
    sb("c_c1", [128, 8], F32)
    sb("c_aim", [128, 8], F32)
    sb("c_idf", [128, 128], F32)
    sb("c_ocb", [128, 1], BF16)
    sb("c_orf", [1, 128], F32)
    sb("c_binrow", [1, DH], BF16)
    sb("c_colsum", [1, DH], BF16)
    sb("c_tmp8", [128, 8], F32)
    A_GROUP = (["a_win", "a_wg", "a_xtb", "a_xh0", "a_xh1", "a_xhb", "a_gx", "a_ga",
                "a_hm0", "a_hm1", "a_rows", "a_rows_b", "a_rrep", "a_stage", "a_sm",
                "a_carry", "qa_nat0", "qa_q", "qa_dq", "qa_am",
                "qa_s", "qa_is", "qa_tst0"]
               + [f"a_b{i}_{pp}" for i in range(1, 10) for pp in range(2)])
    sb("a_win", [128, 16 * 16 * 128], BF16)
    sb("a_wg", [128, 2 * 8 * 2 * 128], BF16)
    sb("a_xtb", [128, 16 * SEG], BF16)
    sb("a_xh0", [128, 16 * SEG], F32)
    sb("a_xh1", [128, 16 * SEG], F32)
    sb("a_xhb", [128, 16 * SEG], BF16)
    sb("a_gx", [128, 8 * SEG], F32)
    sb("a_ga", [128, 8 * SEG], F32)
    sb("qa_nat0", [128, 2048], F32)
    sb("qa_q", [128, 2048], BF16)
    sb("qa_dq", [128, 2048], BF16)
    sb("qa_am", [128, 64], F32)
    sb("qa_s", [128, 64], F32)
    sb("qa_is", [128, 64], F32)
    sb("qa_tst0", [128, 1024], BF16)
    for i in range(1, 10):
        for pp in range(2):
            sb(f"a_b{i}_{pp}", [128, 8 * SUB], F32)
    sb("a_hm0", [128, 16 * SUB], BF16)
    sb("a_hm1", [128, 16 * SUB], BF16)
    sb("a_rows", [1, SEG], F32)
    sb("a_rows_b", [1, 2 * SEG], BF16)
    sb("a_rrep", [128, SEG], F32)
    sb("a_stage", [128, 4], F32)
    sb("a_sm", [128, 16], F32)
    sb("a_carry", [128, 24], F32)
    P = [psum(f"P{i}") for i in range(6)]
    PN = [f"P{i}" for i in range(6)]
    PB6 = psum("PB6", (128, 1024), BF16)
    PB7 = psum("PB7", (128, 1024), BF16)

    p = Plan()

    def em(eng, fn, reads=(), writes=(), waits=(), sig=None, dma=False, cost=None):
        return p.emit(eng, fn, reads, writes, waits, sig, dma, cost)

    def mmc(n):
        return 60 + int(0.5 * n)

    def dvec(n, tier=1.1):
        return 300 + int(tier * n)

    def actc(n):
        return 250 + int(0.85 * n)

    def gpsc(n):
        return 500 + int(1.2 * n)

    def dmac(nbytes):
        return 1200 + int(nbytes / 250)

    SV, SS, ST, SG = "v", "s", "t", "g"

    LITS = [0.0, 1.0, 2.0, 0.5, EPS, PI / 2, -PI, 1e-24]
    for i, v in enumerate(LITS):
        nc.const_aps.register(F32, v, ctx["c_lit"][:, i:i + 1])
    em("sync", lambda e: e.dma_start(out=ctx["c_lit"][:], in_=lit_in),
       writes=["c_lit"], sig="dh", dma=True)
    p.barrier()

    for (dst, srcap) in [("c_bg", b_g), ("c_ar", ar_col), ("c_aim", aim_col),
                         ("c_idf", ident_f), ("c_idb", ident_b), ("c_ocb", ones_col_b),
                         ("c_orf", ones_row_f)]:
        em("sync", (lambda e, d=dst, s=srcap: e.dma_start(out=ctx[d][:], in_=s)),
           writes=[dst], sig="dh", dma=True)
    em("gpsimd", lambda e: e.dma_start(out=ctx["c_binrow"][:], in_=b_in_row),
       writes=["c_binrow"], sig="dg", dma=True)
    em("gpsimd", lambda e: e.dma_start(out=ctx["a_win"][:], in_=w_in),
       writes=["a_win"], sig="dg", dma=True)
    em("gpsimd", lambda e: e.dma_start(out=ctx["a_wg"][:], in_=w_g),
       writes=["a_wg"], sig="dg", dma=True)
    em("vector", lambda e: e.memset(ctx["a_carry"][:], 0.0), writes=["a_carry"], sig=SV)
    em("scalar", lambda e: e.activation(ctx["c_tmp8"][:], ctx["c_ar"][:], ACT.Exp),
       reads=["c_ar"], writes=["c_tmp8"], sig=SS)
    em("scalar", lambda e: e.activation(ctx["c_tmp8"][:], ctx["c_tmp8"][:] if False else ctx["c_tmp8"][:], ACT.Ln, bias=1.0) if False else None,
       sig=None) if False else None
    em("scalar", lambda e: e.activation(ctx["c_c1"][:], ctx["c_tmp8"][:], ACT.Ln, bias=1.0),
       reads=["c_tmp8"], writes=["c_c1"], sig=SS)
    em("vector", lambda e: e.tensor_scalar(ctx["c_c1"][:], ctx["c_c1"][:], -8.0, None, ALU.mult),
       reads=["c_c1"], writes=["c_c1"], sig=SV)

    for cc in range(4):
        for c4 in range(4):
            cht = cc * 4 + c4
            for kt in range(16):
                r0 = ctx["a_win"][:, (kt * 16 + cht) * 128:(kt * 16 + cht + 1) * 128]
                em("tensor",
                   (lambda e, o=P[5].ap()[0:1, c4 * 128:(c4 + 1) * 128], l=ctx["c_ocb"][:],
                    r=r0, st=(kt == 0), sp=(kt == 15): e.matmul(o, l, r, start=st, stop=sp)),
                   reads=["a_win", "c_ocb"], writes=["P5"],
                   sig=ST if (kt == 15 and c4 == 3) else None)
        em("vector",
           (lambda e, cc=cc: e.tensor_copy(ctx["c_colsum"][0:1, cc * 512:(cc + 1) * 512],
                                           P[5].ap()[0:1, 0:512])),
           reads=["P5"], writes=["c_colsum"], sig=SV)

    def quant_unit(kind, i, par):
        nat = "qa_nat0"
        if kind == 0:
            src_ap = gate_w_nat[i * 128:(i + 1) * 128, :]
            dst = gwq_sp
            roff, coff = 0, i * 128
        elif kind == 1:
            src_ap = up_w_nat[i * 128:(i + 1) * 128, :]
            dst = uwq_sp
            roff, coff = 0, i * 128
        else:
            n_i, q_i = i % 16, i // 16
            src_ap = down_w_nat[n_i * 128:(n_i + 1) * 128, q_i * 2048:(q_i + 1) * 2048]
            dst = dwq_sp
            roff, coff = q_i * 16, n_i * 128
        dstn = ["gwq_sp", "uwq_sp", "dwq_sp"][kind]
        em("gpsimd", (lambda e, s=src_ap, nat=nat: e.dma_start(out=ctx[nat][:], in_=s)),
           writes=[nat], sig="dg", dma=True, cost=dmac(128 * 2048 * 4))
        em("vector",
           (lambda e, nat=nat: e.tensor_reduce(
               ctx["qa_am"][:], ctx[nat][:].rearrange("p (nb k) -> p nb k", k=32),
               AXX, ALU.max, apply_absolute_value=True)),
           reads=[nat], writes=["qa_am"], sig=SV, cost=dvec(2048))
        em("vector",
           (lambda e: e.tensor_scalar(ctx["qa_s"][:], ctx["qa_am"][:], 1.0 / 448.0,
                                      1e-30, ALU.mult, ALU.max)),
           reads=["qa_am"], writes=["qa_s"], sig=SV)
        em("vector", (lambda e: e.reciprocal(ctx["qa_is"][:], ctx["qa_s"][:])),
           reads=["qa_s"], writes=["qa_is"], sig=SV)
        em("vector",
           (lambda e, nat=nat: e._custom_dve(
               FP8_RNE_MUL,
               out=ctx["qa_q"][:].rearrange("p (nb k) -> p nb k", k=32),
               in0=ctx["qa_is"][:].unsqueeze(2).broadcast_to([128, 64, 32]),
               in1=ctx[nat][:].rearrange("p (nb k) -> p nb k", k=32),
               s0=FP8_NOTMASK, s1=FP8_C1, imm2=FP8_C2)),
           reads=["qa_is", nat], writes=["qa_q"], sig=SV, cost=dvec(2048))
        em("gpsimd",
           (lambda e: e.tensor_tensor(
               ctx["qa_dq"][:].rearrange("p (nb k) -> p nb k", k=32),
               ctx["qa_q"][:].rearrange("p (nb k) -> p nb k", k=32),
               ctx["qa_s"][:].unsqueeze(2).broadcast_to([128, 64, 32]), ALU.mult)),
           reads=["qa_q", "qa_s"], writes=["qa_dq"], sig=SG, cost=gpsc(2048))
        for batch in range(2):
            tst = "qa_tst0"
            pbk = PB6 if batch == 0 else PB7
            pbn = "PB6" if batch == 0 else "PB7"
            for k8 in range(8):
                kt = batch * 8 + k8
                em("tensor",
                   (lambda e, o=pbk.ap()[:, k8 * 128:(k8 + 1) * 128],
                    i_=ctx["qa_dq"][:, kt * 128:(kt + 1) * 128]:
                    e.transpose(o, i_, ctx["c_idb"][:])),
                   reads=["qa_dq", "c_idb"], writes=[pbn], sig=ST if k8 == 7 else None)
            em("scalar",
               (lambda e, tst=tst, pbk=pbk: e.activation(ctx[tst][:], pbk.ap()[:, 0:1024], ACT.Copy)),
               reads=[pbn], writes=[tst], sig=SS, cost=actc(1024))
            em("sync",
               (lambda e, dst=dst, roff=roff, coff=coff, batch=batch, tst=tst: e.dma_start(
                   out=dst.rearrange("(r p) n -> p r n", p=128)[
                       :, roff + batch * 8:roff + (batch + 1) * 8, coff:coff + 128],
                   in_=ctx[tst][:].rearrange("p (r n) -> p r n", r=8))),
               reads=[tst], writes=[dstn], sig="dh", dma=True, cost=dmac(128 * 1024 * 2))

    QUANT_UNITS = ([(0, i) for i in range(64)] + [(1, i) for i in range(64)]
                   + [(2, i) for i in range(64)])
    q_cursor = [0]

    def emit_quant_batch(n):
        for _ in range(n):
            if q_cursor[0] >= len(QUANT_UNITS):
                return
            kind, i = QUANT_UNITS[q_cursor[0]]
            quant_unit(kind, i, q_cursor[0] % 2)
            q_cursor[0] += 1

    for seg in range(NSEG):
        emit_quant_batch(24)
        t0 = seg * SEG
        xhn = f"a_xh{seg % 2}"
        em("gpsimd",
           (lambda e, t0=t0: e.dma_start(
               out=ctx["a_xtb"][:].rearrange("p (kt t) -> p kt t", kt=16),
               in_=x_tr.rearrange("(kt p) t -> p kt t", p=128)[:, :, t0:t0 + SEG])),
           writes=["a_xtb"], sig="dg", dma=True, cost=dmac(128 * 16 * SEG * 4))

        for c in range(SEG // 128):
            for kt in range(16):
                xt = ctx["a_xtb"][:, kt * SEG + c * 128: kt * SEG + (c + 1) * 128]
                em("tensor",
                   (lambda e, o=P[4].ap()[:, 0:128], l=xt, r=xt, st=(kt == 0), sp=(kt == 15):
                    e.matmul(o, l, r, start=st, stop=sp)),
                   reads=["a_xtb"], writes=["P4"], sig=None)
            for kt in range(16):
                xt = ctx["a_xtb"][:, kt * SEG + c * 128: kt * SEG + (c + 1) * 128]
                em("tensor",
                   (lambda e, o=P[4].ap()[:, 128:129], l=xt, r=ctx["c_ocb"][:],
                    st=(kt == 0), sp=(kt == 15): e.matmul(o, l, r, start=st, stop=sp)),
                   reads=["a_xtb", "c_ocb"], writes=["P4"], sig=ST if kt == 15 else None)
            em("vector",
               (lambda e: e.tensor_tensor_reduce(ctx["a_b1_0"][:, 0:128], P[4].ap()[:, 0:128],
                                                 ctx["c_idf"][:], 1.0, 0.0, ALU.mult,
                                                 ALU.add, ctx["a_sm"][:, 0:1])),
               reads=["P4", "c_idf"], writes=["a_b1_0", "a_sm"], sig=SV)
            em("vector",
               (lambda e: e.tensor_scalar(ctx["a_sm"][:, 1:2], P[4].ap()[:, 128:129],
                                          1.0 / D, None, ALU.mult)),
               reads=["P4"], writes=["a_sm"], sig=SV)
            em("vector",
               (lambda e: e.tensor_scalar(ctx["a_sm"][:, 2:3], ctx["a_sm"][:, 0:1],
                                          1.0 / D, None, ALU.mult)),
               reads=["a_sm"], writes=["a_sm"], sig=SV)
            em("vector",
               (lambda e: e.scalar_tensor_tensor(ctx["a_sm"][:, 3:4], ctx["a_sm"][:, 1:2],
                                                 -1.0, ctx["a_sm"][:, 1:2], ALU.mult, ALU.mult)),
               reads=["a_sm"], writes=["a_sm"], sig=SV)
            em("vector",
               (lambda e: e.tensor_tensor(ctx["a_sm"][:, 4:5], ctx["a_sm"][:, 2:3],
                                          ctx["a_sm"][:, 3:4], ALU.add)),
               reads=["a_sm"], writes=["a_sm"], sig=SV)
            em("scalar",
               (lambda e: e.activation(ctx["a_stage"][:, 1:2], ctx["a_sm"][:, 4:5],
                                       ACT.Sqrt, bias=EPS)),
               reads=["a_sm"], writes=["a_stage"], sig=SS)
            em("vector",
               (lambda e: e.reciprocal(ctx["a_stage"][:, 2:3], ctx["a_stage"][:, 1:2])),
               reads=["a_stage"], writes=["a_stage"], sig=SV)
            em("vector",
               (lambda e: e.scalar_tensor_tensor(ctx["a_stage"][:, 0:1], ctx["a_sm"][:, 1:2],
                                                 -1.0, ctx["a_stage"][:, 2:3], ALU.mult, ALU.mult)),
               reads=["a_sm", "a_stage"], writes=["a_stage"], sig=SV)
            for k in range(3):
                em("tensor",
                   (lambda e, k=k: e.transpose(P[5].ap()[0:1, k * 128:(k + 1) * 128],
                                               ctx["a_stage"][:, k:k + 1], ctx["c_idf"][:])),
                   reads=["a_stage", "c_idf"], writes=["P5"], sig=ST if k == 2 else None)
            em("vector",
               (lambda e, c=c: e.tensor_copy(
                   ctx["a_rows_b"][0:1, :].rearrange("o (r s) -> o r s", r=2)[:, :, c * 128:(c + 1) * 128],
                   P[5].ap()[0:1, 0:256].rearrange("o (r s) -> o r s", r=2))),
               reads=["P5"], writes=["a_rows_b"], sig=SV)
            em("vector",
               (lambda e, c=c: e.tensor_copy(ctx["a_rows"][0:1, c * 128:(c + 1) * 128],
                                             P[5].ap()[0:1, 256:384])),
               reads=["P5"], writes=["a_rows"], sig=SV)
        em("tensor",
           (lambda e: e.matmul(P[5].ap()[:, 0:SEG], ctx["c_orf"][0:1, :],
                               ctx["a_rows"][0:1, 0:SEG], start=True, stop=True)),
           reads=["c_orf", "a_rows"], writes=["P5"], sig=ST)
        em("vector", (lambda e: e.tensor_copy(ctx["a_rrep"][:], P[5].ap()[:, 0:SEG])),
           reads=["P5"], writes=["a_rrep"], sig=SV)

        for cht in range(16):
            bk, bnm = P[cht % 2], PN[cht % 2]
            for kt in range(16):
                l = ctx["a_win"][:, (kt * 16 + cht) * 128:(kt * 16 + cht + 1) * 128]
                r = ctx["a_xtb"][:, kt * SEG:(kt + 1) * SEG]
                em("tensor",
                   (lambda e, o=bk.ap()[:, 0:SEG], l=l, r=r, st=(kt == 0):
                    e.matmul(o, l, r, start=st, stop=False)),
                   reads=(["a_win", "a_xtb"] if kt < 15 else ["a_win", "a_xtb"]),
                   writes=[bnm], sig=ST if kt == 15 else None)
            em("tensor",
               (lambda e, o=bk.ap()[:, 0:SEG], l=ctx["c_colsum"][0:1, cht * 128:(cht + 1) * 128],
                r=ctx["a_rows_b"][0:1, 0:SEG]: e.matmul(o, l, r, start=False, stop=False)),
               reads=["c_colsum", "a_rows_b"], writes=[bnm], sig=None)
            em("tensor",
               (lambda e, o=bk.ap()[:, 0:SEG], l=ctx["c_binrow"][0:1, cht * 128:(cht + 1) * 128],
                r=ctx["a_rows_b"][0:1, SEG:2 * SEG]: e.matmul(o, l, r, start=False, stop=True)),
               reads=["c_binrow", "a_rows_b"], writes=[bnm], sig=ST)
            em("vector",
               (lambda e, cht=cht, bk=bk, xhn=xhn: e.tensor_tensor(
                   ctx[xhn][:, cht * SEG:(cht + 1) * SEG], bk.ap()[:, 0:SEG],
                   ctx["a_rrep"][:], ALU.mult)),
               reads=[bnm, "a_rrep"], writes=[xhn], sig=SV)
            em("vector",
               (lambda e, cht=cht, xhn=xhn: e.tensor_copy(
                   ctx["a_xhb"][:, cht * SEG:(cht + 1) * SEG],
                   ctx[xhn][:, cht * SEG:(cht + 1) * SEG])),
               reads=[xhn], writes=["a_xhb"], sig=SV)

        for blk in range(8):
            for g in range(2):
                bk, bnm = P[2 + (blk * 2 + g) % 2], PN[2 + (blk * 2 + g) % 2]
                for kt2 in range(2):
                    l = ctx["a_wg"][:, ((g * 8 + blk) * 2 + kt2) * 128:
                                    ((g * 8 + blk) * 2 + kt2 + 1) * 128]
                    r = ctx["a_xhb"][:, (blk * 2 + kt2) * SEG:(blk * 2 + kt2 + 1) * SEG]
                    em("tensor",
                       (lambda e, o=bk.ap()[:, 0:SEG], l=l, r=r, st=(kt2 == 0), sp=(kt2 == 1):
                        e.matmul(o, l, r, start=st, stop=sp)),
                       reads=["a_wg", "a_xhb"], writes=[bnm], sig=ST if kt2 == 1 else None)
                dstn = "a_gx" if g == 0 else "a_ga"
                em("scalar",
                   (lambda e, bk=bk, dstn=dstn, blk=blk, g=g: e.activation(
                       ctx[dstn][:, blk * SEG:(blk + 1) * SEG], bk.ap()[:, 0:SEG],
                       ACT.Sigmoid, bias=ctx["c_bg"][:, g * 8 + blk:g * 8 + blk + 1])),
                   reads=[bnm, "c_bg"], writes=[dstn], sig=SS)

        c1b = ctx["c_c1"][:].unsqueeze(2).broadcast_to([128, 8, SUB])
        aimb = ctx["c_aim"][:].unsqueeze(2).broadcast_to([128, 8, SUB])
        for sub in range(NSUB):
            spp = (seg * NSUB + sub) % 2
            ts = slice(sub * SUB, (sub + 1) * SUB)
            gas = ctx["a_ga"][:].rearrange("p (g t) -> p g t", g=8)[:, :, ts]
            gxs = ctx["a_gx"][:].rearrange("p (g t) -> p g t", g=8)[:, :, ts]
            xh16 = ctx[xhn][:].rearrange("p (g t) -> p g t", g=16)
            xr = xh16[:, 0:8, ts]
            xi = xh16[:, 8:16, ts]
            hmn = f"a_hm{spp}"
            hm16 = ctx[hmn][:].rearrange("p (g t) -> p g t", g=16)

            def bb(i, spp=spp):
                return ctx[f"a_b{i}_{spp}"][:].rearrange("p (g t) -> p g t", g=8)

            def bf(i, spp=spp):
                return ctx[f"a_b{i}_{spp}"][:]

            def bn(i, spp=spp):
                return f"a_b{i}_{spp}"

            def V(o, fn, rd, wr):
                em("vector", fn, reads=rd, writes=wr, sig=SV)

            em("vector", (lambda e, o=bb(1), a=gas, b=c1b: e.tensor_tensor(o, a, b, ALU.mult)),
               reads=["a_ga", "c_c1"], writes=[bn(1)], sig=SV)
            em("vector", (lambda e, o=bb(2), a=gas, b=aimb: e.tensor_tensor(o, a, b, ALU.mult)),
               reads=["a_ga", "c_aim"], writes=[bn(2)], sig=SV)
            em("scalar", (lambda e, o=bf(3), i=bf(1): e.activation(o, i, ACT.Exp)),
               reads=[bn(1)], writes=[bn(3)], sig=SS)
            em("scalar", (lambda e, o=bf(4), i=bf(1): e.activation(o, i, ACT.Exp, scale=2.0)),
               reads=[bn(1)], writes=[bn(4)], sig=SS)
            em("scalar", (lambda e, o=bf(5), i=bf(2): e.activation(o, i, ACT.Sin, scale=2.0, bias=PI / 2)),
               reads=[bn(2)], writes=[bn(5)], sig=SS)
            em("scalar", (lambda e, o=bf(6), i=bf(2): e.activation(o, i, ACT.Sin, scale=2.0)),
               reads=[bn(2)], writes=[bn(6)], sig=SS)
            em("vector", (lambda e, o=bf(1), a=bf(4), b=bf(5): e.tensor_tensor(o, a, b, ALU.mult)),
               reads=[bn(4), bn(5)], writes=[bn(1)], sig=SV)
            em("vector", (lambda e, o=bf(5), a=bf(4), b=bf(6): e.tensor_tensor(o, a, b, ALU.mult)),
               reads=[bn(4), bn(6)], writes=[bn(5)], sig=SV)
            em("vector", (lambda e, o=bf(4), i=bf(1): e.tensor_scalar(o, i, -1.0, 1.0, ALU.mult, ALU.add)),
               reads=[bn(1)], writes=[bn(4)], sig=SV)
            em("scalar", (lambda e, o=bf(1), i=bf(4): e.activation(o, i, ACT.Square)),
               reads=[bn(4)], writes=[bn(1)], sig=SS)
            em("scalar", (lambda e, o=bf(6), i=bf(5): e.activation(o, i, ACT.Square)),
               reads=[bn(5)], writes=[bn(6)], sig=SS)
            em("vector", (lambda e, o=bf(6), a=bf(1), b=bf(6): e.tensor_tensor(o, a, b, ALU.add)),
               reads=[bn(1), bn(6)], writes=[bn(6)], sig=SV)
            em("scalar", (lambda e, o=bf(1), i=bf(6): e.activation(o, i, ACT.Sqrt)),
               reads=[bn(6)], writes=[bn(1)], sig=SS)
            em("vector", (lambda e, o=bf(6), a=bf(1), b=bf(4): e.tensor_tensor(o, a, b, ALU.add)),
               reads=[bn(1), bn(4)], writes=[bn(6)], sig=SV)
            em("scalar", (lambda e, o=bf(1), i=bf(6): e.activation(o, i, ACT.Sqrt, scale=0.5, bias=1e-24)),
               reads=[bn(6)], writes=[bn(1)], sig=SS)
            em("vector", (lambda e, o=bf(6), i=bf(1): e._custom_dve(
                RCF, out=o, in0=i, s0=RCF_C["s0"], s1=RCF_C["s1"], imm2=RCF_C["imm2"])),
               reads=[bn(1)], writes=[bn(6)], sig=SV)
            em("vector", (lambda e, o=bf(7), i=bf(5), b=bf(6): e.scalar_tensor_tensor(
                o, i, -0.5, b, ALU.mult, ALU.mult)),
               reads=[bn(5), bn(6)], writes=[bn(7)], sig=SV)
            for grp in range(8):
                em("vector",
                   (lambda e, grp=grp, spp=spp: e.tensor_tensor_scan(
                       ctx[f"a_b8_{spp}"][:, grp * SUB:(grp + 1) * SUB],
                       ctx[f"a_b2_{spp}"][:, grp * SUB:(grp + 1) * SUB],
                       ctx[f"a_b2_{spp}"][:, grp * SUB:(grp + 1) * SUB],
                       ctx["a_carry"][:, 16 + grp:17 + grp], ALU.add, ALU.bypass)),
                   reads=[bn(2), "a_carry"], writes=[bn(8)], sig=SV if grp == 7 else None)
            em("vector",
               (lambda e, spp=spp: e.tensor_copy(
                   ctx["a_carry"][:, 16:24],
                   ctx[f"a_b8_{spp}"][:].rearrange("p (g t) -> p g t", g=8)[:, :, SUB - 1:SUB].squeeze(2))),
               reads=[bn(8)], writes=["a_carry"], sig=SV)
            em("vector", (lambda e, o=bf(2), i=bf(8): e._custom_dve(
                WRAP_RNE, out=o, in0=i, s0=INV2PI, s1=WRAP_MAGIC, imm2=TWOPI)),
               reads=[bn(8)], writes=[bn(2)], sig=SV)
            em("vector", (lambda e, o=bf(4), i=bf(2): e.tensor_scalar(
                o, i, -PI_CLAMP, PI_CLAMP, ALU.max, ALU.min)),
               reads=[bn(2)], writes=[bn(4)], sig=SV)
            em("scalar", (lambda e, o=bf(2), i=bf(4): e.activation(o, i, ACT.Sin)),
               reads=[bn(4)], writes=[bn(2)], sig=SS)
            em("scalar", (lambda e, o=bf(5), i=bf(4): e.activation(o, i, ACT.Abs)),
               reads=[bn(4)], writes=[bn(5)], sig=SS)
            em("scalar", (lambda e, o=bf(4), i=bf(5): e.activation(o, i, ACT.Sin, scale=-1.0, bias=PI / 2)),
               reads=[bn(5)], writes=[bn(4)], sig=SS)
            em("vector", (lambda e, o=bf(5), a=bf(1), b=bf(4): e.tensor_tensor(o, a, b, ALU.mult)),
               reads=[bn(1), bn(4)], writes=[bn(5)], sig=SV)
            em("vector", (lambda e, o=bf(9), a=bf(7), b=bf(2): e.tensor_tensor(o, a, b, ALU.mult)),
               reads=[bn(7), bn(2)], writes=[bn(9)], sig=SV)
            em("vector", (lambda e, o=bf(9), a=bf(5), b=bf(9): e.tensor_tensor(o, a, b, ALU.add)),
               reads=[bn(5), bn(9)], writes=[bn(9)], sig=SV)
            em("vector", (lambda e, o=bb(5), a=gxs, b=bb(9): e.tensor_tensor(o, a, b, ALU.mult)),
               reads=["a_gx", bn(9)], writes=[bn(5)], sig=SV)
            em("vector", (lambda e, o=bf(9), a=bf(7), b=bf(4): e.tensor_tensor(o, a, b, ALU.mult)),
               reads=[bn(7), bn(4)], writes=[bn(9)], sig=SV)
            em("vector", (lambda e, o=bf(6), a=bf(1), b=bf(2): e.tensor_tensor(o, a, b, ALU.mult)),
               reads=[bn(1), bn(2)], writes=[bn(6)], sig=SV)
            em("vector", (lambda e, o=bf(9), a=bf(9), b=bf(6): e.tensor_tensor(o, a, b, ALU.subtract)),
               reads=[bn(9), bn(6)], writes=[bn(9)], sig=SV)
            em("vector", (lambda e, o=bb(6), a=gxs, b=bb(9): e.tensor_tensor(o, a, b, ALU.mult)),
               reads=["a_gx", bn(9)], writes=[bn(6)], sig=SV)
            em("gpsimd", (lambda e, o=bb(1), a=xr, b=bb(5): e.tensor_tensor(o, a, b, ALU.mult)),
               reads=[xhn, bn(5)], writes=[bn(1)], sig=SG)
            em("gpsimd", (lambda e, o=bb(9), a=xi, b=bb(6): e.tensor_tensor(o, a, b, ALU.mult)),
               reads=[xhn, bn(6)], writes=[bn(9)], sig=SG)
            em("gpsimd", (lambda e, o=bf(4), a=bf(1), b=bf(9): e.tensor_tensor(o, a, b, ALU.subtract)),
               reads=[bn(1), bn(9)], writes=[bn(4)], sig=SG)
            em("gpsimd", (lambda e, o=bb(1), a=xr, b=bb(6): e.tensor_tensor(o, a, b, ALU.mult)),
               reads=[xhn, bn(6)], writes=[bn(1)], sig=SG)
            em("gpsimd", (lambda e, o=bb(9), a=xi, b=bb(5): e.tensor_tensor(o, a, b, ALU.mult)),
               reads=[xhn, bn(5)], writes=[bn(9)], sig=SG)
            em("gpsimd", (lambda e, o=bf(2), a=bf(1), b=bf(9): e.tensor_tensor(o, a, b, ALU.add)),
               reads=[bn(1), bn(9)], writes=[bn(2)], sig=SG)
            for grp in range(8):
                em("vector",
                   (lambda e, grp=grp, spp=spp: e.tensor_tensor_scan(
                       ctx[f"a_b5_{spp}"][:, grp * SUB:(grp + 1) * SUB],
                       ctx[f"a_b3_{spp}"][:, grp * SUB:(grp + 1) * SUB],
                       ctx[f"a_b4_{spp}"][:, grp * SUB:(grp + 1) * SUB],
                       ctx["a_carry"][:, grp:grp + 1], ALU.mult, ALU.add)),
                   reads=[bn(3), bn(4), "a_carry"], writes=[bn(5)], sig=SV if grp == 7 else None)
            for grp in range(8):
                em("vector",
                   (lambda e, grp=grp, spp=spp: e.tensor_tensor_scan(
                       ctx[f"a_b6_{spp}"][:, grp * SUB:(grp + 1) * SUB],
                       ctx[f"a_b3_{spp}"][:, grp * SUB:(grp + 1) * SUB],
                       ctx[f"a_b2_{spp}"][:, grp * SUB:(grp + 1) * SUB],
                       ctx["a_carry"][:, 8 + grp:9 + grp], ALU.mult, ALU.add)),
                   reads=[bn(3), bn(2), "a_carry"], writes=[bn(6)], sig=SV if grp == 7 else None)
            em("vector",
               (lambda e, spp=spp: e.tensor_copy(
                   ctx["a_carry"][:, 0:8],
                   ctx[f"a_b5_{spp}"][:].rearrange("p (g t) -> p g t", g=8)[:, :, SUB - 1:SUB].squeeze(2))),
               reads=[bn(5)], writes=["a_carry"], sig=SV)
            em("vector",
               (lambda e, spp=spp: e.tensor_copy(
                   ctx["a_carry"][:, 8:16],
                   ctx[f"a_b6_{spp}"][:].rearrange("p (g t) -> p g t", g=8)[:, :, SUB - 1:SUB].squeeze(2))),
               reads=[bn(6)], writes=["a_carry"], sig=SV)
            em("vector", (lambda e, o=bf(1), i=bf(8): e._custom_dve(
                WRAP_RNE, out=o, in0=i, s0=INV2PI, s1=WRAP_MAGIC, imm2=TWOPI)),
               reads=[bn(8)], writes=[bn(1)], sig=SV)
            em("vector", (lambda e, o=bf(4), i=bf(1): e.tensor_scalar(
                o, i, -PI_CLAMP, PI_CLAMP, ALU.max, ALU.min)),
               reads=[bn(1)], writes=[bn(4)], sig=SV)
            em("scalar", (lambda e, o=bf(1), i=bf(4): e.activation(o, i, ACT.Sin)),
               reads=[bn(4)], writes=[bn(1)], sig=SS)
            em("scalar", (lambda e, o=bf(7), i=bf(4): e.activation(o, i, ACT.Abs)),
               reads=[bn(4)], writes=[bn(7)], sig=SS)
            em("scalar", (lambda e, o=bf(4), i=bf(7): e.activation(o, i, ACT.Sin, scale=-1.0, bias=PI / 2)),
               reads=[bn(7)], writes=[bn(4)], sig=SS)
            em("gpsimd", (lambda e, o=bf(7), a=bf(5), b=bf(4): e.tensor_tensor(o, a, b, ALU.mult)),
               reads=[bn(5), bn(4)], writes=[bn(7)], sig=SG)
            em("gpsimd", (lambda e, o=bf(9), a=bf(6), b=bf(1): e.tensor_tensor(o, a, b, ALU.mult)),
               reads=[bn(6), bn(1)], writes=[bn(9)], sig=SG)
            em("gpsimd", (lambda e, o=hm16[:, 0:8, :], a=bb(7), b=bb(9):
                          e.tensor_tensor(o, a, b, ALU.subtract)),
               reads=[bn(7), bn(9)], writes=[hmn], sig=SG)
            em("gpsimd", (lambda e, o=bf(7), a=bf(5), b=bf(1): e.tensor_tensor(o, a, b, ALU.mult)),
               reads=[bn(5), bn(1)], writes=[bn(7)], sig=SG)
            em("gpsimd", (lambda e, o=bf(9), a=bf(6), b=bf(4): e.tensor_tensor(o, a, b, ALU.mult)),
               reads=[bn(6), bn(4)], writes=[bn(9)], sig=SG)
            em("gpsimd", (lambda e, o=hm16[:, 8:16, :], a=bb(7), b=bb(9):
                          e.tensor_tensor(o, a, b, ALU.add)),
               reads=[bn(7), bn(9)], writes=[hmn], sig=SG)
            em("sync",
               (lambda e, ts0=t0 + sub * SUB, hmn=hmn: e.dma_start(
                   out=hm_sp.rearrange("(kt p) t -> p kt t", p=128)[:, :, ts0:ts0 + SUB],
                   in_=ctx[hmn][:].rearrange("p (kt t) -> p kt t", kt=16))),
               reads=[hmn], writes=[f"hm_sp{seg}"], sig="dh", dma=True,
               cost=dmac(128 * 16 * SUB * 2))

    p.barrier()
    free_group(A_GROUP)
    B_GROUP = ["b_wout", "b_hmt0", "b_hmt1", "b_y0", "b_y1", "b_xnat", "b_x1", "b_xn",
               "b_qx", "b_dqx", "b_scr0", "b_am", "b_s", "b_is",
               "b_sm0", "b_sm1", "c_boutrow", "c_orb"]
    sb("c_xqt", [128, 16 * 2048], BF16)
    sb("b_wout", [128, 16 * 2048], BF16)
    sb("b_hmt0", [128, 16 * 128], BF16)
    sb("b_hmt1", [128, 16 * 128], BF16)
    sb("b_y0", [128, 2048], F32)
    sb("b_y1", [128, 2048], F32)
    sb("b_xnat", [128, 2048], F32)
    sb("b_x1", [128, 2048], F32)
    sb("b_xn", [128, 2048], F32)
    sb("b_qx", [128, 2048], BF16)
    sb("b_dqx", [128, 2048], BF16)
    sb("b_scr0", [128, 2048], F32)
    sb("b_am", [128, 64], F32)
    sb("b_s", [128, 64], F32)
    sb("b_is", [128, 64], F32)
    sb("b_sm0", [128, 16], F32)
    sb("b_sm1", [128, 16], F32)
    sb("c_boutrow", [1, D], BF16)
    sb("c_orb", [1, 128], BF16)
    em("gpsimd", lambda e: e.dma_start(out=ctx["c_boutrow"][:], in_=b_out_row),
       writes=["c_boutrow"], sig="dg", dma=True)
    em("sync", lambda e: e.dma_start(out=ctx["c_orb"][:], in_=ones_row_b),
       writes=["c_orb"], sig="dh", dma=True)
    em("gpsimd", lambda e: e.dma_start(out=ctx["b_wout"][:], in_=w_out),
       writes=["b_wout"], sig="dg", dma=True, cost=dmac(128 * 16 * 2048 * 4))

    for tt in range(NTT):
        hmt = "b_hmt0" if tt % 2 == 0 else "b_hmt1"
        byn = f"b_y{tt % 2}"
        bscrn = "b_scr0"
        bsmn = f"b_sm{tt % 2}"
        em("sync",
           (lambda e, tt=tt, hmt=hmt: e.dma_start(
               out=ctx[hmt][:].rearrange("p (kt t) -> p kt t", kt=16),
               in_=hm_sp.rearrange("(kt p) t -> p kt t", p=128)[:, :, tt * 128:(tt + 1) * 128])),
           reads=[f"hm_sp{tt // 4}"], writes=[hmt], sig="dh", dma=True,
           cost=dmac(128 * 16 * 128 * 2))
        em("gpsimd",
           (lambda e, tt=tt: e.dma_start(out=ctx["b_xnat"][:],
                                         in_=x_nat[tt * 128:(tt + 1) * 128, :])),
           writes=["b_xnat"], sig="dg", dma=True, cost=dmac(128 * 2048 * 4))
        for oc in range(4):
            bk, bnm = P[oc], PN[oc]
            for kt in range(16):
                em("tensor",
                   (lambda e, o=bk.ap()[:, 0:512],
                    l=ctx[hmt][:, kt * 128:(kt + 1) * 128],
                    r=ctx["b_wout"][:, kt * 2048 + oc * 512: kt * 2048 + (oc + 1) * 512],
                    st=(kt == 0): e.matmul(o, l, r, start=st, stop=False)),
                   reads=[hmt, "b_wout"], writes=[bnm], sig=ST if kt == 15 else None)
            em("tensor",
               (lambda e, o=bk.ap()[:, 0:512], l=ctx["c_orb"][0:1, :],
                r=ctx["c_boutrow"][0:1, oc * 512:(oc + 1) * 512]:
                e.matmul(o, l, r, start=False, stop=True)),
               reads=["c_orb", "c_boutrow"], writes=[bnm], sig=ST)
            em("scalar",
               (lambda e, oc=oc, bk=bk, byn=byn: e.activation(
                   ctx[byn][:, oc * 512:(oc + 1) * 512], bk.ap()[:, 0:512], ACT.Copy)),
               reads=[bnm], writes=[byn], sig=SS)

        def ln_stats(src, smn, c_ss, c_s, scrn=None):
            scrn = scrn or bscrn
            em("vector",
               (lambda e, scrn=scrn: e.scalar_tensor_tensor(ctx[scrn][:], ctx[src][:], 1.0,
                                                 ctx[src][:], ALU.mult, ALU.mult,
                                                 accum_out=ctx[smn][:, c_ss:c_ss + 1])),
               reads=[src], writes=[scrn, smn], sig=SV, cost=dvec(2048))
            em("vector",
               (lambda e, scrn=scrn: e.tensor_scalar(ctx[scrn][:], ctx[src][:], 1.0, None,
                                          ALU.mult, ALU.add,
                                          accum_out=ctx[smn][:, c_s:c_s + 1])),
               reads=[src], writes=[scrn, smn], sig=SV, cost=dvec(2048))

        def ln_smalls(smn, c_ss, c_s, c_mean, c_var, c_sd, c_r, c_mrn, n_elem, eng_sm="vector"):
            em(eng_sm,
               (lambda e: e.tensor_scalar(ctx[smn][:, c_mean:c_mean + 1],
                                          ctx[smn][:, c_s:c_s + 1], 1.0 / n_elem, None, ALU.mult)),
               reads=[smn], writes=[smn], sig=SV)
            em(eng_sm,
               (lambda e: e.tensor_scalar(ctx[smn][:, c_var:c_var + 1],
                                          ctx[smn][:, c_ss:c_ss + 1], 1.0 / n_elem, None, ALU.mult)),
               reads=[smn], writes=[smn], sig=SV)
            em(eng_sm,
               (lambda e: e.scalar_tensor_tensor(ctx[smn][:, c_sd:c_sd + 1],
                                                 ctx[smn][:, c_mean:c_mean + 1], -1.0,
                                                 ctx[smn][:, c_mean:c_mean + 1], ALU.mult, ALU.mult)),
               reads=[smn], writes=[smn], sig=SV)
            em(eng_sm,
               (lambda e: e.tensor_tensor(ctx[smn][:, c_var:c_var + 1],
                                          ctx[smn][:, c_var:c_var + 1],
                                          ctx[smn][:, c_sd:c_sd + 1], ALU.add)),
               reads=[smn], writes=[smn], sig=SV)
            em("scalar",
               (lambda e: e.activation(ctx[smn][:, c_sd:c_sd + 1], ctx[smn][:, c_var:c_var + 1],
                                       ACT.Sqrt, bias=EPS)),
               reads=[smn], writes=[smn], sig=SS)
            em(eng_sm,
               (lambda e: e.reciprocal(ctx[smn][:, c_r:c_r + 1], ctx[smn][:, c_sd:c_sd + 1])),
               reads=[smn], writes=[smn], sig=SV)
            em(eng_sm,
               (lambda e: e.scalar_tensor_tensor(ctx[smn][:, c_mrn:c_mrn + 1],
                                                 ctx[smn][:, c_mean:c_mean + 1], -1.0,
                                                 ctx[smn][:, c_r:c_r + 1], ALU.mult, ALU.mult)),
               reads=[smn], writes=[smn], sig=SV)

        ln_stats(byn, bsmn, 0, 1)
        ln_smalls(bsmn, 0, 1, 2, 3, 4, 5, 6, D)
        em("scalar",
           (lambda e, byn=byn, bsmn=bsmn, bscrn=bscrn: e.activation(
               ctx[bscrn][:], ctx[byn][:], ACT.Identity,
               scale=ctx[bsmn][:, 5:6], bias=ctx[bsmn][:, 6:7])),
           reads=[byn, bsmn], writes=[bscrn], sig=SS, cost=actc(2048))
        em("gpsimd",
           (lambda e, bscrn=bscrn: e.tensor_tensor(ctx["b_x1"][:], ctx[bscrn][:],
                                                   ctx["b_xnat"][:], ALU.add)),
           reads=[bscrn, "b_xnat"], writes=["b_x1"], sig=SG, cost=gpsc(2048))
        em("sync",
           (lambda e, tt=tt: e.dma_start(out=x1_sp[tt * 128:(tt + 1) * 128, :], in_=ctx["b_x1"][:])),
           reads=["b_x1"], writes=[f"x1_sp{tt}"], sig="dh", dma=True,
           cost=dmac(128 * 2048 * 4))
        ln_stats("b_x1", bsmn, 8, 9)
        ln_smalls(bsmn, 8, 9, 10, 11, 12, 13, 14, D)
        em("scalar",
           (lambda e, bsmn=bsmn: e.activation(ctx["b_xn"][:], ctx["b_x1"][:], ACT.Identity,
                                   scale=ctx[bsmn][:, 13:14], bias=ctx[bsmn][:, 14:15])),
           reads=["b_x1", bsmn], writes=["b_xn"], sig=SS, cost=actc(2048))
        em("vector",
           (lambda e: e.tensor_reduce(ctx["b_am"][:],
                                      ctx["b_xn"][:].rearrange("p (nb k) -> p nb k", k=32),
                                      AXX, ALU.max, apply_absolute_value=True)),
           reads=["b_xn"], writes=["b_am"], sig=SV, cost=dvec(2048))
        em("vector",
           (lambda e: e.tensor_scalar(ctx["b_s"][:], ctx["b_am"][:], 1.0 / 448.0, 1e-30,
                                      ALU.mult, ALU.max)),
           reads=["b_am"], writes=["b_s"], sig=SV)
        em("vector", (lambda e: e.reciprocal(ctx["b_is"][:], ctx["b_s"][:])),
           reads=["b_s"], writes=["b_is"], sig=SV)
        em("vector",
           (lambda e: e._custom_dve(
               FP8_RNE_MUL,
               out=ctx["b_qx"][:].rearrange("p (nb k) -> p nb k", k=32),
               in0=ctx["b_is"][:].unsqueeze(2).broadcast_to([128, 64, 32]),
               in1=ctx["b_xn"][:].rearrange("p (nb k) -> p nb k", k=32),
               s0=FP8_NOTMASK, s1=FP8_C1, imm2=FP8_C2)),
           reads=["b_is", "b_xn"], writes=["b_qx"], sig=SV, cost=dvec(2048))
        em("vector",
           (lambda e: e.tensor_tensor(
               ctx["b_dqx"][:].rearrange("p (nb k) -> p nb k", k=32),
               ctx["b_qx"][:].rearrange("p (nb k) -> p nb k", k=32),
               ctx["b_s"][:].unsqueeze(2).broadcast_to([128, 64, 32]), ALU.mult)),
           reads=["b_qx", "b_s"], writes=["b_dqx"], sig=SV, cost=dvec(2048))
        em("sync",
           (lambda e, tt=tt: e.dma_start(out=xq_sp[tt * 128:(tt + 1) * 128, :], in_=ctx["b_dqx"][:])),
           reads=["b_dqx"], writes=[f"xq_sp{tt}"], sig="dh", dma=True,
           cost=dmac(128 * 2048 * 2))
        for kt in range(16):
            qeng, qsig = ("sync", "dh") if kt % 2 == 0 else ("scalar", "ds")
            em(qeng,
               (lambda e, tt=tt, kt=kt: e.dma_start(
                   out=ctx["c_xqt"][:, kt * 2048 + tt * 128: kt * 2048 + (tt + 1) * 128],
                   in_=xq_sp[tt * 128:(tt + 1) * 128, kt * 128:(kt + 1) * 128],
                   transpose=True)),
               reads=[f"xq_sp{tt}"], writes=["c_xqt"], sig=qsig, dma=True,
               cost=1300)

    p.barrier()
    free_group(B_GROUP)
    C_GROUP = ["c_xqt", "c_rhs_g0", "c_rhs_g1", "c_rhs_u0", "c_rhs_u1",
               "c_sg0", "c_sg1", "c_tg0", "c_tg1", "c_hs0", "c_hs1",
               "c_qh0", "c_qh1", "c_dqh0", "c_dqh1", "c_dqht0", "c_dqht1",
               "c_ham", "c_hss", "c_his"]
    for par in range(2):
        sb(f"c_rhs_g{par}", [128, 16 * 512], BF16)
        sb(f"c_rhs_u{par}", [128, 16 * 512], BF16)
        sb(f"c_sg{par}", [128, 512], F32)
        sb(f"c_tg{par}", [128, 512], F32)
        sb(f"c_hs{par}", [128, 512], F32)
        sb(f"c_qh{par}", [128, 512], BF16)
        sb(f"c_dqh{par}", [128, 512], BF16)
        sb(f"c_dqht{par}", [128, 512], BF16)
    sb("c_ham", [128, 16], F32)
    sb("c_hss", [128, 16], F32)
    sb("c_his", [128, 16], F32)
    PB = [PB6, PB7]
    PBN = ["PB6", "PB7"]
    def quantize(src, n_free, nblk, amn, sn, isn, qn, dqn, eng_dq):
        em("vector",
           (lambda e: e.tensor_reduce(ctx[amn][:, 0:nblk],
                                      ctx[src][:, 0:n_free].rearrange("p (nb k) -> p nb k", k=32),
                                      AXX, ALU.max, apply_absolute_value=True)),
           reads=[src], writes=[amn], sig=SV, cost=dvec(n_free))
        em("vector",
           (lambda e: e.tensor_scalar(ctx[sn][:, 0:nblk], ctx[amn][:, 0:nblk],
                                      1.0 / 448.0, 1e-30, ALU.mult, ALU.max)),
           reads=[amn], writes=[sn], sig=SV)
        em("vector", (lambda e: e.reciprocal(ctx[isn][:, 0:nblk], ctx[sn][:, 0:nblk])),
           reads=[sn], writes=[isn], sig=SV)
        em("vector",
           (lambda e: e._custom_dve(
               FP8_RNE_MUL,
               out=ctx[qn][:, 0:n_free].rearrange("p (nb k) -> p nb k", k=32),
               in0=ctx[isn][:, 0:nblk].unsqueeze(2).broadcast_to([128, nblk, 32]),
               in1=ctx[src][:, 0:n_free].rearrange("p (nb k) -> p nb k", k=32),
               s0=FP8_NOTMASK, s1=FP8_C1, imm2=FP8_C2)),
           reads=[isn, src], writes=[qn], sig=SV, cost=dvec(n_free))
        em(eng_dq,
           (lambda e: e.tensor_tensor(
               ctx[dqn][:, 0:n_free].rearrange("p (nb k) -> p nb k", k=32),
               ctx[qn][:, 0:n_free].rearrange("p (nb k) -> p nb k", k=32),
               ctx[sn][:, 0:nblk].unsqueeze(2).broadcast_to([128, nblk, 32]), ALU.mult)),
           reads=[qn, sn], writes=[dqn], sig=SG if eng_dq == "gpsimd" else SV,
           cost=gpsc(n_free) if eng_dq == "gpsimd" else dvec(n_free))

    for hch in range(NHCH):
        par = hch % 2
        rg, ru = f"c_rhs_g{par}", f"c_rhs_u{par}"
        for (sp_t, rhsn) in ((gwq_sp, rg), (uwq_sp, ru)):
            em("gpsimd",
               (lambda e, sp_t=sp_t, rhsn=rhsn, hch=hch: e.dma_start(
                   out=ctx[rhsn][:].rearrange("p (kt n) -> p kt n", kt=16),
                   in_=sp_t.rearrange("(kt p) n -> p kt n", p=128)[
                       :, :, hch * 512:(hch + 1) * 512])),
               reads=[["gwq_sp", "uwq_sp"][0 if sp_t is gwq_sp else 1]],
               writes=[rhsn], sig="dg", dma=True, cost=dmac(128 * 16 * 512 * 2))
        for tt in range(NTT):
            tpar = tt % 2
            bg_, bgn = P[tpar * 2], PN[tpar * 2]
            bu_, bun = P[tpar * 2 + 1], PN[tpar * 2 + 1]
            sgn, tgn, hsn = f"c_sg{tpar}", f"c_tg{tpar}", f"c_hs{tpar}"
            qhn, dqhn, dqhtn = f"c_qh{tpar}", f"c_dqh{tpar}", f"c_dqht{tpar}"
            for kt in range(16):
                l = ctx["c_xqt"][:, kt * 2048 + tt * 128: kt * 2048 + (tt + 1) * 128]
                em("tensor",
                   (lambda e, o=bg_.ap()[:, 0:512], l=l,
                    r=ctx[rg][:, kt * 512:(kt + 1) * 512], st=(kt == 0), sp=(kt == 15):
                    e.matmul(o, l, r, start=st, stop=sp)),
                   reads=["c_xqt", rg], writes=[bgn], sig=ST if kt == 15 else None)
            for kt in range(16):
                l = ctx["c_xqt"][:, kt * 2048 + tt * 128: kt * 2048 + (tt + 1) * 128]
                em("tensor",
                   (lambda e, o=bu_.ap()[:, 0:512], l=l,
                    r=ctx[ru][:, kt * 512:(kt + 1) * 512], st=(kt == 0), sp=(kt == 15):
                    e.matmul(o, l, r, start=st, stop=sp)),
                   reads=["c_xqt", ru], writes=[bun], sig=ST if kt == 15 else None)
            em("scalar",
               (lambda e, bg_=bg_, sgn=sgn: e.activation(ctx[sgn][:], bg_.ap()[:, 0:512], ACT.Sigmoid)),
               reads=[bgn], writes=[sgn], sig=SS)
            em("vector",
               (lambda e, bg_=bg_, sgn=sgn, tgn=tgn: e.tensor_tensor(
                   ctx[tgn][:], ctx[sgn][:], bg_.ap()[:, 0:512], ALU.mult)),
               reads=[sgn, bgn], writes=[tgn], sig=SV, cost=dvec(512, 2.2))
            em("vector",
               (lambda e, bu_=bu_, tgn=tgn, hsn=hsn: e.tensor_tensor(
                   ctx[hsn][:], ctx[tgn][:], bu_.ap()[:, 0:512], ALU.mult)),
               reads=[tgn, bun], writes=[hsn], sig=SV, cost=dvec(512, 2.2))
            quantize(hsn, 512, 16, "c_ham", "c_hss", "c_his", qhn, dqhn, "gpsimd")
            bk, bnm = PB[tpar], PBN[tpar]
            for k4 in range(4):
                em("tensor",
                   (lambda e, o=bk.ap()[:, k4 * 128:(k4 + 1) * 128],
                    i_=ctx[dqhn][:, k4 * 128:(k4 + 1) * 128]:
                    e.transpose(o, i_, ctx["c_idb"][:])),
                   reads=[dqhn, "c_idb"], writes=[bnm], sig=ST if k4 == 3 else None)
            em("scalar",
               (lambda e, bk=bk, dqhtn=dqhtn: e.activation(ctx[dqhtn][:], bk.ap()[:, 0:512], ACT.Copy)),
               reads=[bnm], writes=[dqhtn], sig=SS)
            em("sync",
               (lambda e, hch=hch, tt=tt, dqhtn=dqhtn: e.dma_start(
                   out=hq_sp.rearrange("(r p) t -> p r t", p=128)[
                       :, hch * 4:(hch + 1) * 4, tt * 128:(tt + 1) * 128],
                   in_=ctx[dqhtn][:].rearrange("p (r t) -> p r t", r=4))),
               reads=[dqhtn], writes=["hq_sp"], sig="dh", dma=True,
               cost=dmac(128 * 512 * 2))

    p.barrier()
    free_group(C_GROUP)
    psum_free(["PB7", "PB6"])
    Q6 = psum("Q6")
    Q7 = psum("Q7")
    DP = P + [Q6, Q7]
    DPN = PN + ["Q6", "Q7"]
    sb("d_hq", [128, 64 * QT], BF16)
    sb("d_rhs0", [128, 8 * 512], BF16)
    sb("d_rhs1", [128, 8 * 512], BF16)
    for i in range(4):
        sb(f"d_y{i}", [128, 2048], F32)
    sb("d_xn1", [128, 2048], F32)
    sb("d_fin", [128, 2048], F32)
    sb("d_sm", [128, 16], F32)
    HQ_CH_Q = (("gpsimd", "dg"), ("sync", "dh"), ("scalar", "ds"), ("gpsimd", "dg"))
    for q in range(NQT):
        for ch in range(4):
            eng_c, sig_c = HQ_CH_Q[ch]
            em(eng_c,
               (lambda e, q=q, ch=ch: e.dma_start(
                   out=ctx["d_hq"][:, ch * 16 * QT:(ch + 1) * 16 * QT]
                       .rearrange("p (kt t) -> p kt t", kt=16),
                   in_=hq_sp.rearrange("(kt p) t -> p kt t", p=128)[
                       :, ch * 16:(ch + 1) * 16, q * QT:(q + 1) * QT])),
               reads=["hq_sp"], writes=[f"d_hqc{ch}"], sig=sig_c, dma=True,
               cost=dmac(128 * 16 * QT * 2))
        for oc in range(4):
            pb = 4 * (oc % 2)
            for kg in range(8):
                rhs = "d_rhs0" if kg % 2 == 0 else "d_rhs1"
                em("sync",
                   (lambda e, kg=kg, oc=oc, rhs=rhs: e.dma_start(
                       out=ctx[rhs][:].rearrange("p (k n) -> p k n", k=8),
                       in_=dwq_sp.rearrange("(k p) n -> p k n", p=128)[
                           :, kg * 8:(kg + 1) * 8, oc * 512:(oc + 1) * 512])),
                   reads=["dwq_sp"], writes=[rhs], sig="dh", dma=True,
                   cost=dmac(128 * 8 * 512 * 2))
                for k8 in range(8):
                    kt = kg * 8 + k8
                    for t4 in range(4):
                        bk, bnm = DP[pb + t4], DPN[pb + t4]
                        em("tensor",
                           (lambda e, o=bk.ap()[:, 0:512],
                            l=ctx["d_hq"][:, kt * QT + t4 * 128: kt * QT + (t4 + 1) * 128],
                            r=ctx[rhs][:, k8 * 512:(k8 + 1) * 512],
                            st=(kt == 0), sp=(kt == 63): e.matmul(o, l, r, start=st, stop=sp)),
                           reads=[f"d_hqc{kt // 16}", rhs], writes=[bnm],
                           sig=ST if (kt % 16 == 15 or t4 == 3) else None)
            for t4 in range(4):
                bk, bnm = DP[pb + t4], DPN[pb + t4]
                em("scalar",
                   (lambda e, t4=t4, oc=oc, bk=bk: e.activation(
                       ctx[f"d_y{t4}"][:, oc * 512:(oc + 1) * 512], bk.ap()[:, 0:512], ACT.Copy)),
                   reads=[bnm], writes=[f"d_y{t4}"], sig=SS)
        for t4 in range(4):
            tt = q * 4 + t4
            yn = f"d_y{t4}"
            em("vector",
               (lambda e, yn=yn: e.scalar_tensor_tensor(ctx["d_fin"][:], ctx[yn][:], 1.0,
                                                        ctx[yn][:], ALU.mult, ALU.mult,
                                                        accum_out=ctx["d_sm"][:, 0:1])),
               reads=[yn], writes=["d_fin", "d_sm"], sig=SV, cost=dvec(2048))
            em("vector",
               (lambda e, yn=yn: e.tensor_scalar(ctx["d_fin"][:], ctx[yn][:], 1.0, None,
                                                 ALU.mult, ALU.add,
                                                 accum_out=ctx["d_sm"][:, 1:2])),
               reads=[yn], writes=["d_fin", "d_sm"], sig=SV, cost=dvec(2048))
            em("vector",
               (lambda e: e.tensor_scalar(ctx["d_sm"][:, 2:3], ctx["d_sm"][:, 1:2],
                                          1.0 / D, None, ALU.mult)),
               reads=["d_sm"], writes=["d_sm"], sig=SV)
            em("vector",
               (lambda e: e.tensor_scalar(ctx["d_sm"][:, 3:4], ctx["d_sm"][:, 0:1],
                                          1.0 / D, None, ALU.mult)),
               reads=["d_sm"], writes=["d_sm"], sig=SV)
            em("vector",
               (lambda e: e.scalar_tensor_tensor(ctx["d_sm"][:, 4:5], ctx["d_sm"][:, 2:3],
                                                 -1.0, ctx["d_sm"][:, 2:3], ALU.mult, ALU.mult)),
               reads=["d_sm"], writes=["d_sm"], sig=SV)
            em("vector",
               (lambda e: e.tensor_tensor(ctx["d_sm"][:, 3:4], ctx["d_sm"][:, 3:4],
                                          ctx["d_sm"][:, 4:5], ALU.add)),
               reads=["d_sm"], writes=["d_sm"], sig=SV)
            em("scalar",
               (lambda e: e.activation(ctx["d_sm"][:, 5:6], ctx["d_sm"][:, 3:4], ACT.Sqrt, bias=EPS)),
               reads=["d_sm"], writes=["d_sm"], sig=SS)
            em("vector",
               (lambda e: e.reciprocal(ctx["d_sm"][:, 6:7], ctx["d_sm"][:, 5:6])),
               reads=["d_sm"], writes=["d_sm"], sig=SV)
            em("vector",
               (lambda e: e.scalar_tensor_tensor(ctx["d_sm"][:, 7:8], ctx["d_sm"][:, 2:3],
                                                 -1.0, ctx["d_sm"][:, 6:7], ALU.mult, ALU.mult)),
               reads=["d_sm"], writes=["d_sm"], sig=SV)
            em("gpsimd",
               (lambda e, tt=tt: e.dma_start(out=ctx["d_xn1"][:],
                                             in_=x1_sp[tt * 128:(tt + 1) * 128, :])),
               reads=[f"x1_sp{tt}"], writes=["d_xn1"], sig="dg", dma=True,
               cost=dmac(128 * 2048 * 4))
            em("scalar",
               (lambda e, yn=yn: e.activation(ctx["d_fin"][:], ctx[yn][:], ACT.Identity,
                                              scale=ctx["d_sm"][:, 6:7], bias=ctx["d_sm"][:, 7:8])),
               reads=[yn, "d_sm"], writes=["d_fin"], sig=SS, cost=actc(2048))
            em("vector",
               (lambda e: e.tensor_tensor(ctx["d_fin"][:], ctx["d_fin"][:], ctx["d_xn1"][:], ALU.add)),
               reads=["d_fin", "d_xn1"], writes=["d_fin"], sig=SV, cost=dvec(2048))
            em("sync",
               (lambda e, tt=tt: e.dma_start(out=out_hbm[tt * 128:(tt + 1) * 128, :],
                                             in_=ctx["d_fin"][:])),
               reads=["d_fin"], sig="dh", dma=True, cost=dmac(128 * 2048 * 4))

    sched = p.schedule()
    final_waits = [(s, v) for (s, v) in p.final_cnt.items()
                   if s.startswith(("dh", "dg", "ds"))]

    sems = {}
    for name in sorted(p.final_cnt):
        sems[name] = nc.semaphore(name).__enter__()

    with nc.Block() as block:
        def run_engine(eng_obj, name):
            emitted = {}
            for (fn, waits, sig) in sched[name]:
                for (sname, val) in waits:
                    if val > emitted.get(sname, -1):
                        eng_obj.wait_ge(sems[sname], val)
                        emitted[sname] = val
                ins = fn(eng_obj)
                if sig is not None:
                    sname, amt = sig
                    ins.then_inc(sems[sname], amt)
            if name == "sync":
                for (sname, val) in final_waits:
                    eng_obj.wait_ge(sems[sname], val)

        @block.sync
        def _(e):
            run_engine(e, "sync")

        @block.scalar
        def _(e):
            run_engine(e, "scalar")

        @block.tensor
        def _(e):
            run_engine(e, "tensor")

        @block.vector
        def _(e):
            run_engine(e, "vector")

        @block.gpsimd
        def _(e):
            run_engine(e, "gpsimd")

    lower_extended_insts(nc)
    return nc


_NC_CACHE = [None]


def kernel(L0_attn_x, L0_attn_W_in, L0_attn_b_in, L0_attn_W_gx, L0_attn_b_gx,
           L0_attn_W_ga, L0_attn_b_ga, L0_attn_a_real, L0_attn_a_imag,
           L0_attn_W_out, L0_attn_b_out, L0_ffn_gate_w, L0_ffn_up_w, L0_ffn_down_w):
    x = np.ascontiguousarray(np.asarray(L0_attn_x, dtype=np.float32))
    W_in = np.asarray(L0_attn_W_in, dtype=np.float32)
    W_out = np.asarray(L0_attn_W_out, dtype=np.float32)
    W_gx = np.asarray(L0_attn_W_gx, dtype=np.float32)
    W_ga = np.asarray(L0_attn_W_ga, dtype=np.float32)
    b_gx = np.asarray(L0_attn_b_gx, dtype=np.float32)
    b_ga = np.asarray(L0_attn_b_ga, dtype=np.float32)
    b_in = np.asarray(L0_attn_b_in, dtype=np.float32)
    b_out = np.asarray(L0_attn_b_out, dtype=np.float32)
    a_real = np.asarray(L0_attn_a_real, dtype=np.float32)
    a_imag = np.asarray(L0_attn_a_imag, dtype=np.float32)
    gate_w = np.ascontiguousarray(np.asarray(L0_ffn_gate_w, dtype=np.float32))
    up_w = np.ascontiguousarray(np.asarray(L0_ffn_up_w, dtype=np.float32))
    down_w = np.ascontiguousarray(np.asarray(L0_ffn_down_w, dtype=np.float32))

    w_in_l = np.ascontiguousarray(
        W_in.reshape(16, 128, 16, 128).transpose(1, 0, 2, 3).reshape(128, 16 * 16 * 128))
    w_out_l = np.ascontiguousarray(
        W_out.reshape(16, 128, 2048).transpose(1, 0, 2).reshape(128, 16 * 2048))
    wg = np.stack([W_gx, W_ga], 0).reshape(2, 8, 2, 128, 128)
    w_g_l = np.ascontiguousarray(wg.transpose(3, 0, 1, 2, 4).reshape(128, 2 * 8 * 2 * 128))
    b_g_l = np.ascontiguousarray(np.concatenate([b_gx.reshape(-1), b_ga.reshape(-1)], 0)
                                 .reshape(16, 128).T)
    ar_l = np.ascontiguousarray(a_real.reshape(8, 128).T)
    aim_l = np.ascontiguousarray(a_imag.reshape(8, 128).T)
    b_in_l = np.ascontiguousarray(b_in.reshape(1, DH))
    b_out_l = np.ascontiguousarray(b_out.reshape(1, D))
    ident_f = np.eye(128, dtype=np.float32)
    ident_b = np.eye(128).astype(ml_dtypes.bfloat16)
    ones_col_b = np.ones((128, 1), dtype=ml_dtypes.bfloat16)
    ones_row_f = np.ones((1, 128), dtype=np.float32)
    ones_row_b = np.ones((1, 128), dtype=ml_dtypes.bfloat16)
    lit_arr = np.tile(np.array([0.0, 1.0, 2.0, 0.5, EPS, PI / 2, -PI, 1e-24],
                               dtype=np.float32), (128, 1))

    if _NC_CACHE[0] is None:
        _NC_CACHE[0] = build_program()
    nc = _NC_CACHE[0]

    in_maps = []
    for c in range(NCORES):
        xc = np.ascontiguousarray(x[c])
        in_maps.append({
            "x_tr": np.ascontiguousarray(xc.T),
            "x_nat": xc,
            "w_in": w_in_l,
            "w_out": w_out_l,
            "w_g": w_g_l,
            "b_g": b_g_l,
            "b_in_row": b_in_l,
            "b_out_row": b_out_l,
            "ar_col": ar_l,
            "aim_col": aim_l,
            "gate_w_nat": gate_w,
            "up_w_nat": up_w,
            "down_w_nat": down_w,
            "ident_f": ident_f,
            "ident_b": ident_b,
            "ones_col_b": ones_col_b,
            "ones_row_f": ones_row_f,
            "ones_row_b": ones_row_b,
            "lit_in": lit_arr,
        })
    res = run_bass_kernel_spmd(nc, in_maps, core_ids=list(range(NCORES)))
    out = np.stack([np.asarray(r["out"]) for r in res.results], 0)
    return out.astype(np.float32)


# version 10 (final)
# speedup vs baseline: 1.2260x; 1.0030x over previous
import numpy as np
import ml_dtypes
import concourse.bass as bass
import concourse.mybir as mybir
from concourse.bass_utils import run_bass_kernel_spmd
from concourse import dve_ops
from concourse.dve_op import DveOp
from concourse.dve_spec import AluOp as DAlu, Bin, C0, C1, C2, Spec, Src0, Src1, maxx
from concourse.library_overlay import lower_extended_insts

F32 = mybir.dt.float32
BF16 = mybir.dt.bfloat16
ALU = mybir.AluOpType
AXX = mybir.AxisListType.X
ACT = mybir.ActivationFunctionType

_FP8_X = Src0 * Src1
_FP8_NX = Bin(DAlu.BITWISE_NOT, _FP8_X, _FP8_X)
_FP8_P = Bin(DAlu.BITWISE_NOT, Bin(DAlu.BITWISE_OR, _FP8_NX, C0), _FP8_NX)
_FP8_CM = maxx(_FP8_P, C1) * C2
FP8_NOTMASK = float(np.uint32(0x807FFFFF).view(np.float32))
FP8_C1 = float(2.0 ** -6)
FP8_C2 = float(1.5 * 2.0 ** 20)


def _fp8_rne_ref(in0, in1, s0, s1, imm2):
    f = np.float64 if in0.dtype == np.float64 else np.float32
    x = in0.astype(f) * in1.astype(f)
    if f is np.float64:
        ax = np.abs(x)
        _, e = np.frexp(ax)
        pc = np.maximum(np.ldexp(np.float64(1.0), e - 1), 2.0 ** -6)
        ulp = pc * (2.0 ** -3)
        return np.round(x / ulp) * ulp
    x = x.astype(np.float32)
    p = (x.view(np.int32) & np.int32(0x7F800000)).view(np.float32)
    pc = np.maximum(p, np.float32(2.0 ** -6))
    cm = (pc * np.float32(1.5 * 2.0 ** 20)).astype(np.float32)
    t = (x + cm).astype(np.float32)
    return (t - cm).astype(np.float32)


FP8_RNE_MUL = DveOp(
    "FP8_RNE_MUL",
    Spec(body=(_FP8_X + _FP8_CM) - _FP8_CM),
    subdim=False,
    uops_sha={"v3": "23f3018ff8a04129"},
    reference=_fp8_rne_ref,
)

if not any(o.name == "FP8_RNE_MUL" for o in dve_ops.OPS):
    dve_ops.OPS.append(FP8_RNE_MUL)
    dve_ops.CUSTOM_DVE_SPECS[FP8_RNE_MUL.name] = FP8_RNE_MUL.spec
    dve_ops.CUSTOM_DVE_REFS[FP8_RNE_MUL.name] = FP8_RNE_MUL.reference
    dve_ops._SUB_OPCODE_FOR_NAME[FP8_RNE_MUL.name] = 31

RCF = dve_ops.RECIPROCAL_APPROX_FAST
RCF_C = dve_ops.RECIP_APPROX_FAST_CONSTS

_WR_R = (Src0 * C0 + C1) - C1
WRAP_RNE = DveOp(
    "WRAP_RNE",
    Spec(body=Src0 - _WR_R * C2),
    subdim=False,
    uops_sha={"v3": "384a601106e5b300"},
    reference=lambda in0, in1, s0, s1, imm2: (
        (lambda f, x: (x - np.round(x * np.float64(s0)) * np.float64(imm2))
         if f is np.float64 else
         (x - (((x * np.float32(s0)).astype(np.float32) + np.float32(s1)).astype(np.float32)
               - np.float32(s1)).astype(np.float32) * np.float32(imm2)).astype(np.float32)
         )(np.float64 if in0.dtype == np.float64 else np.float32, in0)
    ),
)
if not any(o.name == "WRAP_RNE" for o in dve_ops.OPS):
    dve_ops.OPS.append(WRAP_RNE)
    dve_ops.CUSTOM_DVE_SPECS[WRAP_RNE.name] = WRAP_RNE.spec
    dve_ops.CUSTOM_DVE_REFS[WRAP_RNE.name] = WRAP_RNE.reference
    dve_ops._SUB_OPCODE_FOR_NAME[WRAP_RNE.name] = 30
WRAP_MAGIC = float(1.5 * 2.0 ** 23)
PI_CLAMP = float(np.float32(3.14159251))

NCORES = 8
T = 2048
D = 2048
DH = 2048
I_FFN = 8192
SEG = 256
NSEG = T // SEG
SUB = 64
NSUB = SEG // SUB
NTT = T // 128
EPS = 1e-5
PI = float(np.pi)
INV2PI = float(1.0 / (2.0 * np.pi))
TWOPI = float(2.0 * np.pi)
NHCH = I_FFN // 512
QT = 512
NQT = T // QT


FIFO_SCHED = False


class Buf:
    __slots__ = ("wtok", "rtoks", "lw", "ridx")

    def __init__(self):
        self.wtok = None
        self.rtoks = []
        self.lw = None
        self.ridx = []


class Plan:

    ENGINES = ("sync", "scalar", "tensor", "vector", "gpsimd")
    DEF_COST = {"sync": 1500, "scalar": 600, "tensor": 290, "vector": 700,
                "gpsimd": 1500}

    def __init__(self):
        self.ops = []
        self.cnt = {}
        self.tokprod = {}
        self.bufs = {}
        self.epoch = 0
        self.last_op = {e: None for e in self.ENGINES}

    def barrier(self):
        self.epoch += 1

    def buf(self, name):
        if name not in self.bufs:
            self.bufs[name] = Buf()
        return self.bufs[name]

    def emit(self, eng, fn, reads=(), writes=(), waits=(), sig=None, dma=False,
             cost=None):
        wl = list(waits)
        idx = len(self.ops)
        deps = set()
        for n in reads:
            b = self.buf(n)
            if b.wtok is not None:
                wl.append(b.wtok)
            if b.lw is not None and self.ops[b.lw]["engine"] == eng:
                deps.add(b.lw)
        for n in writes:
            b = self.buf(n)
            if b.wtok is not None:
                wl.append(b.wtok)
            wl.extend(b.rtoks)
            if b.lw is not None and self.ops[b.lw]["engine"] == eng:
                deps.add(b.lw)
            for r in b.ridx:
                if self.ops[r]["engine"] == eng:
                    deps.add(r)
        for (s, v) in wl:
            p = self.tokprod.get((s, v))
            if p is not None:
                deps.add(p)
        lo = self.last_op[eng]
        if lo is not None and (not lo[1] or sig is None):
            deps.add(lo[0])
        tok = None
        if sig is not None:
            self.cnt[sig] = self.cnt.get(sig, 0) + (16 if dma else 1)
            tok = (sig, self.cnt[sig])
            self.tokprod[tok] = idx
        self.last_op[eng] = (idx, sig is not None)
        self.ops.append({"engine": eng, "fn": fn, "deps": deps, "sig": sig,
                         "dma": dma, "epoch": self.epoch,
                         "cost": cost if cost is not None else self.DEF_COST[eng]})
        if tok is not None:
            for n in writes:
                b = self.buf(n)
                b.wtok = tok
                b.rtoks = []
            for n in reads:
                self.buf(n).rtoks.append(tok)
        for n in writes:
            b = self.buf(n)
            b.lw = idx
            b.ridx = []
        for n in reads:
            self.buf(n).ridx.append(idx)
        return tok

    def schedule(self):
        import heapq
        n = len(self.ops)
        children = [[] for _ in range(n)]
        indeg = [0] * n
        for i, op in enumerate(self.ops):
            for d in op["deps"]:
                children[d].append(i)
                indeg[i] += 1
        level = [0.0] * n
        for i in range(n - 1, -1, -1):
            c = self.ops[i]["cost"]
            m = 0.0
            for ch in children[i]:
                if level[ch] > m:
                    m = level[ch]
            level[i] = c + m
        etime = {e: 0.0 for e in self.ENGINES}
        done = [0.0] * n
        order = {e: [] for e in self.ENGINES}
        if FIFO_SCHED:
            for i, op in enumerate(self.ops):
                order[op["engine"]].append(i)
            return self._materialize(order)
        n_epochs = self.epoch + 1
        by_epoch = [[] for _ in range(n_epochs)]
        for i, op in enumerate(self.ops):
            by_epoch[op["epoch"]].append(i)
        for ep in range(n_epochs):
            ready = {e: [] for e in self.ENGINES}
            for i in by_epoch[ep]:
                if indeg[i] == 0:
                    heapq.heappush(ready[self.ops[i]["engine"]], (-level[i], i))
            remaining = len(by_epoch[ep])
            ep_floor = max(etime.values()) if ep > 0 else 0.0
            for e in self.ENGINES:
                etime[e] = max(etime[e], ep_floor)
            while remaining:
                cands = [e for e in self.ENGINES if ready[e]]
                e = min(cands, key=lambda x: etime[x])
                _, i = heapq.heappop(ready[e])
                op = self.ops[i]
                start = etime[e]
                for d in op["deps"]:
                    if done[d] > start:
                        start = done[d]
                fin = start + op["cost"]
                etime[e] = fin
                done[i] = fin
                order[e].append(i)
                remaining -= 1
                for ch in children[i]:
                    indeg[ch] -= 1
                    if indeg[ch] == 0 and self.ops[ch]["epoch"] == ep:
                        heapq.heappush(ready[self.ops[ch]["engine"]], (-level[ch], ch))
        return self._materialize(order)

    def _materialize(self, order):
        n = len(self.ops)
        n_epochs = self.epoch + 1
        NLANE = 8
        cnt = {}
        toks = [None] * n
        prewait = [None] * n
        lane_idx = {}
        for e in self.ENGINES:
            for i in order[e]:
                op = self.ops[i]
                if op["sig"] is None:
                    continue
                if op["dma"]:
                    q = op["sig"]
                    k = lane_idx.get(q, 0)
                    lane_idx[q] = k + 1
                    lane = f"{q}{k % NLANE}"
                    prev = cnt.get(lane, 0)
                    if k >= NLANE:
                        prewait[i] = (lane, prev)
                    cnt[lane] = prev + 16
                    toks[i] = (lane, cnt[lane])
                    op["lane_sig"] = (lane, 16)
                else:
                    cnt[op["sig"]] = cnt.get(op["sig"], 0) + 1
                    toks[i] = (op["sig"], cnt[op["sig"]])
                    op["lane_sig"] = (op["sig"], 1)
        ecnt = [dict() for _ in range(n_epochs + 1)]
        run = {}
        for ep in range(n_epochs):
            for e in self.ENGINES:
                for i in order[e]:
                    if self.ops[i]["epoch"] == ep and toks[i] is not None:
                        s, v = toks[i]
                        if v > run.get(s, -1):
                            run[s] = v
            ecnt[ep + 1] = dict(run)
        sched = {e: [] for e in self.ENGINES}
        for e in self.ENGINES:
            for i in order[e]:
                op = self.ops[i]
                wd = dict(ecnt[op["epoch"]])
                if prewait[i] is not None and prewait[i][1] > wd.get(prewait[i][0], -1):
                    wd[prewait[i][0]] = prewait[i][1]
                for d in op["deps"]:
                    t = toks[d]
                    if t is not None and t[1] > wd.get(t[0], -1):
                        wd[t[0]] = t[1]
                sg = op.get("lane_sig") if op["sig"] else None
                sched[e].append((op["fn"], sorted(wd.items()), sg))
        self.final_cnt = cnt
        self._dbg_order = order
        return sched


def build_program():
    nc = bass.Bass("TRN2", target_bir_lowering=False)
    dt = nc.dram_tensor

    x_tr = dt("x_tr", [D, T], F32, kind="ExternalInput").ap()
    x_nat = dt("x_nat", [T, D], F32, kind="ExternalInput").ap()
    w_in = dt("w_in", [128, 16 * 16 * 128], F32, kind="ExternalInput").ap()
    w_out = dt("w_out", [128, 16 * 2048], F32, kind="ExternalInput").ap()
    w_g = dt("w_g", [128, 2 * 8 * 2 * 128], F32, kind="ExternalInput").ap()
    b_g = dt("b_g", [128, 16], F32, kind="ExternalInput").ap()
    b_in_row = dt("b_in_row", [1, DH], F32, kind="ExternalInput").ap()
    b_out_row = dt("b_out_row", [1, D], F32, kind="ExternalInput").ap()
    ar_col = dt("ar_col", [128, 8], F32, kind="ExternalInput").ap()
    aim_col = dt("aim_col", [128, 8], F32, kind="ExternalInput").ap()
    gate_w_nat = dt("gate_w_nat", [I_FFN, D], F32, kind="ExternalInput").ap()
    up_w_nat = dt("up_w_nat", [I_FFN, D], F32, kind="ExternalInput").ap()
    down_w_nat = dt("down_w_nat", [D, I_FFN], F32, kind="ExternalInput").ap()
    ident_f = dt("ident_f", [128, 128], F32, kind="ExternalInput").ap()
    ident_b = dt("ident_b", [128, 128], BF16, kind="ExternalInput").ap()
    ones_col_b = dt("ones_col_b", [128, 1], BF16, kind="ExternalInput").ap()
    ones_row_f = dt("ones_row_f", [1, 128], F32, kind="ExternalInput").ap()
    ones_row_b = dt("ones_row_b", [1, 128], BF16, kind="ExternalInput").ap()
    lit_in = dt("lit_in", [128, 8], F32, kind="ExternalInput").ap()

    out_hbm = dt("out", [T, D], F32, kind="ExternalOutput").ap()
    hm_sp = dt("hm_sp", [DH, T], BF16, kind="Internal").ap()
    x1_sp = dt("x1_sp", [T, D], F32, kind="Internal").ap()
    xq_sp = dt("xq_sp", [T, D], BF16, kind="Internal").ap()
    hq_sp = dt("hq_sp", [I_FFN, T], BF16, kind="Internal").ap()
    dwq_sp = dt("dwq_sp", [I_FFN, D], BF16, kind="Internal").ap()
    gwq_sp = dt("gwq_sp", [D, I_FFN], BF16, kind="Internal").ap()
    uwq_sp = dt("uwq_sp", [D, I_FFN], BF16, kind="Internal").ap()

    ctx = {}
    handles = {}
    alloc_seq = []

    def sb(name, shape, dtype):
        h = nc.sbuf_tensor(name, shape, dtype)
        t = h.__enter__()
        handles[name] = h
        ctx[name] = t
        alloc_seq.append(name)
        return t

    def free_group(names):
        ns = set(names)
        for n in reversed(alloc_seq):
            if n in ns:
                handles[n].__exit__(None, None, None)
        alloc_seq[:] = [n for n in alloc_seq if n not in ns]

    psum_handles = {}

    def psum(name, shape=(128, 512), dtype=F32):
        h = nc.psum_tensor(list(shape), dtype)
        t = h.__enter__()
        ctx[name] = t
        psum_handles[name] = h
        return t

    def psum_free(names):
        for n in names:
            psum_handles[n].__exit__(None, None, None)

    sb("c_lit", [128, 8], F32)
    sb("c_idb", [128, 128], BF16)
    sb("c_bg", [128, 16], F32)
    sb("c_ar", [128, 8], F32)
    sb("c_c1", [128, 8], F32)
    sb("c_aim", [128, 8], F32)
    sb("c_idf", [128, 128], F32)
    sb("c_ocb", [128, 1], BF16)
    sb("c_orf", [1, 128], F32)
    sb("c_binrow", [1, DH], BF16)
    sb("c_colsum", [1, DH], BF16)
    sb("c_tmp8", [128, 8], F32)
    A_GROUP = (["a_win", "a_wg", "a_xtb", "a_xh0", "a_xh1", "a_xhb", "a_gx", "a_ga",
                "a_hm0", "a_hm1", "a_rows", "a_rows_b", "a_rrep", "a_stage", "a_sm",
                "a_carry", "qa_nat0", "qa_q", "qa_dq", "qa_am",
                "qa_s", "qa_is", "qa_tst0", "c_tmp8", "c_binrow", "c_colsum"]
               + [f"a_b{i}_{pp}" for i in range(1, 10) for pp in range(2)])
    sb("a_win", [128, 16 * 16 * 128], BF16)
    sb("a_wg", [128, 2 * 8 * 2 * 128], BF16)
    sb("a_xtb", [128, 16 * SEG], BF16)
    sb("a_xh0", [128, 16 * SEG], F32)
    sb("a_xh1", [128, 16 * SEG], F32)
    sb("a_xhb", [128, 16 * SEG], BF16)
    sb("a_gx", [128, 8 * SEG], F32)
    sb("a_ga", [128, 8 * SEG], F32)
    sb("qa_nat0", [128, 2048], F32)
    sb("qa_q", [128, 2048], BF16)
    sb("qa_dq", [128, 2048], BF16)
    sb("qa_am", [128, 64], F32)
    sb("qa_s", [128, 64], F32)
    sb("qa_is", [128, 64], F32)
    sb("qa_tst0", [128, 1024], BF16)
    for i in range(1, 10):
        for pp in range(2):
            sb(f"a_b{i}_{pp}", [128, 8 * SUB], F32)
    sb("a_hm0", [128, 16 * SUB], BF16)
    sb("a_hm1", [128, 16 * SUB], BF16)
    sb("a_rows", [1, SEG], F32)
    sb("a_rows_b", [1, 2 * SEG], BF16)
    sb("a_rrep", [128, SEG], F32)
    sb("a_stage", [128, 4], F32)
    sb("a_sm", [128, 16], F32)
    sb("a_carry", [128, 24], F32)
    P = [psum(f"P{i}") for i in range(6)]
    PN = [f"P{i}" for i in range(6)]
    PB6 = psum("PB6", (128, 1024), BF16)
    PB7 = psum("PB7", (128, 1024), BF16)

    p = Plan()

    def em(eng, fn, reads=(), writes=(), waits=(), sig=None, dma=False, cost=None):
        return p.emit(eng, fn, reads, writes, waits, sig, dma, cost)

    def mmc(n):
        return 60 + int(0.5 * n)

    def dvec(n, tier=1.1):
        return 300 + int(tier * n)

    def actc(n):
        return 250 + int(0.85 * n)

    def gpsc(n):
        return 500 + int(1.2 * n)

    def dmac(nbytes):
        return 1200 + int(nbytes / 250)

    SV, SS, ST, SG = "v", "s", "t", "g"

    LITS = [0.0, 1.0, 2.0, 0.5, EPS, PI / 2, -PI, 1e-24]
    for i, v in enumerate(LITS):
        nc.const_aps.register(F32, v, ctx["c_lit"][:, i:i + 1])
    em("sync", lambda e: e.dma_start(out=ctx["c_lit"][:], in_=lit_in),
       writes=["c_lit"], sig="dh", dma=True)
    p.barrier()

    for (dst, srcap) in [("c_bg", b_g), ("c_ar", ar_col), ("c_aim", aim_col),
                         ("c_idf", ident_f), ("c_idb", ident_b), ("c_ocb", ones_col_b),
                         ("c_orf", ones_row_f)]:
        em("sync", (lambda e, d=dst, s=srcap: e.dma_start(out=ctx[d][:], in_=s)),
           writes=[dst], sig="dh", dma=True)
    em("gpsimd", lambda e: e.dma_start(out=ctx["c_binrow"][:], in_=b_in_row),
       writes=["c_binrow"], sig="dg", dma=True)
    em("gpsimd", lambda e: e.dma_start(out=ctx["a_win"][:], in_=w_in),
       writes=["a_win"], sig="dg", dma=True)
    em("gpsimd", lambda e: e.dma_start(out=ctx["a_wg"][:], in_=w_g),
       writes=["a_wg"], sig="dg", dma=True)
    em("vector", lambda e: e.memset(ctx["a_carry"][:], 0.0), writes=["a_carry"], sig=SV)
    em("scalar", lambda e: e.activation(ctx["c_tmp8"][:], ctx["c_ar"][:], ACT.Exp),
       reads=["c_ar"], writes=["c_tmp8"], sig=SS)
    em("scalar", lambda e: e.activation(ctx["c_tmp8"][:], ctx["c_tmp8"][:] if False else ctx["c_tmp8"][:], ACT.Ln, bias=1.0) if False else None,
       sig=None) if False else None
    em("scalar", lambda e: e.activation(ctx["c_c1"][:], ctx["c_tmp8"][:], ACT.Ln, bias=1.0),
       reads=["c_tmp8"], writes=["c_c1"], sig=SS)
    em("vector", lambda e: e.tensor_scalar(ctx["c_c1"][:], ctx["c_c1"][:], -8.0, None, ALU.mult),
       reads=["c_c1"], writes=["c_c1"], sig=SV)

    for cc in range(4):
        for c4 in range(4):
            cht = cc * 4 + c4
            for kt in range(16):
                r0 = ctx["a_win"][:, (kt * 16 + cht) * 128:(kt * 16 + cht + 1) * 128]
                em("tensor",
                   (lambda e, o=P[5].ap()[0:1, c4 * 128:(c4 + 1) * 128], l=ctx["c_ocb"][:],
                    r=r0, st=(kt == 0), sp=(kt == 15): e.matmul(o, l, r, start=st, stop=sp)),
                   reads=["a_win", "c_ocb"], writes=["P5"],
                   sig=ST if (kt == 15 and c4 == 3) else None)
        em("vector",
           (lambda e, cc=cc: e.tensor_copy(ctx["c_colsum"][0:1, cc * 512:(cc + 1) * 512],
                                           P[5].ap()[0:1, 0:512])),
           reads=["P5"], writes=["c_colsum"], sig=SV)

    def quant_unit(kind, i, par):
        nat = "qa_nat0"
        if kind == 0:
            src_ap = gate_w_nat[i * 128:(i + 1) * 128, :]
            dst = gwq_sp
            roff, coff = 0, i * 128
        elif kind == 1:
            src_ap = up_w_nat[i * 128:(i + 1) * 128, :]
            dst = uwq_sp
            roff, coff = 0, i * 128
        else:
            n_i, q_i = i % 16, i // 16
            src_ap = down_w_nat[n_i * 128:(n_i + 1) * 128, q_i * 2048:(q_i + 1) * 2048]
            dst = dwq_sp
            roff, coff = q_i * 16, n_i * 128
        dstn = ["gwq_sp", "uwq_sp", "dwq_sp"][kind]
        em("gpsimd", (lambda e, s=src_ap, nat=nat: e.dma_start(out=ctx[nat][:], in_=s)),
           writes=[nat], sig="dg", dma=True, cost=dmac(128 * 2048 * 4))
        em("vector",
           (lambda e, nat=nat: e.tensor_reduce(
               ctx["qa_am"][:], ctx[nat][:].rearrange("p (nb k) -> p nb k", k=32),
               AXX, ALU.max, apply_absolute_value=True)),
           reads=[nat], writes=["qa_am"], sig=SV, cost=dvec(2048))
        em("vector",
           (lambda e: e.tensor_scalar(ctx["qa_s"][:], ctx["qa_am"][:], 1.0 / 448.0,
                                      1e-30, ALU.mult, ALU.max)),
           reads=["qa_am"], writes=["qa_s"], sig=SV)
        em("vector", (lambda e: e.reciprocal(ctx["qa_is"][:], ctx["qa_s"][:])),
           reads=["qa_s"], writes=["qa_is"], sig=SV)
        em("vector",
           (lambda e, nat=nat: e._custom_dve(
               FP8_RNE_MUL,
               out=ctx["qa_q"][:].rearrange("p (nb k) -> p nb k", k=32),
               in0=ctx["qa_is"][:].unsqueeze(2).broadcast_to([128, 64, 32]),
               in1=ctx[nat][:].rearrange("p (nb k) -> p nb k", k=32),
               s0=FP8_NOTMASK, s1=FP8_C1, imm2=FP8_C2)),
           reads=["qa_is", nat], writes=["qa_q"], sig=SV, cost=dvec(2048))
        em("gpsimd",
           (lambda e: e.tensor_tensor(
               ctx["qa_dq"][:].rearrange("p (nb k) -> p nb k", k=32),
               ctx["qa_q"][:].rearrange("p (nb k) -> p nb k", k=32),
               ctx["qa_s"][:].unsqueeze(2).broadcast_to([128, 64, 32]), ALU.mult)),
           reads=["qa_q", "qa_s"], writes=["qa_dq"], sig=SG, cost=gpsc(2048))
        for batch in range(2):
            tst = "qa_tst0"
            pbk = PB6 if batch == 0 else PB7
            pbn = "PB6" if batch == 0 else "PB7"
            for k8 in range(8):
                kt = batch * 8 + k8
                em("tensor",
                   (lambda e, o=pbk.ap()[:, k8 * 128:(k8 + 1) * 128],
                    i_=ctx["qa_dq"][:, kt * 128:(kt + 1) * 128]:
                    e.transpose(o, i_, ctx["c_idb"][:])),
                   reads=["qa_dq", "c_idb"], writes=[pbn], sig=ST if k8 == 7 else None)
            em("scalar",
               (lambda e, tst=tst, pbk=pbk: e.activation(ctx[tst][:], pbk.ap()[:, 0:1024], ACT.Copy)),
               reads=[pbn], writes=[tst], sig=SS, cost=actc(1024))
            q_eng, q_sig = ("sync", "dh")
            em(q_eng,
               (lambda e, dst=dst, roff=roff, coff=coff, batch=batch, tst=tst: e.dma_start(
                   out=dst.rearrange("(r p) n -> p r n", p=128)[
                       :, roff + batch * 8:roff + (batch + 1) * 8, coff:coff + 128],
                   in_=ctx[tst][:].rearrange("p (r n) -> p r n", r=8))),
               reads=[tst], writes=[dstn], sig=q_sig, dma=True, cost=dmac(128 * 1024 * 2))

    QUANT_UNITS = ([(0, i) for i in range(64)] + [(1, i) for i in range(64)]
                   + [(2, i) for i in range(64)])
    q_cursor = [0]

    def emit_quant_batch(n):
        for _ in range(n):
            if q_cursor[0] >= len(QUANT_UNITS):
                return
            kind, i = QUANT_UNITS[q_cursor[0]]
            quant_unit(kind, i, q_cursor[0] % 2)
            q_cursor[0] += 1

    for seg in range(NSEG):
        emit_quant_batch(24)
        t0 = seg * SEG
        xhn = f"a_xh{seg % 2}"
        em("gpsimd",
           (lambda e, t0=t0: e.dma_start(
               out=ctx["a_xtb"][:].rearrange("p (kt t) -> p kt t", kt=16),
               in_=x_tr.rearrange("(kt p) t -> p kt t", p=128)[:, :, t0:t0 + SEG])),
           writes=["a_xtb"], sig="dg", dma=True, cost=dmac(128 * 16 * SEG * 4))

        for c in range(SEG // 128):
            for kt in range(16):
                xt = ctx["a_xtb"][:, kt * SEG + c * 128: kt * SEG + (c + 1) * 128]
                em("tensor",
                   (lambda e, o=P[4].ap()[:, 0:128], l=xt, r=xt, st=(kt == 0), sp=(kt == 15):
                    e.matmul(o, l, r, start=st, stop=sp)),
                   reads=["a_xtb"], writes=["P4"], sig=None)
            for kt in range(16):
                xt = ctx["a_xtb"][:, kt * SEG + c * 128: kt * SEG + (c + 1) * 128]
                em("tensor",
                   (lambda e, o=P[4].ap()[:, 128:129], l=xt, r=ctx["c_ocb"][:],
                    st=(kt == 0), sp=(kt == 15): e.matmul(o, l, r, start=st, stop=sp)),
                   reads=["a_xtb", "c_ocb"], writes=["P4"], sig=ST if kt == 15 else None)
            em("vector",
               (lambda e: e.tensor_tensor_reduce(ctx["a_b1_0"][:, 0:128], P[4].ap()[:, 0:128],
                                                 ctx["c_idf"][:], 1.0, 0.0, ALU.mult,
                                                 ALU.add, ctx["a_sm"][:, 0:1])),
               reads=["P4", "c_idf"], writes=["a_b1_0", "a_sm"], sig=SV)
            em("vector",
               (lambda e: e.tensor_scalar(ctx["a_sm"][:, 1:2], P[4].ap()[:, 128:129],
                                          1.0 / D, None, ALU.mult)),
               reads=["P4"], writes=["a_sm"], sig=SV)
            em("vector",
               (lambda e: e.tensor_scalar(ctx["a_sm"][:, 2:3], ctx["a_sm"][:, 0:1],
                                          1.0 / D, None, ALU.mult)),
               reads=["a_sm"], writes=["a_sm"], sig=SV)
            em("vector",
               (lambda e: e.scalar_tensor_tensor(ctx["a_sm"][:, 3:4], ctx["a_sm"][:, 1:2],
                                                 -1.0, ctx["a_sm"][:, 1:2], ALU.mult, ALU.mult)),
               reads=["a_sm"], writes=["a_sm"], sig=SV)
            em("vector",
               (lambda e: e.tensor_tensor(ctx["a_sm"][:, 4:5], ctx["a_sm"][:, 2:3],
                                          ctx["a_sm"][:, 3:4], ALU.add)),
               reads=["a_sm"], writes=["a_sm"], sig=SV)
            em("scalar",
               (lambda e: e.activation(ctx["a_stage"][:, 1:2], ctx["a_sm"][:, 4:5],
                                       ACT.Sqrt, bias=EPS)),
               reads=["a_sm"], writes=["a_stage"], sig=SS)
            em("vector",
               (lambda e: e.reciprocal(ctx["a_stage"][:, 2:3], ctx["a_stage"][:, 1:2])),
               reads=["a_stage"], writes=["a_stage"], sig=SV)
            em("vector",
               (lambda e: e.scalar_tensor_tensor(ctx["a_stage"][:, 0:1], ctx["a_sm"][:, 1:2],
                                                 -1.0, ctx["a_stage"][:, 2:3], ALU.mult, ALU.mult)),
               reads=["a_sm", "a_stage"], writes=["a_stage"], sig=SV)
            for k in range(3):
                em("tensor",
                   (lambda e, k=k: e.transpose(P[5].ap()[0:1, k * 128:(k + 1) * 128],
                                               ctx["a_stage"][:, k:k + 1], ctx["c_idf"][:])),
                   reads=["a_stage", "c_idf"], writes=["P5"], sig=ST if k == 2 else None)
            em("vector",
               (lambda e, c=c: e.tensor_copy(
                   ctx["a_rows_b"][0:1, :].rearrange("o (r s) -> o r s", r=2)[:, :, c * 128:(c + 1) * 128],
                   P[5].ap()[0:1, 0:256].rearrange("o (r s) -> o r s", r=2))),
               reads=["P5"], writes=["a_rows_b"], sig=SV)
            em("vector",
               (lambda e, c=c: e.tensor_copy(ctx["a_rows"][0:1, c * 128:(c + 1) * 128],
                                             P[5].ap()[0:1, 256:384])),
               reads=["P5"], writes=["a_rows"], sig=SV)
        em("tensor",
           (lambda e: e.matmul(P[5].ap()[:, 0:SEG], ctx["c_orf"][0:1, :],
                               ctx["a_rows"][0:1, 0:SEG], start=True, stop=True)),
           reads=["c_orf", "a_rows"], writes=["P5"], sig=ST)
        em("vector", (lambda e: e.tensor_copy(ctx["a_rrep"][:], P[5].ap()[:, 0:SEG])),
           reads=["P5"], writes=["a_rrep"], sig=SV)

        for cht in range(16):
            bk, bnm = P[cht % 2], PN[cht % 2]
            for kt in range(16):
                l = ctx["a_win"][:, (kt * 16 + cht) * 128:(kt * 16 + cht + 1) * 128]
                r = ctx["a_xtb"][:, kt * SEG:(kt + 1) * SEG]
                em("tensor",
                   (lambda e, o=bk.ap()[:, 0:SEG], l=l, r=r, st=(kt == 0):
                    e.matmul(o, l, r, start=st, stop=False)),
                   reads=(["a_win", "a_xtb"] if kt < 15 else ["a_win", "a_xtb"]),
                   writes=[bnm], sig=ST if kt == 15 else None)
            em("tensor",
               (lambda e, o=bk.ap()[:, 0:SEG], l=ctx["c_colsum"][0:1, cht * 128:(cht + 1) * 128],
                r=ctx["a_rows_b"][0:1, 0:SEG]: e.matmul(o, l, r, start=False, stop=False)),
               reads=["c_colsum", "a_rows_b"], writes=[bnm], sig=None)
            em("tensor",
               (lambda e, o=bk.ap()[:, 0:SEG], l=ctx["c_binrow"][0:1, cht * 128:(cht + 1) * 128],
                r=ctx["a_rows_b"][0:1, SEG:2 * SEG]: e.matmul(o, l, r, start=False, stop=True)),
               reads=["c_binrow", "a_rows_b"], writes=[bnm], sig=ST)
            em("vector",
               (lambda e, cht=cht, bk=bk, xhn=xhn: e.tensor_tensor(
                   ctx[xhn][:, cht * SEG:(cht + 1) * SEG], bk.ap()[:, 0:SEG],
                   ctx["a_rrep"][:], ALU.mult)),
               reads=[bnm, "a_rrep"], writes=[xhn], sig=SV)
            em("vector",
               (lambda e, cht=cht, xhn=xhn: e.tensor_copy(
                   ctx["a_xhb"][:, cht * SEG:(cht + 1) * SEG],
                   ctx[xhn][:, cht * SEG:(cht + 1) * SEG])),
               reads=[xhn], writes=["a_xhb"], sig=SV)

        for blk in range(8):
            for g in range(2):
                bk, bnm = P[2 + (blk * 2 + g) % 2], PN[2 + (blk * 2 + g) % 2]
                for kt2 in range(2):
                    l = ctx["a_wg"][:, ((g * 8 + blk) * 2 + kt2) * 128:
                                    ((g * 8 + blk) * 2 + kt2 + 1) * 128]
                    r = ctx["a_xhb"][:, (blk * 2 + kt2) * SEG:(blk * 2 + kt2 + 1) * SEG]
                    em("tensor",
                       (lambda e, o=bk.ap()[:, 0:SEG], l=l, r=r, st=(kt2 == 0), sp=(kt2 == 1):
                        e.matmul(o, l, r, start=st, stop=sp)),
                       reads=["a_wg", "a_xhb"], writes=[bnm], sig=ST if kt2 == 1 else None)
                dstn = "a_gx" if g == 0 else "a_ga"
                em("scalar",
                   (lambda e, bk=bk, dstn=dstn, blk=blk, g=g: e.activation(
                       ctx[dstn][:, blk * SEG:(blk + 1) * SEG], bk.ap()[:, 0:SEG],
                       ACT.Sigmoid, bias=ctx["c_bg"][:, g * 8 + blk:g * 8 + blk + 1])),
                   reads=[bnm, "c_bg"], writes=[dstn], sig=SS)

        c1b = ctx["c_c1"][:].unsqueeze(2).broadcast_to([128, 8, SUB])
        aimb = ctx["c_aim"][:].unsqueeze(2).broadcast_to([128, 8, SUB])
        for sub in range(NSUB):
            spp = (seg * NSUB + sub) % 2
            ts = slice(sub * SUB, (sub + 1) * SUB)
            gas = ctx["a_ga"][:].rearrange("p (g t) -> p g t", g=8)[:, :, ts]
            gxs = ctx["a_gx"][:].rearrange("p (g t) -> p g t", g=8)[:, :, ts]
            xh16 = ctx[xhn][:].rearrange("p (g t) -> p g t", g=16)
            xr = xh16[:, 0:8, ts]
            xi = xh16[:, 8:16, ts]
            hmn = f"a_hm{spp}"
            hm16 = ctx[hmn][:].rearrange("p (g t) -> p g t", g=16)

            def bb(i, spp=spp):
                return ctx[f"a_b{i}_{spp}"][:].rearrange("p (g t) -> p g t", g=8)

            def bf(i, spp=spp):
                return ctx[f"a_b{i}_{spp}"][:]

            def bn(i, spp=spp):
                return f"a_b{i}_{spp}"

            def V(o, fn, rd, wr):
                em("vector", fn, reads=rd, writes=wr, sig=SV)

            em("vector", (lambda e, o=bb(1), a=gas, b=c1b: e.tensor_tensor(o, a, b, ALU.mult)),
               reads=["a_ga", "c_c1"], writes=[bn(1)], sig=SV)
            em("vector", (lambda e, o=bb(2), a=gas, b=aimb: e.tensor_tensor(o, a, b, ALU.mult)),
               reads=["a_ga", "c_aim"], writes=[bn(2)], sig=SV)
            em("scalar", (lambda e, o=bf(3), i=bf(1): e.activation(o, i, ACT.Exp)),
               reads=[bn(1)], writes=[bn(3)], sig=SS)
            em("scalar", (lambda e, o=bf(4), i=bf(1): e.activation(o, i, ACT.Exp, scale=2.0)),
               reads=[bn(1)], writes=[bn(4)], sig=SS)
            em("scalar", (lambda e, o=bf(5), i=bf(2): e.activation(o, i, ACT.Sin, scale=2.0, bias=PI / 2)),
               reads=[bn(2)], writes=[bn(5)], sig=SS)
            em("scalar", (lambda e, o=bf(6), i=bf(2): e.activation(o, i, ACT.Sin, scale=2.0)),
               reads=[bn(2)], writes=[bn(6)], sig=SS)
            em("vector", (lambda e, o=bf(1), a=bf(4), b=bf(5): e.tensor_tensor(o, a, b, ALU.mult)),
               reads=[bn(4), bn(5)], writes=[bn(1)], sig=SV)
            em("vector", (lambda e, o=bf(5), a=bf(4), b=bf(6): e.tensor_tensor(o, a, b, ALU.mult)),
               reads=[bn(4), bn(6)], writes=[bn(5)], sig=SV)
            em("vector", (lambda e, o=bf(4), i=bf(1): e.tensor_scalar(o, i, -1.0, 1.0, ALU.mult, ALU.add)),
               reads=[bn(1)], writes=[bn(4)], sig=SV)
            em("scalar", (lambda e, o=bf(1), i=bf(4): e.activation(o, i, ACT.Square)),
               reads=[bn(4)], writes=[bn(1)], sig=SS)
            em("scalar", (lambda e, o=bf(6), i=bf(5): e.activation(o, i, ACT.Square)),
               reads=[bn(5)], writes=[bn(6)], sig=SS)
            em("vector", (lambda e, o=bf(6), a=bf(1), b=bf(6): e.tensor_tensor(o, a, b, ALU.add)),
               reads=[bn(1), bn(6)], writes=[bn(6)], sig=SV)
            em("scalar", (lambda e, o=bf(1), i=bf(6): e.activation(o, i, ACT.Sqrt)),
               reads=[bn(6)], writes=[bn(1)], sig=SS)
            em("vector", (lambda e, o=bf(6), a=bf(1), b=bf(4): e.tensor_tensor(o, a, b, ALU.add)),
               reads=[bn(1), bn(4)], writes=[bn(6)], sig=SV)
            em("scalar", (lambda e, o=bf(1), i=bf(6): e.activation(o, i, ACT.Sqrt, scale=0.5, bias=1e-24)),
               reads=[bn(6)], writes=[bn(1)], sig=SS)
            em("vector", (lambda e, o=bf(6), i=bf(1): e._custom_dve(
                RCF, out=o, in0=i, s0=RCF_C["s0"], s1=RCF_C["s1"], imm2=RCF_C["imm2"])),
               reads=[bn(1)], writes=[bn(6)], sig=SV)
            em("vector", (lambda e, o=bf(7), i=bf(5), b=bf(6): e.scalar_tensor_tensor(
                o, i, -0.5, b, ALU.mult, ALU.mult)),
               reads=[bn(5), bn(6)], writes=[bn(7)], sig=SV)
            for grp in range(8):
                em("vector",
                   (lambda e, grp=grp, spp=spp: e.tensor_tensor_scan(
                       ctx[f"a_b8_{spp}"][:, grp * SUB:(grp + 1) * SUB],
                       ctx[f"a_b2_{spp}"][:, grp * SUB:(grp + 1) * SUB],
                       ctx[f"a_b2_{spp}"][:, grp * SUB:(grp + 1) * SUB],
                       ctx["a_carry"][:, 16 + grp:17 + grp], ALU.add, ALU.bypass)),
                   reads=[bn(2), "a_carry"], writes=[bn(8)], sig=SV if grp == 7 else None)
            em("vector",
               (lambda e, spp=spp: e.tensor_copy(
                   ctx["a_carry"][:, 16:24],
                   ctx[f"a_b8_{spp}"][:].rearrange("p (g t) -> p g t", g=8)[:, :, SUB - 1:SUB].squeeze(2))),
               reads=[bn(8)], writes=["a_carry"], sig=SV)
            em("vector", (lambda e, o=bf(2), i=bf(8): e._custom_dve(
                WRAP_RNE, out=o, in0=i, s0=INV2PI, s1=WRAP_MAGIC, imm2=TWOPI)),
               reads=[bn(8)], writes=[bn(2)], sig=SV)
            em("vector", (lambda e, o=bf(4), i=bf(2): e.tensor_scalar(
                o, i, -PI_CLAMP, PI_CLAMP, ALU.max, ALU.min)),
               reads=[bn(2)], writes=[bn(4)], sig=SV)
            em("scalar", (lambda e, o=bf(2), i=bf(4): e.activation(o, i, ACT.Sin)),
               reads=[bn(4)], writes=[bn(2)], sig=SS)
            em("scalar", (lambda e, o=bf(5), i=bf(4): e.activation(o, i, ACT.Abs)),
               reads=[bn(4)], writes=[bn(5)], sig=SS)
            em("scalar", (lambda e, o=bf(4), i=bf(5): e.activation(o, i, ACT.Sin, scale=-1.0, bias=PI / 2)),
               reads=[bn(5)], writes=[bn(4)], sig=SS)
            em("vector", (lambda e, o=bf(5), a=bf(1), b=bf(4): e.tensor_tensor(o, a, b, ALU.mult)),
               reads=[bn(1), bn(4)], writes=[bn(5)], sig=SV)
            em("vector", (lambda e, o=bf(9), a=bf(7), b=bf(2): e.tensor_tensor(o, a, b, ALU.mult)),
               reads=[bn(7), bn(2)], writes=[bn(9)], sig=SV)
            em("vector", (lambda e, o=bf(9), a=bf(5), b=bf(9): e.tensor_tensor(o, a, b, ALU.add)),
               reads=[bn(5), bn(9)], writes=[bn(9)], sig=SV)
            em("vector", (lambda e, o=bb(5), a=gxs, b=bb(9): e.tensor_tensor(o, a, b, ALU.mult)),
               reads=["a_gx", bn(9)], writes=[bn(5)], sig=SV)
            em("vector", (lambda e, o=bf(9), a=bf(7), b=bf(4): e.tensor_tensor(o, a, b, ALU.mult)),
               reads=[bn(7), bn(4)], writes=[bn(9)], sig=SV)
            em("vector", (lambda e, o=bf(6), a=bf(1), b=bf(2): e.tensor_tensor(o, a, b, ALU.mult)),
               reads=[bn(1), bn(2)], writes=[bn(6)], sig=SV)
            em("vector", (lambda e, o=bf(9), a=bf(9), b=bf(6): e.tensor_tensor(o, a, b, ALU.subtract)),
               reads=[bn(9), bn(6)], writes=[bn(9)], sig=SV)
            em("vector", (lambda e, o=bb(6), a=gxs, b=bb(9): e.tensor_tensor(o, a, b, ALU.mult)),
               reads=["a_gx", bn(9)], writes=[bn(6)], sig=SV)
            em("gpsimd", (lambda e, o=bb(1), a=xr, b=bb(5): e.tensor_tensor(o, a, b, ALU.mult)),
               reads=[xhn, bn(5)], writes=[bn(1)], sig=SG)
            em("gpsimd", (lambda e, o=bb(9), a=xi, b=bb(6): e.tensor_tensor(o, a, b, ALU.mult)),
               reads=[xhn, bn(6)], writes=[bn(9)], sig=SG)
            em("gpsimd", (lambda e, o=bf(4), a=bf(1), b=bf(9): e.tensor_tensor(o, a, b, ALU.subtract)),
               reads=[bn(1), bn(9)], writes=[bn(4)], sig=SG)
            em("gpsimd", (lambda e, o=bb(1), a=xr, b=bb(6): e.tensor_tensor(o, a, b, ALU.mult)),
               reads=[xhn, bn(6)], writes=[bn(1)], sig=SG)
            em("gpsimd", (lambda e, o=bb(9), a=xi, b=bb(5): e.tensor_tensor(o, a, b, ALU.mult)),
               reads=[xhn, bn(5)], writes=[bn(9)], sig=SG)
            em("gpsimd", (lambda e, o=bf(2), a=bf(1), b=bf(9): e.tensor_tensor(o, a, b, ALU.add)),
               reads=[bn(1), bn(9)], writes=[bn(2)], sig=SG)
            for grp in range(8):
                em("vector",
                   (lambda e, grp=grp, spp=spp: e.tensor_tensor_scan(
                       ctx[f"a_b5_{spp}"][:, grp * SUB:(grp + 1) * SUB],
                       ctx[f"a_b3_{spp}"][:, grp * SUB:(grp + 1) * SUB],
                       ctx[f"a_b4_{spp}"][:, grp * SUB:(grp + 1) * SUB],
                       ctx["a_carry"][:, grp:grp + 1], ALU.mult, ALU.add)),
                   reads=[bn(3), bn(4), "a_carry"], writes=[bn(5)], sig=SV if grp == 7 else None)
            for grp in range(8):
                em("vector",
                   (lambda e, grp=grp, spp=spp: e.tensor_tensor_scan(
                       ctx[f"a_b6_{spp}"][:, grp * SUB:(grp + 1) * SUB],
                       ctx[f"a_b3_{spp}"][:, grp * SUB:(grp + 1) * SUB],
                       ctx[f"a_b2_{spp}"][:, grp * SUB:(grp + 1) * SUB],
                       ctx["a_carry"][:, 8 + grp:9 + grp], ALU.mult, ALU.add)),
                   reads=[bn(3), bn(2), "a_carry"], writes=[bn(6)], sig=SV if grp == 7 else None)
            em("vector",
               (lambda e, spp=spp: e.tensor_copy(
                   ctx["a_carry"][:, 0:8],
                   ctx[f"a_b5_{spp}"][:].rearrange("p (g t) -> p g t", g=8)[:, :, SUB - 1:SUB].squeeze(2))),
               reads=[bn(5)], writes=["a_carry"], sig=SV)
            em("vector",
               (lambda e, spp=spp: e.tensor_copy(
                   ctx["a_carry"][:, 8:16],
                   ctx[f"a_b6_{spp}"][:].rearrange("p (g t) -> p g t", g=8)[:, :, SUB - 1:SUB].squeeze(2))),
               reads=[bn(6)], writes=["a_carry"], sig=SV)
            em("vector", (lambda e, o=bf(1), i=bf(8): e._custom_dve(
                WRAP_RNE, out=o, in0=i, s0=INV2PI, s1=WRAP_MAGIC, imm2=TWOPI)),
               reads=[bn(8)], writes=[bn(1)], sig=SV)
            em("vector", (lambda e, o=bf(4), i=bf(1): e.tensor_scalar(
                o, i, -PI_CLAMP, PI_CLAMP, ALU.max, ALU.min)),
               reads=[bn(1)], writes=[bn(4)], sig=SV)
            em("scalar", (lambda e, o=bf(1), i=bf(4): e.activation(o, i, ACT.Sin)),
               reads=[bn(4)], writes=[bn(1)], sig=SS)
            em("scalar", (lambda e, o=bf(7), i=bf(4): e.activation(o, i, ACT.Abs)),
               reads=[bn(4)], writes=[bn(7)], sig=SS)
            em("scalar", (lambda e, o=bf(4), i=bf(7): e.activation(o, i, ACT.Sin, scale=-1.0, bias=PI / 2)),
               reads=[bn(7)], writes=[bn(4)], sig=SS)
            em("gpsimd", (lambda e, o=bf(7), a=bf(5), b=bf(4): e.tensor_tensor(o, a, b, ALU.mult)),
               reads=[bn(5), bn(4)], writes=[bn(7)], sig=SG)
            em("gpsimd", (lambda e, o=bf(9), a=bf(6), b=bf(1): e.tensor_tensor(o, a, b, ALU.mult)),
               reads=[bn(6), bn(1)], writes=[bn(9)], sig=SG)
            em("gpsimd", (lambda e, o=hm16[:, 0:8, :], a=bb(7), b=bb(9):
                          e.tensor_tensor(o, a, b, ALU.subtract)),
               reads=[bn(7), bn(9)], writes=[hmn], sig=SG)
            em("gpsimd", (lambda e, o=bf(7), a=bf(5), b=bf(1): e.tensor_tensor(o, a, b, ALU.mult)),
               reads=[bn(5), bn(1)], writes=[bn(7)], sig=SG)
            em("gpsimd", (lambda e, o=bf(9), a=bf(6), b=bf(4): e.tensor_tensor(o, a, b, ALU.mult)),
               reads=[bn(6), bn(4)], writes=[bn(9)], sig=SG)
            em("gpsimd", (lambda e, o=hm16[:, 8:16, :], a=bb(7), b=bb(9):
                          e.tensor_tensor(o, a, b, ALU.add)),
               reads=[bn(7), bn(9)], writes=[hmn], sig=SG)
            hq_eng, hq_sig = ("sync", "dh")
            em(hq_eng,
               (lambda e, ts0=t0 + sub * SUB, hmn=hmn: e.dma_start(
                   out=hm_sp.rearrange("(kt p) t -> p kt t", p=128)[:, :, ts0:ts0 + SUB],
                   in_=ctx[hmn][:].rearrange("p (kt t) -> p kt t", kt=16))),
               reads=[hmn], writes=[f"hm_sp{seg}"], sig=hq_sig, dma=True,
               cost=dmac(128 * 16 * SUB * 2))

    p.barrier()
    free_group(A_GROUP)
    B_GROUP = ["b_wout", "b_hmt0", "b_hmt1", "b_y0", "b_xnat0", "b_xnat1",
               "b_x10", "b_x11",
               "b_qx", "b_dqx", "b_scr0", "b_scr1", "b_am", "b_s", "b_is",
               "b_sm0", "b_sm1", "c_boutrow", "c_orb"]
    sb("c_xqt", [128, 16 * 2048], BF16)
    sb("b_wout", [128, 16 * 2048], BF16)
    sb("b_hmt0", [128, 16 * 128], BF16)
    sb("b_hmt1", [128, 16 * 128], BF16)
    sb("b_y0", [128, 2048], F32)
    sb("b_xnat0", [128, 2048], F32)
    sb("b_xnat1", [128, 2048], F32)
    sb("b_x10", [128, 2048], F32)
    sb("b_x11", [128, 2048], F32)
    sb("b_qx", [128, 2048], BF16)
    sb("b_dqx", [128, 2048], BF16)
    sb("b_scr0", [128, 2048], F32)
    sb("b_scr1", [128, 2048], F32)
    sb("b_am", [128, 64], F32)
    sb("b_s", [128, 64], F32)
    sb("b_is", [128, 64], F32)
    sb("b_sm0", [128, 16], F32)
    sb("b_sm1", [128, 16], F32)
    sb("c_boutrow", [1, D], BF16)
    sb("c_orb", [1, 128], BF16)
    em("gpsimd", lambda e: e.dma_start(out=ctx["c_boutrow"][:], in_=b_out_row),
       writes=["c_boutrow"], sig="dg", dma=True)
    em("sync", lambda e: e.dma_start(out=ctx["c_orb"][:], in_=ones_row_b),
       writes=["c_orb"], sig="dh", dma=True)
    for wch in range(4):
        w_eng, w_sig = "gpsimd", "dg"
        em(w_eng,
           (lambda e, wch=wch: e.dma_start(
               out=ctx["b_wout"][:, wch * 4 * 2048:(wch + 1) * 4 * 2048],
               in_=w_out[:, wch * 4 * 2048:(wch + 1) * 4 * 2048])),
           writes=[f"b_woutc{wch}"], sig=w_sig, dma=True,
           cost=dmac(128 * 4 * 2048 * 4))

    for tt in range(NTT):
        hmt = "b_hmt0" if tt % 2 == 0 else "b_hmt1"
        byn = "b_y0"
        bscrn = f"b_scr{tt % 2}"
        bxnatn = f"b_xnat{tt % 2}"
        bx1n = f"b_x1{tt % 2}"
        bsmn = f"b_sm{tt % 2}"
        em("sync",
           (lambda e, tt=tt, hmt=hmt: e.dma_start(
               out=ctx[hmt][:].rearrange("p (kt t) -> p kt t", kt=16),
               in_=hm_sp.rearrange("(kt p) t -> p kt t", p=128)[:, :, tt * 128:(tt + 1) * 128])),
           reads=[f"hm_sp{tt // 4}"], writes=[hmt], sig="dh", dma=True,
           cost=dmac(128 * 16 * 128 * 2))
        em("gpsimd",
           (lambda e, tt=tt, bxnatn=bxnatn: e.dma_start(out=ctx[bxnatn][:],
                                         in_=x_nat[tt * 128:(tt + 1) * 128, :])),
           writes=[bxnatn], sig="dg", dma=True, cost=dmac(128 * 2048 * 4))
        for oc in range(4):
            bk, bnm = P[oc], PN[oc]
            for kt in range(16):
                em("tensor",
                   (lambda e, o=bk.ap()[:, 0:512],
                    l=ctx[hmt][:, kt * 128:(kt + 1) * 128],
                    r=ctx["b_wout"][:, kt * 2048 + oc * 512: kt * 2048 + (oc + 1) * 512],
                    st=(kt == 0): e.matmul(o, l, r, start=st, stop=False)),
                   reads=[hmt, f"b_woutc{kt // 4}"], writes=[bnm],
                   sig=ST if kt == 15 else None)
            em("tensor",
               (lambda e, o=bk.ap()[:, 0:512], l=ctx["c_orb"][0:1, :],
                r=ctx["c_boutrow"][0:1, oc * 512:(oc + 1) * 512]:
                e.matmul(o, l, r, start=False, stop=True)),
               reads=["c_orb", "c_boutrow"], writes=[bnm], sig=ST)
            em("scalar",
               (lambda e, oc=oc, bk=bk, byn=byn: e.activation(
                   ctx[byn][:, oc * 512:(oc + 1) * 512], bk.ap()[:, 0:512], ACT.Copy)),
               reads=[bnm], writes=[byn], sig=SS)

        def ln_stats(src, smn, c_ss, c_s, scrn=None):
            scrn = scrn or bscrn
            em("vector",
               (lambda e, scrn=scrn: e.scalar_tensor_tensor(ctx[scrn][:], ctx[src][:], 1.0,
                                                 ctx[src][:], ALU.mult, ALU.mult,
                                                 accum_out=ctx[smn][:, c_ss:c_ss + 1])),
               reads=[src], writes=[scrn, smn], sig=SV, cost=dvec(2048))
            em("vector",
               (lambda e, scrn=scrn: e.tensor_scalar(ctx[scrn][:], ctx[src][:], 1.0, None,
                                          ALU.mult, ALU.add,
                                          accum_out=ctx[smn][:, c_s:c_s + 1])),
               reads=[src], writes=[scrn, smn], sig=SV, cost=dvec(2048))

        def ln_smalls(smn, c_ss, c_s, c_mean, c_var, c_sd, c_r, c_mrn, n_elem, eng_sm="vector"):
            em(eng_sm,
               (lambda e: e.tensor_scalar(ctx[smn][:, c_mean:c_mean + 1],
                                          ctx[smn][:, c_s:c_s + 1], 1.0 / n_elem, None, ALU.mult)),
               reads=[smn], writes=[smn], sig=SV)
            em(eng_sm,
               (lambda e: e.tensor_scalar(ctx[smn][:, c_var:c_var + 1],
                                          ctx[smn][:, c_ss:c_ss + 1], 1.0 / n_elem, None, ALU.mult)),
               reads=[smn], writes=[smn], sig=SV)
            em(eng_sm,
               (lambda e: e.scalar_tensor_tensor(ctx[smn][:, c_sd:c_sd + 1],
                                                 ctx[smn][:, c_mean:c_mean + 1], -1.0,
                                                 ctx[smn][:, c_mean:c_mean + 1], ALU.mult, ALU.mult)),
               reads=[smn], writes=[smn], sig=SV)
            em(eng_sm,
               (lambda e: e.tensor_tensor(ctx[smn][:, c_var:c_var + 1],
                                          ctx[smn][:, c_var:c_var + 1],
                                          ctx[smn][:, c_sd:c_sd + 1], ALU.add)),
               reads=[smn], writes=[smn], sig=SV)
            em("scalar",
               (lambda e: e.activation(ctx[smn][:, c_sd:c_sd + 1], ctx[smn][:, c_var:c_var + 1],
                                       ACT.Sqrt, bias=EPS)),
               reads=[smn], writes=[smn], sig=SS)
            em(eng_sm,
               (lambda e: e.reciprocal(ctx[smn][:, c_r:c_r + 1], ctx[smn][:, c_sd:c_sd + 1])),
               reads=[smn], writes=[smn], sig=SV)
            em(eng_sm,
               (lambda e: e.scalar_tensor_tensor(ctx[smn][:, c_mrn:c_mrn + 1],
                                                 ctx[smn][:, c_mean:c_mean + 1], -1.0,
                                                 ctx[smn][:, c_r:c_r + 1], ALU.mult, ALU.mult)),
               reads=[smn], writes=[smn], sig=SV)

        ln_stats(byn, bsmn, 0, 1)
        ln_smalls(bsmn, 0, 1, 2, 3, 4, 5, 6, D)
        em("scalar",
           (lambda e, byn=byn, bsmn=bsmn, bscrn=bscrn: e.activation(
               ctx[bscrn][:], ctx[byn][:], ACT.Identity,
               scale=ctx[bsmn][:, 5:6], bias=ctx[bsmn][:, 6:7])),
           reads=[byn, bsmn], writes=[bscrn], sig=SS, cost=actc(2048))
        em("gpsimd",
           (lambda e, bscrn=bscrn, bx1n=bx1n, bxnatn=bxnatn: e.tensor_tensor(
               ctx[bx1n][:], ctx[bscrn][:], ctx[bxnatn][:], ALU.add)),
           reads=[bscrn, bxnatn], writes=[bx1n], sig=SG, cost=gpsc(2048))
        em("sync",
           (lambda e, tt=tt, bx1n=bx1n: e.dma_start(
               out=x1_sp[tt * 128:(tt + 1) * 128, :], in_=ctx[bx1n][:])),
           reads=[bx1n], writes=[f"x1_sp{tt}"], sig="dh", dma=True,
           cost=dmac(128 * 2048 * 4))
        ln_stats(bx1n, bsmn, 8, 9)
        ln_smalls(bsmn, 8, 9, 10, 11, 12, 13, 14, D)
        em("scalar",
           (lambda e, bsmn=bsmn, bx1n=bx1n, bscrn=bscrn: e.activation(
               ctx[bscrn][:], ctx[bx1n][:], ACT.Identity,
               scale=ctx[bsmn][:, 13:14], bias=ctx[bsmn][:, 14:15])),
           reads=[bx1n, bsmn], writes=[bscrn], sig=SS, cost=actc(2048))
        em("vector",
           (lambda e, bscrn=bscrn: e.tensor_reduce(ctx["b_am"][:],
                                      ctx[bscrn][:].rearrange("p (nb k) -> p nb k", k=32),
                                      AXX, ALU.max, apply_absolute_value=True)),
           reads=[bscrn], writes=["b_am"], sig=SV, cost=dvec(2048))
        em("vector",
           (lambda e: e.tensor_scalar(ctx["b_s"][:], ctx["b_am"][:], 1.0 / 448.0, 1e-30,
                                      ALU.mult, ALU.max)),
           reads=["b_am"], writes=["b_s"], sig=SV)
        em("vector", (lambda e: e.reciprocal(ctx["b_is"][:], ctx["b_s"][:])),
           reads=["b_s"], writes=["b_is"], sig=SV)
        em("vector",
           (lambda e, bscrn=bscrn: e._custom_dve(
               FP8_RNE_MUL,
               out=ctx["b_qx"][:].rearrange("p (nb k) -> p nb k", k=32),
               in0=ctx["b_is"][:].unsqueeze(2).broadcast_to([128, 64, 32]),
               in1=ctx[bscrn][:].rearrange("p (nb k) -> p nb k", k=32),
               s0=FP8_NOTMASK, s1=FP8_C1, imm2=FP8_C2)),
           reads=["b_is", bscrn], writes=["b_qx"], sig=SV, cost=dvec(2048))
        em("vector",
           (lambda e: e.tensor_tensor(
               ctx["b_dqx"][:].rearrange("p (nb k) -> p nb k", k=32),
               ctx["b_qx"][:].rearrange("p (nb k) -> p nb k", k=32),
               ctx["b_s"][:].unsqueeze(2).broadcast_to([128, 64, 32]), ALU.mult)),
           reads=["b_qx", "b_s"], writes=["b_dqx"], sig=SV, cost=dvec(2048))
        em("sync",
           (lambda e, tt=tt: e.dma_start(out=xq_sp[tt * 128:(tt + 1) * 128, :], in_=ctx["b_dqx"][:])),
           reads=["b_dqx"], writes=[f"xq_sp{tt}"], sig="dh", dma=True,
           cost=dmac(128 * 2048 * 2))
        for kt in range(16):
            qeng, qsig = ("sync", "dh") if kt % 2 == 0 else ("scalar", "ds")
            em(qeng,
               (lambda e, tt=tt, kt=kt: e.dma_start(
                   out=ctx["c_xqt"][:, kt * 2048 + tt * 128: kt * 2048 + (tt + 1) * 128],
                   in_=xq_sp[tt * 128:(tt + 1) * 128, kt * 128:(kt + 1) * 128],
                   transpose=True)),
               reads=[f"xq_sp{tt}"], writes=[f"c_xqt_t{tt}"], sig=qsig, dma=True,
               cost=1300)

    p.barrier()
    free_group(B_GROUP)
    C_GROUP = ["c_xqt", "c_rhs_g0", "c_rhs_g1", "c_rhs_g2",
               "c_rhs_u0", "c_rhs_u1", "c_rhs_u2",
               "c_sg0", "c_sg1", "c_tg0", "c_tg1", "c_hs0", "c_hs1",
               "c_qh0", "c_qh1", "c_dqh0", "c_dqh1", "c_dqht0", "c_dqht1",
               "c_ham", "c_hss", "c_his"]
    for b4 in range(3):
        sb(f"c_rhs_g{b4}", [128, 8 * 512], BF16)
        sb(f"c_rhs_u{b4}", [128, 8 * 512], BF16)
    for par in range(2):
        sb(f"c_sg{par}", [128, 512], F32)
        sb(f"c_tg{par}", [128, 512], F32)
        sb(f"c_hs{par}", [128, 512], F32)
        sb(f"c_qh{par}", [128, 512], BF16)
        sb(f"c_dqh{par}", [128, 512], BF16)
        sb(f"c_dqht{par}", [128, 512], BF16)
    sb("c_ham", [128, 16], F32)
    sb("c_hss", [128, 16], F32)
    sb("c_his", [128, 16], F32)
    PB = [PB6, PB7]
    PBN = ["PB6", "PB7"]
    def quantize(src, n_free, nblk, amn, sn, isn, qn, dqn, eng_dq):
        em("vector",
           (lambda e: e.tensor_reduce(ctx[amn][:, 0:nblk],
                                      ctx[src][:, 0:n_free].rearrange("p (nb k) -> p nb k", k=32),
                                      AXX, ALU.max, apply_absolute_value=True)),
           reads=[src], writes=[amn], sig=SV, cost=dvec(n_free))
        em("vector",
           (lambda e: e.tensor_scalar(ctx[sn][:, 0:nblk], ctx[amn][:, 0:nblk],
                                      1.0 / 448.0, 1e-30, ALU.mult, ALU.max)),
           reads=[amn], writes=[sn], sig=SV)
        em("vector", (lambda e: e.reciprocal(ctx[isn][:, 0:nblk], ctx[sn][:, 0:nblk])),
           reads=[sn], writes=[isn], sig=SV)
        em("vector",
           (lambda e: e._custom_dve(
               FP8_RNE_MUL,
               out=ctx[qn][:, 0:n_free].rearrange("p (nb k) -> p nb k", k=32),
               in0=ctx[isn][:, 0:nblk].unsqueeze(2).broadcast_to([128, nblk, 32]),
               in1=ctx[src][:, 0:n_free].rearrange("p (nb k) -> p nb k", k=32),
               s0=FP8_NOTMASK, s1=FP8_C1, imm2=FP8_C2)),
           reads=[isn, src], writes=[qn], sig=SV, cost=dvec(n_free))
        em(eng_dq,
           (lambda e: e.tensor_tensor(
               ctx[dqn][:, 0:n_free].rearrange("p (nb k) -> p nb k", k=32),
               ctx[qn][:, 0:n_free].rearrange("p (nb k) -> p nb k", k=32),
               ctx[sn][:, 0:nblk].unsqueeze(2).broadcast_to([128, nblk, 32]), ALU.mult)),
           reads=[qn, sn], writes=[dqn], sig=SG if eng_dq == "gpsimd" else SV,
           cost=gpsc(n_free) if eng_dq == "gpsimd" else dvec(n_free))

    pend_tp = [None]

    def flush_tp(dep_tok):
        if pend_tp[0] is not None:
            pend_tp[0](dep_tok)
            pend_tp[0] = None

    for hch in range(NHCH):
        for half in range(2):
            b4 = (hch * 2 + half) % 3
            for (sp_t, srcn, pfx) in ((gwq_sp, "gwq_sp", "g"), (uwq_sp, "uwq_sp", "u")):
                em("gpsimd",
                   (lambda e, sp_t=sp_t, pfx=pfx, b4=b4, hch=hch, half=half: e.dma_start(
                       out=ctx[f"c_rhs_{pfx}{b4}"][:].rearrange("p (kt n) -> p kt n", kt=8),
                       in_=sp_t.rearrange("(kt p) n -> p kt n", p=128)[
                           :, half * 8:(half + 1) * 8, hch * 512:(hch + 1) * 512])),
                   reads=[srcn], writes=[f"c_rhs_{pfx}{b4}"], sig="dg", dma=True,
                   cost=dmac(128 * 8 * 512 * 2))
        for tt in range(NTT):
            tpar = tt % 2
            xqtn = f"c_xqt_t{tt}"
            bg_, bgn = P[tpar * 2], PN[tpar * 2]
            bu_, bun = P[tpar * 2 + 1], PN[tpar * 2 + 1]
            sgn, tgn, hsn = f"c_sg{tpar}", f"c_tg{tpar}", f"c_hs{tpar}"
            qhn, dqhn, dqhtn = f"c_qh{tpar}", f"c_dqh{tpar}", f"c_dqht{tpar}"
            for kt in range(16):
                l = ctx["c_xqt"][:, kt * 2048 + tt * 128: kt * 2048 + (tt + 1) * 128]
                rgb = f"c_rhs_g{(hch * 2 + kt // 8) % 3}"
                em("tensor",
                   (lambda e, o=bg_.ap()[:, 0:512], l=l,
                    r=ctx[rgb][:, (kt % 8) * 512:(kt % 8 + 1) * 512],
                    st=(kt == 0), sp=(kt == 15):
                    e.matmul(o, l, r, start=st, stop=sp)),
                   reads=[xqtn, rgb], writes=[bgn], sig=ST if kt % 8 == 7 else None)
            up_tok = None
            for kt in range(16):
                l = ctx["c_xqt"][:, kt * 2048 + tt * 128: kt * 2048 + (tt + 1) * 128]
                rub = f"c_rhs_u{(hch * 2 + kt // 8) % 3}"
                up_tok = em("tensor",
                   (lambda e, o=bu_.ap()[:, 0:512], l=l,
                    r=ctx[rub][:, (kt % 8) * 512:(kt % 8 + 1) * 512],
                    st=(kt == 0), sp=(kt == 15):
                    e.matmul(o, l, r, start=st, stop=sp)),
                   reads=[xqtn, rub], writes=[bun], sig=ST if kt % 8 == 7 else None) or up_tok
            flush_tp(up_tok)
            em("scalar",
               (lambda e, bg_=bg_, sgn=sgn: e.activation(ctx[sgn][:], bg_.ap()[:, 0:512], ACT.Sigmoid)),
               reads=[bgn], writes=[sgn], sig=SS)
            em("vector",
               (lambda e, bg_=bg_, sgn=sgn, tgn=tgn: e.tensor_tensor(
                   ctx[tgn][:], ctx[sgn][:], bg_.ap()[:, 0:512], ALU.mult)),
               reads=[sgn, bgn], writes=[tgn], sig=SV, cost=dvec(512, 2.2))
            em("vector",
               (lambda e, bu_=bu_, tgn=tgn, hsn=hsn: e.tensor_tensor(
                   ctx[hsn][:], ctx[tgn][:], bu_.ap()[:, 0:512], ALU.mult)),
               reads=[tgn, bun], writes=[hsn], sig=SV, cost=dvec(512, 2.2))
            quantize(hsn, 512, 16, "c_ham", "c_hss", "c_his", qhn, dqhn, "gpsimd")

            def emit_tp(dep_tok, hch=hch, tt=tt, tpar=tpar, dqhn=dqhn, dqhtn=dqhtn):
                bk, bnm = PB[tpar], PBN[tpar]
                w0 = [dep_tok] if dep_tok is not None else []
                for k4 in range(4):
                    em("tensor",
                       (lambda e, o=bk.ap()[:, k4 * 128:(k4 + 1) * 128],
                        i_=ctx[dqhn][:, k4 * 128:(k4 + 1) * 128]:
                        e.transpose(o, i_, ctx["c_idb"][:])),
                       reads=[dqhn, "c_idb"], writes=[bnm],
                       waits=w0 if k4 == 0 else [],
                       sig=ST if k4 == 3 else None)
                em("scalar",
                   (lambda e, bk=bk, dqhtn=dqhtn: e.activation(
                       ctx[dqhtn][:], bk.ap()[:, 0:512], ACT.Copy)),
                   reads=[bnm], writes=[dqhtn], sig=SS)
                em("sync",
                   (lambda e, hch=hch, tt=tt, dqhtn=dqhtn: e.dma_start(
                       out=hq_sp.rearrange("(r p) t -> p r t", p=128)[
                           :, hch * 4:(hch + 1) * 4, tt * 128:(tt + 1) * 128],
                       in_=ctx[dqhtn][:].rearrange("p (r t) -> p r t", r=4))),
                   reads=[dqhtn], writes=["hq_sp"], sig="dh", dma=True,
                   cost=dmac(128 * 512 * 2))

            emit_tp(None)
    flush_tp(None)

    p.barrier()
    free_group(C_GROUP)
    psum_free(["PB7", "PB6"])
    Q6 = psum("Q6")
    Q7 = psum("Q7")
    DP = P + [Q6, Q7]
    DPN = PN + ["Q6", "Q7"]
    sb("d_hq", [128, 64 * QT], BF16)
    sb("d_rhs0", [128, 8 * 512], BF16)
    sb("d_rhs1", [128, 8 * 512], BF16)
    for i in range(4):
        sb(f"d_y{i}", [128, 2048], F32)
    sb("d_xn1", [128, 2048], F32)
    sb("d_fin", [128, 2048], F32)
    sb("d_sm", [128, 16], F32)
    HQ_CH_Q = (("gpsimd", "dg"), ("sync", "dh"), ("scalar", "ds"), ("gpsimd", "dg"))
    for q in range(NQT):
        for ch in range(4):
            eng_c, sig_c = HQ_CH_Q[ch]
            em(eng_c,
               (lambda e, q=q, ch=ch: e.dma_start(
                   out=ctx["d_hq"][:, ch * 16 * QT:(ch + 1) * 16 * QT]
                       .rearrange("p (kt t) -> p kt t", kt=16),
                   in_=hq_sp.rearrange("(kt p) t -> p kt t", p=128)[
                       :, ch * 16:(ch + 1) * 16, q * QT:(q + 1) * QT])),
               reads=["hq_sp"], writes=[f"d_hqc{ch}"], sig=sig_c, dma=True,
               cost=dmac(128 * 16 * QT * 2))
        for oc in range(4):
            pb = 4 * (oc % 2)
            for kg in range(8):
                rhs = "d_rhs0" if kg % 2 == 0 else "d_rhs1"
                em("sync",
                   (lambda e, kg=kg, oc=oc, rhs=rhs: e.dma_start(
                       out=ctx[rhs][:].rearrange("p (k n) -> p k n", k=8),
                       in_=dwq_sp.rearrange("(k p) n -> p k n", p=128)[
                           :, kg * 8:(kg + 1) * 8, oc * 512:(oc + 1) * 512])),
                   reads=["dwq_sp"], writes=[rhs], sig="dh", dma=True,
                   cost=dmac(128 * 8 * 512 * 2))
                for k8 in range(8):
                    kt = kg * 8 + k8
                    for t4 in range(4):
                        bk, bnm = DP[pb + t4], DPN[pb + t4]
                        em("tensor",
                           (lambda e, o=bk.ap()[:, 0:512],
                            l=ctx["d_hq"][:, kt * QT + t4 * 128: kt * QT + (t4 + 1) * 128],
                            r=ctx[rhs][:, k8 * 512:(k8 + 1) * 512],
                            st=(kt == 0), sp=(kt == 63): e.matmul(o, l, r, start=st, stop=sp)),
                           reads=[f"d_hqc{kt // 16}", rhs], writes=[bnm],
                           sig=ST if (kt % 16 == 15 or t4 == 3) else None)
            for t4 in range(4):
                bk, bnm = DP[pb + t4], DPN[pb + t4]
                em("scalar",
                   (lambda e, t4=t4, oc=oc, bk=bk: e.activation(
                       ctx[f"d_y{t4}"][:, oc * 512:(oc + 1) * 512], bk.ap()[:, 0:512], ACT.Copy)),
                   reads=[bnm], writes=[f"d_y{t4}"], sig=SS)
        for t4 in range(4):
            tt = q * 4 + t4
            yn = f"d_y{t4}"
            em("vector",
               (lambda e, yn=yn: e.scalar_tensor_tensor(ctx["d_fin"][:], ctx[yn][:], 1.0,
                                                        ctx[yn][:], ALU.mult, ALU.mult,
                                                        accum_out=ctx["d_sm"][:, 0:1])),
               reads=[yn], writes=["d_fin", "d_sm"], sig=SV, cost=dvec(2048))
            em("vector",
               (lambda e, yn=yn: e.tensor_scalar(ctx["d_fin"][:], ctx[yn][:], 1.0, None,
                                                 ALU.mult, ALU.add,
                                                 accum_out=ctx["d_sm"][:, 1:2])),
               reads=[yn], writes=["d_fin", "d_sm"], sig=SV, cost=dvec(2048))
            em("vector",
               (lambda e: e.tensor_scalar(ctx["d_sm"][:, 2:3], ctx["d_sm"][:, 1:2],
                                          1.0 / D, None, ALU.mult)),
               reads=["d_sm"], writes=["d_sm"], sig=SV)
            em("vector",
               (lambda e: e.tensor_scalar(ctx["d_sm"][:, 3:4], ctx["d_sm"][:, 0:1],
                                          1.0 / D, None, ALU.mult)),
               reads=["d_sm"], writes=["d_sm"], sig=SV)
            em("vector",
               (lambda e: e.scalar_tensor_tensor(ctx["d_sm"][:, 4:5], ctx["d_sm"][:, 2:3],
                                                 -1.0, ctx["d_sm"][:, 2:3], ALU.mult, ALU.mult)),
               reads=["d_sm"], writes=["d_sm"], sig=SV)
            em("vector",
               (lambda e: e.tensor_tensor(ctx["d_sm"][:, 3:4], ctx["d_sm"][:, 3:4],
                                          ctx["d_sm"][:, 4:5], ALU.add)),
               reads=["d_sm"], writes=["d_sm"], sig=SV)
            em("scalar",
               (lambda e: e.activation(ctx["d_sm"][:, 5:6], ctx["d_sm"][:, 3:4], ACT.Sqrt, bias=EPS)),
               reads=["d_sm"], writes=["d_sm"], sig=SS)
            em("vector",
               (lambda e: e.reciprocal(ctx["d_sm"][:, 6:7], ctx["d_sm"][:, 5:6])),
               reads=["d_sm"], writes=["d_sm"], sig=SV)
            em("vector",
               (lambda e: e.scalar_tensor_tensor(ctx["d_sm"][:, 7:8], ctx["d_sm"][:, 2:3],
                                                 -1.0, ctx["d_sm"][:, 6:7], ALU.mult, ALU.mult)),
               reads=["d_sm"], writes=["d_sm"], sig=SV)
            em("gpsimd",
               (lambda e, tt=tt: e.dma_start(out=ctx["d_xn1"][:],
                                             in_=x1_sp[tt * 128:(tt + 1) * 128, :])),
               reads=[f"x1_sp{tt}"], writes=["d_xn1"], sig="dg", dma=True,
               cost=dmac(128 * 2048 * 4))
            em("scalar",
               (lambda e, yn=yn: e.activation(ctx["d_fin"][:], ctx[yn][:], ACT.Identity,
                                              scale=ctx["d_sm"][:, 6:7], bias=ctx["d_sm"][:, 7:8])),
               reads=[yn, "d_sm"], writes=["d_fin"], sig=SS, cost=actc(2048))
            em("vector",
               (lambda e: e.tensor_tensor(ctx["d_fin"][:], ctx["d_fin"][:], ctx["d_xn1"][:], ALU.add)),
               reads=["d_fin", "d_xn1"], writes=["d_fin"], sig=SV, cost=dvec(2048))
            em("sync",
               (lambda e, tt=tt: e.dma_start(out=out_hbm[tt * 128:(tt + 1) * 128, :],
                                             in_=ctx["d_fin"][:])),
               reads=["d_fin"], sig="dh", dma=True, cost=dmac(128 * 2048 * 4))

    sched = p.schedule()
    final_waits = [(s, v) for (s, v) in p.final_cnt.items()
                   if s.startswith(("dh", "dg", "ds"))]

    sems = {}
    for name in sorted(p.final_cnt):
        sems[name] = nc.semaphore(name).__enter__()

    with nc.Block() as block:
        def run_engine(eng_obj, name):
            emitted = {}
            for (fn, waits, sig) in sched[name]:
                for (sname, val) in waits:
                    if val > emitted.get(sname, -1):
                        eng_obj.wait_ge(sems[sname], val)
                        emitted[sname] = val
                ins = fn(eng_obj)
                if sig is not None:
                    sname, amt = sig
                    ins.then_inc(sems[sname], amt)
            if name == "sync":
                for (sname, val) in final_waits:
                    eng_obj.wait_ge(sems[sname], val)

        @block.sync
        def _(e):
            run_engine(e, "sync")

        @block.scalar
        def _(e):
            run_engine(e, "scalar")

        @block.tensor
        def _(e):
            run_engine(e, "tensor")

        @block.vector
        def _(e):
            run_engine(e, "vector")

        @block.gpsimd
        def _(e):
            run_engine(e, "gpsimd")

    lower_extended_insts(nc)
    return nc


_NC_CACHE = [None]


def kernel(L0_attn_x, L0_attn_W_in, L0_attn_b_in, L0_attn_W_gx, L0_attn_b_gx,
           L0_attn_W_ga, L0_attn_b_ga, L0_attn_a_real, L0_attn_a_imag,
           L0_attn_W_out, L0_attn_b_out, L0_ffn_gate_w, L0_ffn_up_w, L0_ffn_down_w):
    x = np.ascontiguousarray(np.asarray(L0_attn_x, dtype=np.float32))
    W_in = np.asarray(L0_attn_W_in, dtype=np.float32)
    W_out = np.asarray(L0_attn_W_out, dtype=np.float32)
    W_gx = np.asarray(L0_attn_W_gx, dtype=np.float32)
    W_ga = np.asarray(L0_attn_W_ga, dtype=np.float32)
    b_gx = np.asarray(L0_attn_b_gx, dtype=np.float32)
    b_ga = np.asarray(L0_attn_b_ga, dtype=np.float32)
    b_in = np.asarray(L0_attn_b_in, dtype=np.float32)
    b_out = np.asarray(L0_attn_b_out, dtype=np.float32)
    a_real = np.asarray(L0_attn_a_real, dtype=np.float32)
    a_imag = np.asarray(L0_attn_a_imag, dtype=np.float32)
    gate_w = np.ascontiguousarray(np.asarray(L0_ffn_gate_w, dtype=np.float32))
    up_w = np.ascontiguousarray(np.asarray(L0_ffn_up_w, dtype=np.float32))
    down_w = np.ascontiguousarray(np.asarray(L0_ffn_down_w, dtype=np.float32))

    w_in_l = np.ascontiguousarray(
        W_in.reshape(16, 128, 16, 128).transpose(1, 0, 2, 3).reshape(128, 16 * 16 * 128))
    w_out_l = np.ascontiguousarray(
        W_out.reshape(16, 128, 2048).transpose(1, 0, 2).reshape(128, 16 * 2048))
    wg = np.stack([W_gx, W_ga], 0).reshape(2, 8, 2, 128, 128)
    w_g_l = np.ascontiguousarray(wg.transpose(3, 0, 1, 2, 4).reshape(128, 2 * 8 * 2 * 128))
    b_g_l = np.ascontiguousarray(np.concatenate([b_gx.reshape(-1), b_ga.reshape(-1)], 0)
                                 .reshape(16, 128).T)
    ar_l = np.ascontiguousarray(a_real.reshape(8, 128).T)
    aim_l = np.ascontiguousarray(a_imag.reshape(8, 128).T)
    b_in_l = np.ascontiguousarray(b_in.reshape(1, DH))
    b_out_l = np.ascontiguousarray(b_out.reshape(1, D))
    ident_f = np.eye(128, dtype=np.float32)
    ident_b = np.eye(128).astype(ml_dtypes.bfloat16)
    ones_col_b = np.ones((128, 1), dtype=ml_dtypes.bfloat16)
    ones_row_f = np.ones((1, 128), dtype=np.float32)
    ones_row_b = np.ones((1, 128), dtype=ml_dtypes.bfloat16)
    lit_arr = np.tile(np.array([0.0, 1.0, 2.0, 0.5, EPS, PI / 2, -PI, 1e-24],
                               dtype=np.float32), (128, 1))

    if _NC_CACHE[0] is None:
        _NC_CACHE[0] = build_program()
    nc = _NC_CACHE[0]

    in_maps = []
    for c in range(NCORES):
        xc = np.ascontiguousarray(x[c])
        in_maps.append({
            "x_tr": np.ascontiguousarray(xc.T),
            "x_nat": xc,
            "w_in": w_in_l,
            "w_out": w_out_l,
            "w_g": w_g_l,
            "b_g": b_g_l,
            "b_in_row": b_in_l,
            "b_out_row": b_out_l,
            "ar_col": ar_l,
            "aim_col": aim_l,
            "gate_w_nat": gate_w,
            "up_w_nat": up_w,
            "down_w_nat": down_w,
            "ident_f": ident_f,
            "ident_b": ident_b,
            "ones_col_b": ones_col_b,
            "ones_row_f": ones_row_f,
            "ones_row_b": ones_row_b,
            "lit_in": lit_arr,
        })
    res = run_bass_kernel_spmd(nc, in_maps, core_ids=list(range(NCORES)))
    out = np.stack([np.asarray(r["out"]) for r in res.results], 0)
    return out.astype(np.float32)
